# version 08
# baseline (speedup 1.0000x reference)
import contextlib

import numpy as np
import ml_dtypes

import concourse.bass as bass
import concourse.mybir as mybir
from concourse.bass_utils import run_bass_kernel_spmd

F32 = mybir.dt.float32
BF16 = mybir.dt.bfloat16
AF = mybir.ActivationFunctionType
ALU = mybir.AluOpType
AX = mybir.AxisListType

B, L, D = 4, 4096, 1024
T = 2048
H, DH = 16, 64
HID = 4096
NIC = D // 128
NHC = HID // 128
NPAIR = H // 2
TCT = 512
NTCT = T // TCT
ROPE_BASE = 10000.0
EPS = 1e-5


class Ctr:

    def __init__(self, stack, nc, name):
        self.sem = stack.enter_context(nc.semaphore(name))
        self.n = 0


class Sched:

    ENGINES = ("tensor", "vector", "scalar", "sync", "gpsimd")

    def __init__(self):
        self.ops = {e: [] for e in self.ENGINES}

    def emit(self, engine, fn, waits=(), inc=None, inck=1):
        if inc is not None:
            inc.n += inck
            inc = (inc, inck)
        self.ops[engine].append((fn, tuple(waits), inc))

    def flush(self, nc):
        with nc.Block() as block:
            def mk(eng_name):
                def _(eng):
                    for fn, waits, inc in self.ops[eng_name]:
                        for ctr, val in waits:
                            if val > 0:
                                eng.wait_ge(ctr.sem, val)
                        inst = fn(eng)
                        if inc is not None:
                            ctr, k = inc
                            inst.then_inc(ctr.sem, k)
                return _
            for name in self.ENGINES:
                if self.ops[name]:
                    getattr(block, name)(mk(name))
        self.ops = {e: [] for e in self.ENGINES}


def build_nc(debug=False):
    stack = contextlib.ExitStack()
    nc = bass.Bass("TRN2", target_bir_lowering=False, detect_race_conditions=False)

    dx = nc.dram_tensor("x", [D, L], F32, kind="ExternalInput")
    dwqkv = nc.dram_tensor("wqkv", [NPAIR * 3 * 128, NIC * 128], F32, kind="ExternalInput")
    dwout = nc.dram_tensor("wout", [NIC * 128, NIC * 128], F32, kind="ExternalInput")
    dw1 = nc.dram_tensor("w1", [NHC * 128, NIC * 128], F32, kind="ExternalInput")
    dw2 = nc.dram_tensor("w2", [NHC * 128, NIC * 128], F32, kind="ExternalInput")
    dh1 = nc.dram_tensor("h1", [NHC * 128, NHC * 128], F32, kind="ExternalInput")
    dh2 = nc.dram_tensor("h2", [NHC * 128, NHC * 128], F32, kind="ExternalInput")
    dw3 = nc.dram_tensor("w3", [NIC * 128, NHC * 128], F32, kind="ExternalInput")
    ds1 = nc.dram_tensor("s1", [128, 32], F32, kind="ExternalInput")
    ds2 = nc.dram_tensor("s2", [128, 32], F32, kind="ExternalInput")
    dh1s = nc.dram_tensor("h1s", [128, 32], F32, kind="ExternalInput")
    dh2s = nc.dram_tensor("h2s", [128, 32], F32, kind="ExternalInput")
    dw3s = nc.dram_tensor("w3s", [128, 8], F32, kind="ExternalInput")
    dw1c = nc.dram_tensor("w1c", [NHC * 128, NIC * 128], BF16, kind="Internal")
    dw2c = nc.dram_tensor("w2c", [NHC * 128, NIC * 128], BF16, kind="Internal")
    dh1c = nc.dram_tensor("h1c", [NHC * 128, NHC * 128], BF16, kind="Internal")
    dh2c = nc.dram_tensor("h2c", [NHC * 128, NHC * 128], BF16, kind="Internal")
    dw3c = nc.dram_tensor("w3c", [NIC * 128, NHC * 128], BF16, kind="Internal")
    dcos = nc.dram_tensor("costab", [128, L], BF16, kind="ExternalInput")
    dsin = nc.dram_tensor("sintab", [128, L], BF16, kind="ExternalInput")
    dy = nc.dram_tensor("y", [D, T], F32, kind="ExternalOutput")
    dbg = {}
    if debug:
        dbg["xn"] = nc.dram_tensor("dbg_xn", [D, L], F32, kind="ExternalOutput")
        dbg["q"] = nc.dram_tensor("dbg_q", [128, T], F32, kind="ExternalOutput")
        dbg["k"] = nc.dram_tensor("dbg_k", [128, L], F32, kind="ExternalOutput")
        dbg["v"] = nc.dram_tensor("dbg_v", [128, 32 * 130], F32, kind="ExternalOutput")
        dbg["ao"] = nc.dram_tensor("dbg_ao", [D, T], F32, kind="ExternalOutput")
        dbg["ap"] = nc.dram_tensor("dbg_ap", [D, T], F32, kind="ExternalOutput")

    cX = [Ctr(stack, nc, f"cX{j}") for j in range(2)]
    cSW = Ctr(stack, nc, "cSW")
    cTAB = Ctr(stack, nc, "cTAB")
    cWQ = [Ctr(stack, nc, f"cWQ{j}") for j in range(2)]
    cWO = [Ctr(stack, nc, f"cWO{j}") for j in range(2)]
    cWH = [Ctr(stack, nc, f"cWH{j}") for j in range(3)]
    cXR = [Ctr(stack, nc, f"cXR{j}") for j in range(2)]
    cWB = [Ctr(stack, nc, f"cWB{j}") for j in range(2)]
    cST = [Ctr(stack, nc, f"cST{j}") for j in range(2)]
    cQS = Ctr(stack, nc, "cQS")
    cU = Ctr(stack, nc, "cU")
    cCS = [Ctr(stack, nc, f"cCS{j}") for j in range(2)]
    cDBG = Ctr(stack, nc, "cDBG")
    cTE = Ctr(stack, nc, "cTE")
    cSC = Ctr(stack, nc, "cSC")
    cVE = Ctr(stack, nc, "cVE")

    sch = Sched()
    E = sch.emit

    xn_own = stack.enter_context(nc.sbuf_tensor("xn_own", [128, NIC * T], BF16))
    ao_sb = stack.enter_context(nc.sbuf_tensor("ao_sb", [128, NPAIR * T], BF16))
    wo_sb = [stack.enter_context(nc.sbuf_tensor(f"wo{j}", [128, 1024], BF16)) for j in range(2)]
    attn_proj = stack.enter_context(nc.sbuf_tensor("attn_proj", [128, NIC * T], BF16))
    s1_sb = stack.enter_context(nc.sbuf_tensor("s1_sb", [128, 32], F32))
    s2_sb = stack.enter_context(nc.sbuf_tensor("s2_sb", [128, 32], F32))
    h1s_sb = stack.enter_context(nc.sbuf_tensor("h1s_sb", [128, 32], F32))
    h2s_sb = stack.enter_context(nc.sbuf_tensor("h2s_sb", [128, 32], F32))
    w3s_sb = stack.enter_context(nc.sbuf_tensor("w3s_sb", [128, 8], F32))
    ones_f32 = stack.enter_context(nc.sbuf_tensor("ones_f32", [65, 128], F32))
    onescol_f32 = stack.enter_context(nc.sbuf_tensor("onescol_f32", [128, 1], F32))
    onescol_bf = stack.enter_context(nc.sbuf_tensor("onescol_bf", [128, 1], BF16))

    E("vector", lambda v: v.memset(ones_f32[:, :], 1.0), inc=cVE)
    E("vector", lambda v: v.memset(onescol_f32[:, :], 1.0), inc=cVE)
    E("vector", lambda v: v.memset(onescol_bf[:, :], 1.0), inc=cVE)
    const_ve = cVE.n
    for dsrc, dst in ((ds1, s1_sb), (ds2, s2_sb), (dh1s, h1s_sb), (dh2s, h2s_sb), (dw3s, w3s_sb)):
        E("gpsimd", lambda g, s=dsrc, d=dst: g.dma_start(out=d[:, :], in_=s[:, :]), inc=cTAB, inck=16)
    scales_wt = cTAB.n

    ap_done_box = [0]

    with contextlib.ExitStack() as sA:
        xn_prt = sA.enter_context(nc.sbuf_tensor("xn_prt", [128, NIC * T], BF16))

        with contextlib.ExitStack() as s0:
            xch = [s0.enter_context(nc.sbuf_tensor(f"xch{j}", [128, L], F32)) for j in range(2)]
            xsq = [s0.enter_context(nc.sbuf_tensor(f"xsq{j}", [128, 512], F32)) for j in range(2)]
            es_sb = s0.enter_context(nc.sbuf_tensor("es_sb", [33, L], F32))
            mean_bc = s0.enter_context(nc.sbuf_tensor("mean_bc", [128, L], BF16))
            rstd_bc = s0.enter_context(nc.sbuf_tensor("rstd_bc", [128, L], BF16))
            t1 = s0.enter_context(nc.sbuf_tensor("t1", [128, T // 2], F32))
            v1 = t1[:, 0:512]
            v2 = t1[:, 512:1024]

            with nc.psum_tensor("stats_ps", [33, L], F32) as stats_ps:
                te_ic = []
                sc_ic = []
                for ic in range(NIC):
                    bj = ic % 2
                    lw = []
                    if ic >= 2:
                        lw = [(cTE, te_ic[ic - 2]), (cSC, sc_ic[ic - 2])]
                    E("sync", lambda s, ic=ic, bj=bj: s.dma_start(
                        out=xch[bj][:, :], in_=dx[ic * 128:(ic + 1) * 128, :]),
                        waits=lw, inc=cX[bj], inck=16)
                    ldc, ld = cX[bj], cX[bj].n
                    for tc in range(8):
                        sq = tc % 2
                        sqw = [(ldc, ld)]
                        if ic > 0 or tc >= 2:
                            sqw.append((cTE, cTE.n))
                        E("scalar", lambda s, bj=bj, tc=tc, sq=sq: s.activation(
                            out=xsq[sq][:, :], in_=xch[bj][:, tc * 512:(tc + 1) * 512], func=AF.Square),
                            waits=sqw, inc=cSC)
                        scm = cSC.n
                        E("tensor", lambda t, ic=ic, bj=bj, tc=tc: t.matmul(
                            stats_ps[0:1, tc * 512:(tc + 1) * 512], onescol_f32[:, :],
                            xch[bj][:, tc * 512:(tc + 1) * 512],
                            start=(ic == 0), stop=(ic == NIC - 1)),
                            waits=([(ldc, ld), (cVE, const_ve)] if ic == 0 else [(ldc, ld)]) if tc == 0 else [])
                        E("tensor", lambda t, ic=ic, tc=tc, sq=sq: t.matmul(
                            stats_ps[32:33, tc * 512:(tc + 1) * 512], onescol_f32[:, :],
                            xsq[sq][:, :],
                            start=(ic == 0), stop=(ic == NIC - 1)),
                            waits=[(cSC, scm)],
                            inc=cTE)
                    te_ic.append(cTE.n)
                    sc_ic.append(cSC.n)
                E("vector", lambda v: v.tensor_single_scalar(
                    es_sb[0:1, :], stats_ps[0:1, :], 1.0 / D, ALU.mult),
                    waits=[(cTE, cTE.n), (cVE, const_ve)], inc=cVE)
                E("vector", lambda v: v.tensor_single_scalar(
                    es_sb[32:33, :], stats_ps[32:33, :], 1.0 / D, ALU.mult), inc=cVE)
                sch.flush(nc)
            p0a_te, p0a_sc, p0a_ve = cTE.n, cSC.n, cVE.n

            with nc.psum_tensor("bc_ps", [128, 2048], F32) as bc_ps:
                vtc = []
                for tc in range(8):
                    par = tc % 2
                    mslot = bc_ps[:, (2 * par) * 512:(2 * par + 1) * 512]
                    eslot = bc_ps[:, (2 * par + 1) * 512:(2 * par + 2) * 512]
                    w0 = [(cVE, p0a_ve)]
                    if tc >= 2:
                        w0.append((cVE, vtc[tc - 2]))
                    sl = slice(tc * 512, (tc + 1) * 512)
                    E("tensor", lambda t, m=mslot, sl=sl: t.matmul(
                        m, ones_f32[0:1, :], es_sb[0:1, sl], start=True, stop=True), waits=w0)
                    E("tensor", lambda t, e=eslot, sl=sl: t.matmul(
                        e, ones_f32[32:33, :], es_sb[32:33, sl], start=True, stop=True), inc=cTE)
                    tm = cTE.n
                    E("vector", lambda v, m=mslot, sl=sl: v.tensor_copy(mean_bc[:, sl], m),
                        waits=[(cTE, tm)], inc=cVE)
                    E("vector", lambda v, m=mslot: v.tensor_copy(v1, m), inc=cVE)
                    E("vector", lambda v, e=eslot: v.tensor_tensor(v2, v1, v1, ALU.mult), inc=cVE)
                    E("vector", lambda v, e=eslot: v.tensor_tensor(v2, e, v2, ALU.subtract), inc=cVE)
                    E("vector", lambda v: v.tensor_single_scalar(v2, v2, EPS, ALU.add), inc=cVE)
                    vsq = cVE.n
                    E("scalar", lambda s: s.activation(out=v1, in_=v2, func=AF.Sqrt),
                        waits=[(cVE, vsq), (cSC, p0a_sc)], inc=cSC)
                    E("vector", lambda v: v.reciprocal(v2, v1),
                        waits=[(cSC, cSC.n)], inc=cVE)
                    E("vector", lambda v, sl=sl: v.tensor_copy(rstd_bc[:, sl], v2), inc=cVE)
                    vtc.append(cVE.n)
                sch.flush(nc)
            p0b_ve = cVE.n

            va = []
            for ic in range(NIC):
                bj = ic % 2
                lw = [(cVE, p0b_ve), (cTE, cTE.n), (cSC, cSC.n)]
                if ic >= 2:
                    lw.append((cVE, va[ic - 2]))
                E("sync", lambda s, ic=ic, bj=bj: s.dma_start(
                    out=xch[bj][:, :], in_=dx[ic * 128:(ic + 1) * 128, :]),
                    waits=lw, inc=cX[bj], inck=16)
                ldc, ld = cX[bj], cX[bj].n
                for quarter in range(4):
                    dst = xn_own if quarter < 2 else xn_prt
                    hs = slice(quarter * (T // 2), (quarter + 1) * (T // 2))
                    ds = slice(ic * T + (quarter % 2) * (T // 2), ic * T + (quarter % 2 + 1) * (T // 2))
                    E("vector", lambda v, bj=bj, hs=hs: v.tensor_tensor(
                        t1[:, :], xch[bj][:, hs], mean_bc[:, hs], ALU.subtract),
                        waits=[(ldc, ld)] if quarter == 0 else [], inc=cVE)
                    E("vector", lambda v, hs=hs, ds=ds, dst=dst: v.tensor_tensor(
                        dst[:, ds], t1[:, :], rstd_bc[:, hs], ALU.mult), inc=cVE)
                va.append(cVE.n)
            xn_done = cVE.n
            if debug:
                for ic in range(NIC):
                    E("gpsimd", lambda g, ic=ic: g.dma_start(
                        out=dbg["xn"][ic * 128:(ic + 1) * 128, 0:T], in_=xn_own[:, ic * T:(ic + 1) * T]),
                        waits=[(cVE, xn_done)], inc=cDBG, inck=16)
                    E("gpsimd", lambda g, ic=ic: g.dma_start(
                        out=dbg["xn"][ic * 128:(ic + 1) * 128, T:L], in_=xn_prt[:, ic * T:(ic + 1) * T]),
                        inc=cDBG, inck=16)
            sch.flush(nc)

        with contextlib.ExitStack() as s1:
            kpA = s1.enter_context(nc.sbuf_tensor("kpA", [65, L], BF16))
            kpB = s1.enter_context(nc.sbuf_tensor("kpB", [65, L], BF16))
            qpA = s1.enter_context(nc.sbuf_tensor("qpA", [65, T], BF16))
            qpB = s1.enter_context(nc.sbuf_tensor("qpB", [65, T], BF16))
            vsb = s1.enter_context(nc.sbuf_tensor("vsb", [128, 32 * 130], BF16))
            pt = s1.enter_context(nc.sbuf_tensor("pt", [128, 4096], BF16))
            wq_sb = s1.enter_context(nc.sbuf_tensor("wq", [128, 1024], BF16))
            wk_sb = s1.enter_context(nc.sbuf_tensor("wk", [128, 1024], BF16))
            wv_sb = s1.enter_context(nc.sbuf_tensor("wv", [128, 1024], BF16))
            qraw = [s1.enter_context(nc.sbuf_tensor(f"qraw{j}", [128, 512], F32)) for j in range(2)]
            qswp = [s1.enter_context(nc.sbuf_tensor(f"qswp{j}", [128, 512], F32)) for j in range(2)]
            rt1 = [s1.enter_context(nc.sbuf_tensor(f"rt1_{j}", [128, 512], F32)) for j in range(2)]
            cs_sb = s1.enter_context(nc.sbuf_tensor("cs_sb", [128, 2048], BF16))
            den_sb = s1.enter_context(nc.sbuf_tensor("den_sb", [65, 1024], F32))
            bc_sb = s1.enter_context(nc.sbuf_tensor("bc_sb", [128, 1024], F32))
            bnd = s1.enter_context(nc.sbuf_tensor("bnd", [33, 8], F32))
            c8col = s1.enter_context(nc.sbuf_tensor("c8col", [128, 2], F32))
            MA = s1.enter_context(nc.sbuf_tensor("MA", [128, 1024], BF16))
            MB = s1.enter_context(nc.sbuf_tensor("MB", [128, 1024], BF16))

            E("vector", lambda v: v.memset(vsb[:, :], 0.0), inc=cVE)
            vsb_r = vsb.rearrange("p (vt c) -> p vt c", c=130)
            E("vector", lambda v: v.memset(vsb_r[:, :, 64:65], 1.0), inc=cVE)
            E("vector", lambda v: v.memset(kpA[64:65, :], 1.0), inc=cVE)
            E("vector", lambda v: v.memset(bc_sb[:, :], 0.0), inc=cVE)
            E("vector", lambda v: v.memset(den_sb[:, :], 1.0), inc=cVE)
            E("vector", lambda v: v.memset(kpB[64:65, :], 1.0), inc=cVE)

            with (
                nc.psum_tensor("pA", [128, 2048], F32) as pA,
                nc.psum_tensor("pB", [128, 2048], F32) as pB,
            ):
                def wload(g, dstT, row0):
                    return g.dma_start(out=dstT[:, :], in_=dwqkv[row0 * 128:(row0 + 1) * 128, :])

                prev_norm = 0
                last_swp = 0
                swp_mark = [0, 0]
                qs_mark = [0, 0]
                rope_tile_idx = [0]

                for p in range(NPAIR):
                    pw = [(cTE, cTE.n)]
                    E("gpsimd", lambda g, p=p: wload(g, wq_sb, p * 3 + 0), waits=pw, inc=cWQ[0], inck=16)
                    E("gpsimd", lambda g, p=p: wload(g, wk_sb, p * 3 + 1), inc=cWQ[0], inck=16)
                    E("gpsimd", lambda g, p=p: wload(g, wv_sb, p * 3 + 2), inc=cWQ[0], inck=16)
                    wt_ready = cWQ[0].n

                    def rope_tiles(wsb, n_tiles, xsrc_fn, dstA, dstB, first_wait):
                        nonlocal last_swp, prev_norm
                        ve_tile = []
                        for j in range(n_tiles):
                            ps = pA[:, (j % 2) * 512:(j % 2 + 1) * 512]
                            gpar = rope_tile_idx[0] % 2
                            gw = [(cVE, ve_tile[j - 2])] if j >= 2 else [(cVE, cVE.n)]
                            E("sync", lambda s, j=j, gpar=gpar: s.dma_start(
                                out=cs_sb[:, gpar * 1024:gpar * 1024 + 512],
                                in_=dcos[:, j * 512:(j + 1) * 512]),
                                waits=gw, inc=cCS[gpar], inck=16)
                            E("sync", lambda s, j=j, gpar=gpar: s.dma_start(
                                out=cs_sb[:, gpar * 1024 + 512:gpar * 1024 + 1024],
                                in_=dsin[:, j * 512:(j + 1) * 512]),
                                inc=cCS[gpar], inck=16)
                            cs_ready = cCS[gpar].n
                            rope_tile_idx[0] += 1
                            w0 = [(cVE, xn_done), (cWQ[0], wt_ready)]
                            if j >= 2:
                                w0.append((cVE, ve_tile[j - 2]))
                            elif first_wait:
                                w0.append((cVE, first_wait))
                            for ic in range(NIC):
                                E("tensor", lambda t, ic=ic, j=j, ps=ps, wsb=wsb: t.matmul(
                                    ps, wsb[:, ic * 128:(ic + 1) * 128], xsrc_fn(ic, j),
                                    start=(ic == 0), stop=(ic == NIC - 1)),
                                    waits=w0 if ic == 0 else [],
                                    inc=cTE if ic == NIC - 1 else None)
                            tm = cTE.n
                            qr, qs, r1 = qraw[gpar], qswp[gpar], rt1[gpar]
                            E("vector", lambda v, ps=ps, qr=qr: v.tensor_copy(qr[:, :], ps),
                                waits=[(cTE, tm), (cSW, swp_mark[gpar]), (cQS, qs_mark[gpar])], inc=cVE)
                            vcp = cVE.n
                            for (a, b2) in ((0, 32), (32, 0), (64, 96), (96, 64)):
                                E("sync", lambda s, a=a, b2=b2, qr=qr, qs=qs: s.dma_start(
                                    out=qs[a:a + 32, :], in_=qr[b2:b2 + 32, :]),
                                    waits=[(cVE, vcp)] if a == 0 else [], inc=cSW, inck=16)
                            swp_mark[gpar] = last_swp = cSW.n
                            E("vector", lambda v, gpar=gpar, r1=r1, qr=qr: v.tensor_tensor(
                                r1[:, :], qr[:, :], cs_sb[:, gpar * 1024:gpar * 1024 + 512], ALU.mult),
                                waits=[(cCS[gpar], cs_ready)], inc=cVE)
                            E("vector", lambda v, gpar=gpar, qs=qs: v.tensor_tensor(
                                qs[:, :], qs[:, :], cs_sb[:, gpar * 1024 + 512:gpar * 1024 + 1024], ALU.mult),
                                waits=[(cSW, last_swp)], inc=cVE)
                            E("vector", lambda v, r1=r1, qs=qs: v.tensor_tensor(
                                r1[:, :], r1[:, :], qs[:, :], ALU.add), inc=cVE)
                            vq = cVE.n
                            sl = slice(j * 512, (j + 1) * 512)
                            E("gpsimd", lambda g, sl=sl, dstA=dstA, r1=r1: g.dma_start(
                                out=dstA[0:64, sl], in_=r1[0:64, :]),
                                waits=[(cVE, vq)], inc=cQS, inck=16)
                            E("gpsimd", lambda g, sl=sl, dstB=dstB, r1=r1: g.dma_start(
                                out=dstB[0:64, sl], in_=r1[64:128, :]),
                                inc=cQS, inck=16)
                            qs_mark[gpar] = cQS.n
                            ve_tile.append(cVE.n)
                        return ve_tile

                    rope_q = rope_tiles(
                        wq_sb, 4,
                        lambda ic, j: xn_own[:, ic * T + j * 512: ic * T + (j + 1) * 512],
                        qpA, qpB, prev_norm)
                    rope_k = rope_tiles(
                        wk_sb, 8,
                        lambda ic, j: (xn_own if j < 4 else xn_prt)[:, ic * T + (j % 4) * 512: ic * T + (j % 4 + 1) * 512],
                        kpA, kpB, rope_q[-1])
                    qk_dmas = cQS.n

                    E("vector", lambda v: v.memset(bnd[:, :], 0.0),
                        waits=[(cTE, cTE.n)], inc=cVE)
                    for srcs, col in (((qpA, qpB), 0), ((kpA, kpB), 1)):
                        n_tiles = 4 if col == 0 else 8
                        for j in range(n_tiles):
                            for hh, srch in enumerate(srcs):
                                E("scalar", lambda s, srch=srch, j=j: s.activation(
                                    out=MA[0:64, 0:512], in_=srch[0:64, j * 512:(j + 1) * 512], func=AF.Square),
                                    waits=[(cQS, qk_dmas), (cTE, cTE.n), (cSC, cSC.n)], inc=cSC)
                                row = hh * 32
                                E("tensor", lambda t, row=row: t.matmul(
                                    pA[row:row + 1, 0:512], onescol_bf[0:64, :], MA[0:64, 0:512],
                                    start=True, stop=True),
                                    waits=[(cSC, cSC.n), (cVE, cVE.n)], inc=cTE)
                                E("vector", lambda v, row=row: v.tensor_reduce(
                                    bnd[row:row + 1, 5:6], pA[row:row + 1, 0:512], AX.X, ALU.max),
                                    waits=[(cTE, cTE.n)], inc=cVE)
                            E("vector", lambda v, col=col: v.tensor_tensor(
                                bnd[:, col:col + 1], bnd[:, col:col + 1], bnd[:, 5:6], ALU.max), inc=cVE)
                    E("vector", lambda v: v.tensor_tensor(
                        bnd[:, 2:3], bnd[:, 0:1], bnd[:, 1:2], ALU.mult), inc=cVE)
                    E("scalar", lambda s: s.activation(
                        out=bnd[:, 3:4], in_=bnd[:, 2:3], func=AF.Sqrt),
                        waits=[(cVE, cVE.n)], inc=cSC)
                    E("vector", lambda v: v.tensor_scalar(
                        bnd[:, 4:5], bnd[:, 3:4], -0.1275 / 8.0, None, ALU.mult),
                        waits=[(cSC, cSC.n)], inc=cVE)
                    E("vector", lambda v: v.tensor_scalar(
                        bnd[:, 6:7], bnd[:, 3:4], -0.1275 * 8.0, None, ALU.mult), inc=cVE)
                    bnd_ve = cVE.n
                    E("tensor", lambda t: t.matmul(
                        pA[0:128, 0:1], ones_f32[0:1, :], bnd[0:1, 4:5], start=True, stop=True),
                        waits=[(cVE, bnd_ve)])
                    E("tensor", lambda t: t.matmul(
                        pA[0:128, 512:513], ones_f32[32:33, :], bnd[32:33, 4:5], start=True, stop=True),
                        inc=cTE)
                    E("vector", lambda v: v.tensor_copy(c8col[:, 0:1], pA[0:128, 0:1]),
                        waits=[(cTE, cTE.n)], inc=cVE)
                    E("vector", lambda v: v.tensor_copy(c8col[:, 1:2], pA[0:128, 512:513]), inc=cVE)
                    c8_ve = cVE.n

                    v_marks = []
                    for vt in range(32):
                        ps = pA[:, (vt % 2) * 512: (vt % 2) * 512 + 128]
                        xsb = xn_own if vt < 16 else xn_prt
                        tcol = (vt % 16) * 128
                        w0 = [(cVE, c8_ve), (cWQ[0], wt_ready)]
                        if vt >= 2:
                            w0.append((cVE, v_marks[vt - 2]))
                        for ic in range(NIC):
                            E("tensor", lambda t, ic=ic, ps=ps, xsb=xsb, tcol=tcol: t.matmul(
                                ps, xsb[:, ic * T + tcol: ic * T + tcol + 128],
                                wv_sb[:, ic * 128:(ic + 1) * 128],
                                start=(ic == 0), stop=(ic == NIC - 1)),
                                waits=w0 if ic == 0 else [],
                                inc=cTE if ic == NIC - 1 else None)
                        tm = cTE.n
                        E("vector", lambda v, ps=ps, vt=vt: v.tensor_copy(
                            vsb[:, vt * 130:vt * 130 + 64], ps[:, 0:64]),
                            waits=[(cTE, tm)], inc=cVE)
                        E("vector", lambda v, ps=ps, vt=vt: v.tensor_copy(
                            vsb[:, vt * 130 + 65:vt * 130 + 129], ps[:, 64:128]), inc=cVE)
                        v_marks.append(cVE.n)
                    v_done = cVE.n

                    if debug and p == 0:
                        E("gpsimd", lambda g: g.dma_start(out=dbg["q"][0:64, :], in_=qpA[0:64, :]),
                            waits=[(cVE, v_done)], inc=cDBG, inck=16)
                        E("gpsimd", lambda g: g.dma_start(out=dbg["q"][64:128, :], in_=qpB[0:64, :]), inc=cDBG, inck=16)
                        E("gpsimd", lambda g: g.dma_start(out=dbg["k"][0:64, :], in_=kpA[0:64, :]), inc=cDBG, inck=16)
                        E("gpsimd", lambda g: g.dma_start(out=dbg["k"][64:128, :], in_=kpB[0:64, :]), inc=cDBG, inck=16)
                        E("gpsimd", lambda g: g.dma_start(out=dbg["v"][:, :], in_=vsb[:, :]), inc=cDBG, inck=16)

                    for qc in range(2):
                        qoff = qc * 1024
                        maxm = []
                        for kt in range(32):
                            par = kt % 2
                            for head, (kph, qph, psh, Mh) in enumerate(
                                    ((kpA, qpA, pA, MA), (kpB, qpB, pB, MB))):
                                pslice = psh[:, par * 1024:(par + 1) * 1024]
                                for half in range(2):
                                    w0 = []
                                    if half == 0 and head == 0:
                                        if kt >= 2:
                                            w0 = [(cVE, maxm[kt - 2])]
                                        else:
                                            w0 = [(cVE, prev_norm)] if (qc > 0 or p > 0) else [(cVE, v_done)]
                                        if kt == 0:
                                            w0.append((cVE, v_done))
                                    E("tensor", lambda t, kt=kt, half=half, qoff=qoff,
                                        kph=kph, qph=qph, pslice=pslice: t.matmul(
                                        pslice[:, half * 512:(half + 1) * 512],
                                        kph[0:64, kt * 128:(kt + 1) * 128],
                                        qph[0:64, qoff + half * 512: qoff + (half + 1) * 512],
                                        start=True, stop=True),
                                        waits=w0, inc=cTE if half == 1 else None)
                                tmh = cTE.n
                                if kt == 0:
                                    E("vector", lambda v, Mh=Mh, pslice=pslice: v.tensor_copy(
                                        Mh[:, :], pslice),
                                        waits=[(cTE, tmh), (cSC, cSC.n)], inc=cVE)
                                else:
                                    E("vector", lambda v, Mh=Mh, pslice=pslice: v.tensor_tensor(
                                        Mh[:, :], pslice, Mh[:, :], ALU.max),
                                        waits=[(cTE, tmh)], inc=cVE)
                            maxm.append(cVE.n)
                        for head, (Mh, qph, row) in enumerate(((MA, qpA, 0), (MB, qpB, 32))):
                            E("scalar", lambda s, Mh=Mh, head=head: s.activation(
                                out=Mh[:, :], in_=Mh[:, :], func=AF.Exp,
                                bias=c8col[:, head:head + 1], scale=0.125 / 8.0),
                                waits=[(cVE, maxm[-1]), (cTE, cTE.n), (cSC, cSC.n)], inc=cSC)
                            scm = cSC.n
                            for half in range(2):
                                E("tensor", lambda t, row=row, Mh=Mh, half=half: t.matmul(
                                    pA[row:row + 1, half * 512:(half + 1) * 512],
                                    onescol_bf[:, :], Mh[:, half * 512:(half + 1) * 512],
                                    start=True, stop=True),
                                    waits=[(cSC, scm)] if half == 0 else [],
                                    inc=cTE if half == 1 else None)
                            E("vector", lambda v, row=row: v.tensor_single_scalar(
                                bc_sb[row:row + 1, 0:1024], pA[row:row + 1, 0:1024], 1e-38, ALU.add),
                                waits=[(cTE, cTE.n)], inc=cVE)
                            E("scalar", lambda s, row=row: s.activation(
                                out=bc_sb[row:row + 1, 0:1024], in_=bc_sb[row:row + 1, 0:1024],
                                func=AF.Ln),
                                waits=[(cVE, cVE.n)], inc=cSC)
                        E("vector", lambda v: v.tensor_scalar(
                            bc_sb[0:33, 0:1024], bc_sb[0:33, 0:1024], -64.0, bnd[:, 6:7], ALU.mult, ALU.add),
                            waits=[(cSC, cSC.n)], inc=cVE)
                        lnm = cVE.n
                        E("gpsimd", lambda g, qoff=qoff: g.dma_start(
                            out=qpA[64:65, qoff:qoff + 1024], in_=bc_sb[0:1, 0:1024]),
                            waits=[(cVE, lnm)], inc=cU, inck=16)
                        E("gpsimd", lambda g, qoff=qoff: g.dma_start(
                            out=qpB[64:65, qoff:qoff + 1024], in_=bc_sb[32:33, 0:1024]),
                            inc=cU, inck=16)
                        u_dma = cU.n

                        expA, expB = [], []
                        for kt in range(32):
                            par = kt % 2
                            for half in range(2):
                                w0 = []
                                if half == 0:
                                    w0 = [(cSC, expA[kt - 1])] if kt >= 1 else [(cU, u_dma), (cSC, cSC.n)]
                                E("tensor", lambda t, kt=kt, half=half, qoff=qoff: t.matmul(
                                    pA[:, half * 512:(half + 1) * 512],
                                    kpA[0:65, kt * 128:(kt + 1) * 128],
                                    qpA[0:65, qoff + half * 512: qoff + (half + 1) * 512],
                                    start=True, stop=True),
                                    waits=w0, inc=cTE if half == 1 else None)
                            qkA = cTE.n
                            for half in range(2):
                                w0 = []
                                if half == 0 and kt >= 1:
                                    w0 = [(cSC, expB[kt - 1])]
                                E("tensor", lambda t, kt=kt, half=half, qoff=qoff: t.matmul(
                                    pB[:, half * 512:(half + 1) * 512],
                                    kpB[0:65, kt * 128:(kt + 1) * 128],
                                    qpB[0:65, qoff + half * 512: qoff + (half + 1) * 512],
                                    start=True, stop=True),
                                    waits=w0, inc=cTE if half == 1 else None)
                            qkB = cTE.n
                            if kt > 0:
                                ktm = kt - 1
                                parm = ktm % 2
                                for half in range(2):
                                    E("tensor", lambda t, ktm=ktm, parm=parm, half=half: t.matmul(
                                        pA[0:65, 1024 + half * 512:1024 + (half + 1) * 512],
                                        vsb[:, ktm * 130:ktm * 130 + 65],
                                        pt[:, parm * 1024 + half * 512: parm * 1024 + (half + 1) * 512],
                                        start=(ktm == 0), stop=False))
                                    E("tensor", lambda t, ktm=ktm, parm=parm, half=half: t.matmul(
                                        pB[64:128, 1024 + half * 512:1024 + (half + 1) * 512],
                                        vsb[:, ktm * 130 + 65:ktm * 130 + 129],
                                        pt[:, 2048 + parm * 1024 + half * 512: 2048 + parm * 1024 + (half + 1) * 512],
                                        start=(ktm == 0), stop=False, tile_position=(0, 64)))
                                    E("tensor", lambda t, ktm=ktm, parm=parm, half=half: t.matmul(
                                        pB[0:1, 1024 + half * 512:1024 + (half + 1) * 512],
                                        onescol_bf[:, :],
                                        pt[:, 2048 + parm * 1024 + half * 512: 2048 + parm * 1024 + (half + 1) * 512],
                                        start=(ktm == 0), stop=False))
                            E("scalar", lambda s, par=par: s.activation(
                                out=pt[:, par * 1024:(par + 1) * 1024], in_=pA[:, 0:1024],
                                func=AF.Exp, scale=0.125),
                                waits=[(cTE, qkA)], inc=cSC)
                            expA.append(cSC.n)
                            E("scalar", lambda s, par=par: s.activation(
                                out=pt[:, 2048 + par * 1024:2048 + (par + 1) * 1024], in_=pB[:, 0:1024],
                                func=AF.Exp, scale=0.125),
                                waits=[(cTE, qkB)], inc=cSC)
                            expB.append(cSC.n)
                        ktm, parm = 31, 1
                        for half in range(2):
                            E("tensor", lambda t, half=half, parm=parm, ktm=ktm: t.matmul(
                                pA[0:65, 1024 + half * 512:1024 + (half + 1) * 512],
                                vsb[:, ktm * 130:ktm * 130 + 65],
                                pt[:, parm * 1024 + half * 512: parm * 1024 + (half + 1) * 512],
                                start=False, stop=True),
                                waits=[(cSC, expA[31])] if half == 0 else [])
                            E("tensor", lambda t, half=half, parm=parm, ktm=ktm: t.matmul(
                                pB[64:128, 1024 + half * 512:1024 + (half + 1) * 512],
                                vsb[:, ktm * 130 + 65:ktm * 130 + 129],
                                pt[:, 2048 + parm * 1024 + half * 512: 2048 + parm * 1024 + (half + 1) * 512],
                                start=False, stop=True, tile_position=(0, 64)),
                                waits=[(cSC, expB[31])] if half == 0 else [])
                            E("tensor", lambda t, half=half, parm=parm: t.matmul(
                                pB[0:1, 1024 + half * 512:1024 + (half + 1) * 512],
                                onescol_bf[:, :],
                                pt[:, 2048 + parm * 1024 + half * 512: 2048 + parm * 1024 + (half + 1) * 512],
                                start=False, stop=True),
                                inc=cTE if half == 1 else None)
                        pv_te = cTE.n
                        E("vector", lambda v: v.tensor_copy(den_sb[64:65, :], pA[64:65, 1024:2048]),
                            waits=[(cTE, pv_te)], inc=cVE)
                        E("vector", lambda v: v.tensor_copy(den_sb[0:1, :], pB[0:1, 1024:2048]), inc=cVE)
                        E("vector", lambda v: v.reciprocal(den_sb[0:65, :], den_sb[0:65, :]), inc=cVE)
                        den_ve = cVE.n
                        for half in range(2):
                            E("tensor", lambda t, half=half: t.matmul(
                                pA[:, half * 512:(half + 1) * 512],
                                ones_f32[64:65, :], den_sb[64:65, half * 512:(half + 1) * 512],
                                start=True, stop=True),
                                waits=[(cVE, den_ve), (cSC, expA[31])] if half == 0 else [],
                                inc=cTE if half == 1 else None)
                        E("vector", lambda v: v.tensor_copy(bc_sb[:, :], pA[:, 0:1024]),
                            waits=[(cTE, cTE.n)], inc=cVE)
                        E("vector", lambda v, p=p, qoff=qoff: v.tensor_tensor(
                            ao_sb[0:64, p * T + qoff: p * T + qoff + 1024],
                            pA[0:64, 1024:2048], bc_sb[0:64, :], ALU.mult),
                            waits=[(cVE, cVE.n)], inc=cVE)
                        for half in range(2):
                            E("tensor", lambda t, half=half: t.matmul(
                                pB[:, half * 512:(half + 1) * 512],
                                ones_f32[0:1, :], den_sb[0:1, half * 512:(half + 1) * 512],
                                start=True, stop=True),
                                waits=[(cVE, cVE.n), (cSC, expB[31])] if half == 0 else [],
                                inc=cTE if half == 1 else None)
                        E("vector", lambda v: v.tensor_copy(bc_sb[:, :], pB[:, 0:1024]),
                            waits=[(cTE, cTE.n), (cVE, cVE.n)], inc=cVE)
                        E("vector", lambda v, p=p, qoff=qoff: v.tensor_tensor(
                            ao_sb[64:128, p * T + qoff: p * T + qoff + 1024],
                            pB[64:128, 1024:2048], bc_sb[64:128, :], ALU.mult),
                            waits=[(cVE, cVE.n)], inc=cVE)
                        prev_norm = cVE.n

                ao_done = cVE.n
                if debug:
                    for ic in range(NIC):
                        E("gpsimd", lambda g, ic=ic: g.dma_start(
                            out=dbg["ao"][ic * 128:(ic + 1) * 128, :], in_=ao_sb[:, ic * T:(ic + 1) * T]),
                            waits=[(cVE, ao_done)], inc=cDBG, inck=16)

                sch.flush(nc)
                ap_done_box[0] = ao_done

    with contextlib.ExitStack() as s2:
        bufA = s2.enter_context(nc.sbuf_tensor("bufA", [128, NHC * TCT], BF16))
        bufB = s2.enter_context(nc.sbuf_tensor("bufB", [128, NHC * TCT], BF16))
        wh = [s2.enter_context(nc.sbuf_tensor(f"wh{j}", [128, 4096], BF16)) for j in range(3)]
        xres = [s2.enter_context(nc.sbuf_tensor(f"xres{j}", [128, 512], F32)) for j in range(2)]
        ytmp = s2.enter_context(nc.sbuf_tensor("ytmp", [128, 512], F32))
        ysb = [s2.enter_context(nc.sbuf_tensor(f"ysb{j}", [128, 512], F32)) for j in range(2)]

        with nc.psum_tensor("pF", [128, 2048], F32) as pF, \
             nc.psum_tensor("pF2", [128, 2048], F32) as pF2:
            prev_tail = [(cVE, ap_done_box[0])] * 4
            prev_wh_marks = [cTE.n, cTE.n, cTE.n]
            y_ve_marks = []
            st_marks = []

            opdone = [0]

            def gen_op():
                ao_done = ap_done_box[0]
                E("gpsimd", lambda g: g.dma_start(out=wo_sb[0][:, :], in_=dwout[0:128, :]),
                    waits=[(cTE, cTE.n)], inc=cWO[0], inck=16)
                op_marks = []
                for oc in range(NIC):
                    ob = oc % 2
                    wt_r = cWO[ob].n
                    if oc < NIC - 1:
                        E("gpsimd", lambda g, oc=oc: g.dma_start(
                            out=wo_sb[(oc + 1) % 2][:, :], in_=dwout[(oc + 1) * 128:(oc + 2) * 128, :]),
                            waits=[(cTE, op_marks[oc * 4 - 1][1]) if oc >= 1 else (cTE, cTE.n)],
                            inc=cWO[(oc + 1) % 2], inck=16)
                    for qt in range(4):
                        idx = oc * 4 + qt
                        ps = pF2[:, (idx % 4) * 512:(idx % 4 + 1) * 512]
                        w0 = [(cVE, ao_done), (cWO[ob], wt_r)]
                        if idx >= 4:
                            w0.append((cVE, op_marks[idx - 4][0]))
                        for dc in range(NIC):
                            E("tensor", lambda t, dc=dc, qt=qt, ps=ps, ob=ob: t.matmul(
                                ps, wo_sb[ob][:, dc * 128:(dc + 1) * 128],
                                ao_sb[:, dc * T + qt * 512:dc * T + (qt + 1) * 512],
                                start=(dc == 0), stop=(dc == NIC - 1)),
                                waits=w0 if dc == 0 else [],
                                inc=cTE if dc == NIC - 1 else None)
                        tm = cTE.n
                        E("vector", lambda v, oc=oc, qt=qt, ps=ps: v.tensor_copy(
                            attn_proj[:, oc * T + qt * 512:oc * T + (qt + 1) * 512], ps),
                            waits=[(cTE, tm)], inc=cVE)
                        op_marks.append((cVE.n, tm))
                        yield
                opdone[0] = cVE.n

            def gen_stage(dw, n_oc, n_ic, ncols, rhs_fn, evac_fn, first_extra=None,
                          dwc=None, first_pass=True):
                nonlocal prev_tail, prev_wh_marks
                te_marks = []
                ev = []
                for oc in range(n_oc):
                    wb = oc % 3
                    lw = [(cTE, te_marks[oc - 3])] if oc >= 3 else [(cTE, prev_wh_marks[oc])]
                    if dwc is not None and first_pass:
                        lw.append((cWB[wb % 2], cWB[wb % 2].n))
                    src_t = dw if (dwc is None or first_pass) else dwc
                    E("gpsimd", lambda g, oc=oc, wb=wb, src_t=src_t, ncols=ncols: g.dma_start(
                        out=wh[wb][:, 0:ncols], in_=src_t[oc * 128:(oc + 1) * 128, :]),
                        waits=lw, inc=cWH[wb], inck=16)
                    wt_r = cWH[wb].n
                    if dwc is not None and first_pass:
                        E("sync", lambda s, oc=oc, wb=wb, dwc=dwc, ncols=ncols: s.dma_start(
                            out=dwc[oc * 128:(oc + 1) * 128, :], in_=wh[wb][:, 0:ncols]),
                            waits=[(cWH[wb], wt_r)], inc=cWB[wb % 2], inck=16)
                    ps = pF[:, 512 * (oc % 4):512 * (oc % 4 + 1)]
                    w0 = [(cWH[wb], wt_r)]
                    if first_extra is not None and oc == 0:
                        w0.append(first_extra)
                    if oc >= 4:
                        w0.append(ev[oc - 4])
                    else:
                        w0.append(prev_tail[oc])
                    for ic in range(n_ic):
                        E("tensor", lambda t, ic=ic, ps=ps, wb=wb: t.matmul(
                            ps, wh[wb][:, ic * 128:(ic + 1) * 128], rhs_fn(ic),
                            start=(ic == 0), stop=(ic == n_ic - 1)),
                            waits=w0 if ic == 0 else [],
                            inc=cTE if ic == n_ic - 1 else None)
                    te_marks.append(cTE.n)
                    ev.append(evac_fn(oc, ps, cTE.n))
                    yield
                prev_tail = ev[-4:]
                prev_wh_marks = [te_marks[-3], te_marks[-2], te_marks[-1]]
                evout[0] = ev

            def stage(*a, **k):
                evout[0] = None
                for _ in gen_stage(*a, **k):
                    pass
                return evout[0]

            evout = [None]

            for tct in range(NTCT):
                tof = tct * TCT

                def evac_w2(oc, ps, tm):
                    E("vector", lambda v, oc=oc, ps=ps: v.tensor_scalar(
                        bufB[:, oc * TCT:(oc + 1) * TCT], ps, s2_sb[:, oc:oc + 1], None, ALU.mult),
                        waits=[(cTE, tm)], inc=cVE)
                    return (cVE, cVE.n)

                def evac_w1(oc, ps, tm):
                    E("scalar", lambda s, oc=oc, ps=ps: s.activation(
                        out=bufA[:, oc * TCT:(oc + 1) * TCT], in_=ps,
                        func=AF.Silu, scale=s1_sb[:, oc:oc + 1]),
                        waits=[(cTE, tm)], inc=cSC)
                    scm = cSC.n
                    E("vector", lambda v, oc=oc: v.tensor_tensor(
                        bufB[:, oc * TCT:(oc + 1) * TCT],
                        bufA[:, oc * TCT:(oc + 1) * TCT],
                        bufB[:, oc * TCT:(oc + 1) * TCT], ALU.mult),
                        waits=[(cSC, scm), (cVE, w2ev[oc][1])], inc=cVE)
                    return (cSC, scm)

                def evac_h1(oc, ps, tm):
                    E("scalar", lambda s, oc=oc, ps=ps: s.activation(
                        out=bufA[:, oc * TCT:(oc + 1) * TCT], in_=ps,
                        func=AF.Gelu, scale=h1s_sb[:, oc:oc + 1]),
                        waits=[(cTE, tm)], inc=cSC)
                    return (cSC, cSC.n)

                def evac_h2(oc, ps, tm):
                    E("scalar", lambda s, oc=oc, ps=ps: s.activation(
                        out=bufB[:, oc * TCT:(oc + 1) * TCT], in_=ps,
                        func=AF.Gelu, scale=h2s_sb[:, oc:oc + 1]),
                        waits=[(cTE, tm)], inc=cSC)
                    return (cSC, cSC.n)

                def evac_w3(oc, ps, tm):
                    jb = oc % 2
                    nst = len(st_marks)
                    lw = [(cVE, y_ve_marks[nst - 2])] if nst >= 2 else []
                    E("sync", lambda s, oc=oc, jb=jb, tof=tof: s.dma_start(
                        out=xres[jb][:, :], in_=dx[oc * 128:(oc + 1) * 128, tof:tof + TCT]),
                        waits=lw, inc=cXR[jb], inck=16)
                    ld = cXR[jb].n
                    E("vector", lambda v, oc=oc, ps=ps, tof=tof: v.scalar_tensor_tensor(
                        ytmp[:, :], ps, w3s_sb[:, oc:oc + 1],
                        attn_proj[:, oc * T + tof: oc * T + tof + TCT],
                        ALU.mult, ALU.add),
                        waits=[(cTE, tm), (cVE, opdone[0])], inc=cVE)
                    sw = [(cXR[jb], ld)]
                    if nst >= 2:
                        sw.append((cST[jb], st_marks[nst - 2]))
                    E("vector", lambda v, jb=jb: v.tensor_tensor(
                        ysb[jb][:, :], ytmp[:, :], xres[jb][:, :], ALU.add),
                        waits=sw, inc=cVE)
                    yv = cVE.n
                    E("sync", lambda s, oc=oc, jb=jb, tof=tof: s.dma_start(
                        out=dy[oc * 128:(oc + 1) * 128, tof:tof + TCT], in_=ysb[jb][:, :]),
                        waits=[(cVE, yv)], inc=cST[jb], inck=16)
                    y_ve_marks.append(yv)
                    st_marks.append(cST[jb].n)
                    return (cVE, cVE.n)

                if tct == 0:
                    evout[0] = None
                    g1 = gen_stage(dw2, NHC, NIC, 1024,
                                   lambda ic: xn_own[:, ic * T + tof: ic * T + tof + TCT],
                                   evac_w2, first_extra=(cTAB, scales_wt),
                                   dwc=dw2c, first_pass=True)
                    g2 = gen_op()
                    live = [g1, g2]
                    while live:
                        for g in live[:]:
                            try:
                                next(g)
                            except StopIteration:
                                live.remove(g)
                    ew2 = evout[0]
                else:
                    ew2 = stage(dw2, NHC, NIC, 1024,
                                lambda ic: xn_own[:, ic * T + tof: ic * T + tof + TCT],
                                evac_w2, first_extra=(cTAB, scales_wt),
                                dwc=dw2c, first_pass=False)
                w2ev = ew2
                ew1 = stage(dw1, NHC, NIC, 1024,
                            lambda ic: xn_own[:, ic * T + tof: ic * T + tof + TCT],
                            evac_w1, dwc=dw1c, first_pass=(tct == 0))
                h_done = cVE.n
                eh1 = stage(dh1, NHC, NHC, 4096,
                            lambda ic: bufB[:, ic * TCT:(ic + 1) * TCT],
                            evac_h1, first_extra=(cVE, h_done),
                            dwc=dh1c, first_pass=(tct == 0))
                eh2 = stage(dh2, NHC, NHC, 4096,
                            lambda ic: bufA[:, ic * TCT:(ic + 1) * TCT],
                            evac_h2, first_extra=(cSC, eh1[-1][1]),
                            dwc=dh2c, first_pass=(tct == 0))
                stage(dw3, NIC, NHC, 4096,
                      lambda ic: bufB[:, ic * TCT:(ic + 1) * TCT],
                      evac_w3, first_extra=(cSC, eh2[-1][1]),
                      dwc=dw3c, first_pass=(tct == 0))

            E("sync", lambda s: None, waits=[(cST[0], cST[0].n), (cST[1], cST[1].n)])
            sch.flush(nc)

    stack.close()
    return nc, dbg



def _strip_img(wT, n_ic):
    return wT.reshape(n_ic, 128, 128).transpose(1, 0, 2).reshape(128, n_ic * 128)


def _weight_img(wT, n_ic, n_oc):
    return np.ascontiguousarray(np.concatenate(
        [_strip_img(wT[:, oc * 128:(oc + 1) * 128], n_ic) for oc in range(n_oc)], axis=0))


def _rope_tables(h):
    pos = np.concatenate([np.arange(h * T, (h + 1) * T),
                          np.arange((1 - h) * T, (2 - h) * T)]).astype(np.float64)
    inv = ROPE_BASE ** (-np.arange(32, dtype=np.float64) / 32.0)
    ang = pos[:, None] * inv[None, :]
    c32 = np.cos(ang).astype(np.float32)
    s32 = np.sin(ang).astype(np.float32)
    cos_rows = np.zeros((128, L), np.float32)
    sin_rows = np.zeros((128, L), np.float32)
    for d in range(128):
        dd = d % 64
        j = dd % 32
        cos_rows[d] = c32[:, j]
        sin_rows[d] = (-1.0 if dd < 32 else 1.0) * s32[:, j]
    return (np.ascontiguousarray(cos_rows.astype(ml_dtypes.bfloat16)),
            np.ascontiguousarray(sin_rows.astype(ml_dtypes.bfloat16)))


def make_in_maps(x, wqkv, wout, w1w, w1s, w2w, w2s, h1w, h1s, h2w, h2s, w3w, w3s):
    x = np.asarray(x, np.float32)
    wqkvT = np.ascontiguousarray(np.asarray(wqkv, np.float32).T)
    woutT = np.ascontiguousarray(np.asarray(wout, np.float32).T)
    w1T = np.ascontiguousarray(np.asarray(w1w, np.float32).T)
    w2T = np.ascontiguousarray(np.asarray(w2w, np.float32).T)
    h1T = np.ascontiguousarray(np.asarray(h1w, np.float32).T)
    h2T = np.ascontiguousarray(np.asarray(h2w, np.float32).T)
    w3T = np.ascontiguousarray(np.asarray(w3w, np.float32).T)

    blocks = []
    for p in range(NPAIR):
        for r in range(3):
            blocks.append(_strip_img(wqkvT[:, r * D + p * 128: r * D + (p + 1) * 128], NIC))
    wqkv_img = np.ascontiguousarray(np.concatenate(blocks, axis=0))
    imgs = {
        "wqkv": wqkv_img,
        "wout": _weight_img(woutT, NIC, NIC),
        "w1": _weight_img(w1T, NIC, NHC),
        "w2": _weight_img(w2T, NIC, NHC),
        "h1": _weight_img(h1T, NHC, NHC),
        "h2": _weight_img(h2T, NHC, NHC),
        "w3": _weight_img(w3T, NHC, NIC),
        "s1": np.ascontiguousarray(np.asarray(w1s, np.float32).reshape(32, 128).T),
        "s2": np.ascontiguousarray(np.asarray(w2s, np.float32).reshape(32, 128).T),
        "h1s": np.ascontiguousarray(np.asarray(h1s, np.float32).reshape(32, 128).T),
        "h2s": np.ascontiguousarray(np.asarray(h2s, np.float32).reshape(32, 128).T),
        "w3s": np.ascontiguousarray(np.asarray(w3s, np.float32).reshape(8, 128).T),
    }
    tabs = [_rope_tables(0), _rope_tables(1)]
    in_maps = []
    for c in range(8):
        b, h = c // 2, c % 2
        xr = np.ascontiguousarray(
            np.concatenate([x[b, h * T:(h + 1) * T, :], x[b, (1 - h) * T:(2 - h) * T, :]], axis=0).T)
        m = {"x": xr, "costab": tabs[h][0], "sintab": tabs[h][1]}
        m.update(imgs)
        in_maps.append(m)
    return in_maps


def kernel(L0_attn_x, L0_attn_w_qkv, L0_attn_w_out,
           L0_ffn_w1_w, L0_ffn_w1_s, L0_ffn_w2_w, L0_ffn_w2_s,
           L0_ffn_h1_w, L0_ffn_h1_s, L0_ffn_h2_w, L0_ffn_h2_s,
           L0_ffn_w3_w, L0_ffn_w3_s):
    in_maps = make_in_maps(L0_attn_x, L0_attn_w_qkv, L0_attn_w_out,
                           L0_ffn_w1_w, L0_ffn_w1_s, L0_ffn_w2_w, L0_ffn_w2_s,
                           L0_ffn_h1_w, L0_ffn_h1_s, L0_ffn_h2_w, L0_ffn_h2_s,
                           L0_ffn_w3_w, L0_ffn_w3_s)
    nc, _ = build_nc(debug=False)
    res = run_bass_kernel_spmd(nc, in_maps, core_ids=list(range(8)))
    out = np.empty((B, L, D), np.float32)
    for c in range(8):
        b, h = c // 2, c % 2
        out[b, h * T:(h + 1) * T, :] = np.asarray(res.results[c]["y"]).T
    return out


# version 09
# speedup vs baseline: 1.0155x; 1.0155x over previous
import contextlib

import numpy as np
import ml_dtypes

import concourse.bass as bass
import concourse.mybir as mybir
from concourse.bass_utils import run_bass_kernel_spmd

F32 = mybir.dt.float32
BF16 = mybir.dt.bfloat16
AF = mybir.ActivationFunctionType
ALU = mybir.AluOpType
AX = mybir.AxisListType

B, L, D = 4, 4096, 1024
T = 2048
H, DH = 16, 64
HID = 4096
NIC = D // 128
NHC = HID // 128
NPAIR = H // 2
TCT = 512
NTCT = T // TCT
ROPE_BASE = 10000.0
EPS = 1e-5


class Ctr:

    def __init__(self, stack, nc, name):
        self.sem = stack.enter_context(nc.semaphore(name))
        self.n = 0


class Sched:

    ENGINES = ("tensor", "vector", "scalar", "sync", "gpsimd")

    def __init__(self):
        self.ops = {e: [] for e in self.ENGINES}

    def emit(self, engine, fn, waits=(), inc=None, inck=1):
        if inc is not None:
            inc.n += inck
            inc = (inc, inck)
        self.ops[engine].append((fn, tuple(waits), inc))

    def flush(self, nc):
        with nc.Block() as block:
            def mk(eng_name):
                def _(eng):
                    for fn, waits, inc in self.ops[eng_name]:
                        for ctr, val in waits:
                            if val > 0:
                                eng.wait_ge(ctr.sem, val)
                        inst = fn(eng)
                        if inc is not None:
                            ctr, k = inc
                            inst.then_inc(ctr.sem, k)
                return _
            for name in self.ENGINES:
                if self.ops[name]:
                    getattr(block, name)(mk(name))
        self.ops = {e: [] for e in self.ENGINES}


def build_nc(debug=False):
    stack = contextlib.ExitStack()
    nc = bass.Bass("TRN2", target_bir_lowering=False, detect_race_conditions=False)

    dx = nc.dram_tensor("x", [D, L], F32, kind="ExternalInput")
    dwqkv = nc.dram_tensor("wqkv", [NPAIR * 3 * 128, NIC * 128], F32, kind="ExternalInput")
    dwout = nc.dram_tensor("wout", [NIC * 128, NIC * 128], F32, kind="ExternalInput")
    dw1 = nc.dram_tensor("w1", [NHC * 128, NIC * 128], F32, kind="ExternalInput")
    dw2 = nc.dram_tensor("w2", [NHC * 128, NIC * 128], F32, kind="ExternalInput")
    dh1 = nc.dram_tensor("h1", [NHC * 128, NHC * 128], F32, kind="ExternalInput")
    dh2 = nc.dram_tensor("h2", [NHC * 128, NHC * 128], F32, kind="ExternalInput")
    dw3 = nc.dram_tensor("w3", [NIC * 128, NHC * 128], F32, kind="ExternalInput")
    ds1 = nc.dram_tensor("s1", [128, 32], F32, kind="ExternalInput")
    ds2 = nc.dram_tensor("s2", [128, 32], F32, kind="ExternalInput")
    dh1s = nc.dram_tensor("h1s", [128, 32], F32, kind="ExternalInput")
    dh2s = nc.dram_tensor("h2s", [128, 32], F32, kind="ExternalInput")
    dw3s = nc.dram_tensor("w3s", [128, 8], F32, kind="ExternalInput")
    dw1c = nc.dram_tensor("w1c", [NHC * 128, NIC * 128], BF16, kind="Internal")
    dw2c = nc.dram_tensor("w2c", [NHC * 128, NIC * 128], BF16, kind="Internal")
    dh1c = nc.dram_tensor("h1c", [NHC * 128, NHC * 128], BF16, kind="Internal")
    dh2c = nc.dram_tensor("h2c", [NHC * 128, NHC * 128], BF16, kind="Internal")
    dw3c = nc.dram_tensor("w3c", [NIC * 128, NHC * 128], BF16, kind="Internal")
    dcos = nc.dram_tensor("costab", [128, L], BF16, kind="ExternalInput")
    dsin = nc.dram_tensor("sintab", [128, L], BF16, kind="ExternalInput")
    dy = nc.dram_tensor("y", [D, T], F32, kind="ExternalOutput")
    dbg = {}
    if debug:
        dbg["xn"] = nc.dram_tensor("dbg_xn", [D, L], F32, kind="ExternalOutput")
        dbg["q"] = nc.dram_tensor("dbg_q", [128, T], F32, kind="ExternalOutput")
        dbg["k"] = nc.dram_tensor("dbg_k", [128, L], F32, kind="ExternalOutput")
        dbg["v"] = nc.dram_tensor("dbg_v", [128, 32 * 130], F32, kind="ExternalOutput")
        dbg["ao"] = nc.dram_tensor("dbg_ao", [D, T], F32, kind="ExternalOutput")
        dbg["ap"] = nc.dram_tensor("dbg_ap", [D, T], F32, kind="ExternalOutput")

    cX = [Ctr(stack, nc, f"cX{j}") for j in range(2)]
    cSW = Ctr(stack, nc, "cSW")
    cTAB = Ctr(stack, nc, "cTAB")
    cWQ = [Ctr(stack, nc, f"cWQ{j}") for j in range(2)]
    cWO = [Ctr(stack, nc, f"cWO{j}") for j in range(2)]
    cWH = [Ctr(stack, nc, f"cWH{j}") for j in range(3)]
    cXR = [Ctr(stack, nc, f"cXR{j}") for j in range(2)]
    cWB = [Ctr(stack, nc, f"cWB{j}") for j in range(2)]
    cST = [Ctr(stack, nc, f"cST{j}") for j in range(2)]
    cQS = Ctr(stack, nc, "cQS")
    cU = Ctr(stack, nc, "cU")
    cCS = [Ctr(stack, nc, f"cCS{j}") for j in range(2)]
    cDBG = Ctr(stack, nc, "cDBG")
    cTE = Ctr(stack, nc, "cTE")
    cSC = Ctr(stack, nc, "cSC")
    cVE = Ctr(stack, nc, "cVE")

    sch = Sched()
    E = sch.emit

    xn_own = stack.enter_context(nc.sbuf_tensor("xn_own", [128, NIC * T], BF16))
    ao_sb = stack.enter_context(nc.sbuf_tensor("ao_sb", [128, NPAIR * T], BF16))
    wo_sb = [stack.enter_context(nc.sbuf_tensor(f"wo{j}", [128, 1024], BF16)) for j in range(2)]
    attn_proj = stack.enter_context(nc.sbuf_tensor("attn_proj", [128, NIC * T], BF16))
    s1_sb = stack.enter_context(nc.sbuf_tensor("s1_sb", [128, 32], F32))
    s2_sb = stack.enter_context(nc.sbuf_tensor("s2_sb", [128, 32], F32))
    h1s_sb = stack.enter_context(nc.sbuf_tensor("h1s_sb", [128, 32], F32))
    h2s_sb = stack.enter_context(nc.sbuf_tensor("h2s_sb", [128, 32], F32))
    w3s_sb = stack.enter_context(nc.sbuf_tensor("w3s_sb", [128, 8], F32))
    ones_f32 = stack.enter_context(nc.sbuf_tensor("ones_f32", [65, 128], F32))
    onescol_f32 = stack.enter_context(nc.sbuf_tensor("onescol_f32", [128, 1], F32))
    onescol_bf = stack.enter_context(nc.sbuf_tensor("onescol_bf", [128, 1], BF16))

    E("vector", lambda v: v.memset(ones_f32[:, :], 1.0), inc=cVE)
    E("vector", lambda v: v.memset(onescol_f32[:, :], 1.0), inc=cVE)
    E("vector", lambda v: v.memset(onescol_bf[:, :], 1.0), inc=cVE)
    const_ve = cVE.n
    for dsrc, dst in ((ds1, s1_sb), (ds2, s2_sb), (dh1s, h1s_sb), (dh2s, h2s_sb), (dw3s, w3s_sb)):
        E("gpsimd", lambda g, s=dsrc, d=dst: g.dma_start(out=d[:, :], in_=s[:, :]), inc=cTAB, inck=16)
    scales_wt = cTAB.n

    ap_done_box = [0]

    with contextlib.ExitStack() as sA:
        xn_prt = sA.enter_context(nc.sbuf_tensor("xn_prt", [128, NIC * T], BF16))

        with contextlib.ExitStack() as s0:
            xch = [s0.enter_context(nc.sbuf_tensor(f"xch{j}", [128, L], F32)) for j in range(2)]
            xsq = [s0.enter_context(nc.sbuf_tensor(f"xsq{j}", [128, 512], F32)) for j in range(2)]
            es_sb = s0.enter_context(nc.sbuf_tensor("es_sb", [33, L], F32))
            mean_bc = s0.enter_context(nc.sbuf_tensor("mean_bc", [128, L], BF16))
            rstd_bc = s0.enter_context(nc.sbuf_tensor("rstd_bc", [128, L], BF16))
            t1 = s0.enter_context(nc.sbuf_tensor("t1", [128, T // 2], F32))
            v1 = t1[:, 0:512]
            v2 = t1[:, 512:1024]

            with nc.psum_tensor("stats_ps", [33, L], F32) as stats_ps:
                te_ic = []
                sc_ic = []
                for ic in range(NIC):
                    bj = ic % 2
                    lw = []
                    if ic >= 2:
                        lw = [(cTE, te_ic[ic - 2]), (cSC, sc_ic[ic - 2])]
                    E("sync", lambda s, ic=ic, bj=bj: s.dma_start(
                        out=xch[bj][:, :], in_=dx[ic * 128:(ic + 1) * 128, :]),
                        waits=lw, inc=cX[bj], inck=16)
                    ldc, ld = cX[bj], cX[bj].n
                    for tc in range(8):
                        sq = tc % 2
                        sqw = [(ldc, ld)]
                        if ic > 0 or tc >= 2:
                            sqw.append((cTE, cTE.n))
                        E("scalar", lambda s, bj=bj, tc=tc, sq=sq: s.activation(
                            out=xsq[sq][:, :], in_=xch[bj][:, tc * 512:(tc + 1) * 512], func=AF.Square),
                            waits=sqw, inc=cSC)
                        scm = cSC.n
                        E("tensor", lambda t, ic=ic, bj=bj, tc=tc: t.matmul(
                            stats_ps[0:1, tc * 512:(tc + 1) * 512], onescol_f32[:, :],
                            xch[bj][:, tc * 512:(tc + 1) * 512],
                            start=(ic == 0), stop=(ic == NIC - 1)),
                            waits=([(ldc, ld), (cVE, const_ve)] if ic == 0 else [(ldc, ld)]) if tc == 0 else [])
                        E("tensor", lambda t, ic=ic, tc=tc, sq=sq: t.matmul(
                            stats_ps[32:33, tc * 512:(tc + 1) * 512], onescol_f32[:, :],
                            xsq[sq][:, :],
                            start=(ic == 0), stop=(ic == NIC - 1)),
                            waits=[(cSC, scm)],
                            inc=cTE)
                    te_ic.append(cTE.n)
                    sc_ic.append(cSC.n)
                E("vector", lambda v: v.tensor_single_scalar(
                    es_sb[0:1, :], stats_ps[0:1, :], 1.0 / D, ALU.mult),
                    waits=[(cTE, cTE.n), (cVE, const_ve)], inc=cVE)
                E("vector", lambda v: v.tensor_single_scalar(
                    es_sb[32:33, :], stats_ps[32:33, :], 1.0 / D, ALU.mult), inc=cVE)
                sch.flush(nc)
            p0a_te, p0a_sc, p0a_ve = cTE.n, cSC.n, cVE.n

            with nc.psum_tensor("bc_ps", [128, 2048], F32) as bc_ps:
                vtc = []
                for tc in range(8):
                    par = tc % 2
                    mslot = bc_ps[:, (2 * par) * 512:(2 * par + 1) * 512]
                    eslot = bc_ps[:, (2 * par + 1) * 512:(2 * par + 2) * 512]
                    w0 = [(cVE, p0a_ve)]
                    if tc >= 2:
                        w0.append((cVE, vtc[tc - 2]))
                    sl = slice(tc * 512, (tc + 1) * 512)
                    E("tensor", lambda t, m=mslot, sl=sl: t.matmul(
                        m, ones_f32[0:1, :], es_sb[0:1, sl], start=True, stop=True), waits=w0)
                    E("tensor", lambda t, e=eslot, sl=sl: t.matmul(
                        e, ones_f32[32:33, :], es_sb[32:33, sl], start=True, stop=True), inc=cTE)
                    tm = cTE.n
                    E("vector", lambda v, m=mslot, sl=sl: v.tensor_copy(mean_bc[:, sl], m),
                        waits=[(cTE, tm)], inc=cVE)
                    E("vector", lambda v, m=mslot: v.tensor_copy(v1, m), inc=cVE)
                    E("vector", lambda v, e=eslot: v.tensor_tensor(v2, v1, v1, ALU.mult), inc=cVE)
                    E("vector", lambda v, e=eslot: v.tensor_tensor(v2, e, v2, ALU.subtract), inc=cVE)
                    E("vector", lambda v: v.tensor_single_scalar(v2, v2, EPS, ALU.add), inc=cVE)
                    vsq = cVE.n
                    E("scalar", lambda s: s.activation(out=v1, in_=v2, func=AF.Sqrt),
                        waits=[(cVE, vsq), (cSC, p0a_sc)], inc=cSC)
                    E("vector", lambda v: v.reciprocal(v2, v1),
                        waits=[(cSC, cSC.n)], inc=cVE)
                    E("vector", lambda v, sl=sl: v.tensor_copy(rstd_bc[:, sl], v2), inc=cVE)
                    vtc.append(cVE.n)
                sch.flush(nc)
            p0b_ve = cVE.n

            va = []
            for ic in range(NIC):
                bj = ic % 2
                lw = [(cVE, p0b_ve), (cTE, cTE.n), (cSC, cSC.n)]
                if ic >= 2:
                    lw.append((cVE, va[ic - 2]))
                E("sync", lambda s, ic=ic, bj=bj: s.dma_start(
                    out=xch[bj][:, :], in_=dx[ic * 128:(ic + 1) * 128, :]),
                    waits=lw, inc=cX[bj], inck=16)
                ldc, ld = cX[bj], cX[bj].n
                for quarter in range(4):
                    dst = xn_own if quarter < 2 else xn_prt
                    hs = slice(quarter * (T // 2), (quarter + 1) * (T // 2))
                    ds = slice(ic * T + (quarter % 2) * (T // 2), ic * T + (quarter % 2 + 1) * (T // 2))
                    E("vector", lambda v, bj=bj, hs=hs: v.tensor_tensor(
                        t1[:, :], xch[bj][:, hs], mean_bc[:, hs], ALU.subtract),
                        waits=[(ldc, ld)] if quarter == 0 else [], inc=cVE)
                    E("vector", lambda v, hs=hs, ds=ds, dst=dst: v.tensor_tensor(
                        dst[:, ds], t1[:, :], rstd_bc[:, hs], ALU.mult), inc=cVE)
                va.append(cVE.n)
            xn_done = cVE.n
            if debug:
                for ic in range(NIC):
                    E("gpsimd", lambda g, ic=ic: g.dma_start(
                        out=dbg["xn"][ic * 128:(ic + 1) * 128, 0:T], in_=xn_own[:, ic * T:(ic + 1) * T]),
                        waits=[(cVE, xn_done)], inc=cDBG, inck=16)
                    E("gpsimd", lambda g, ic=ic: g.dma_start(
                        out=dbg["xn"][ic * 128:(ic + 1) * 128, T:L], in_=xn_prt[:, ic * T:(ic + 1) * T]),
                        inc=cDBG, inck=16)
            sch.flush(nc)

        with contextlib.ExitStack() as s1:
            kpA = s1.enter_context(nc.sbuf_tensor("kpA", [65, L], BF16))
            kpB = s1.enter_context(nc.sbuf_tensor("kpB", [65, L], BF16))
            qpA = s1.enter_context(nc.sbuf_tensor("qpA", [65, T], BF16))
            qpB = s1.enter_context(nc.sbuf_tensor("qpB", [65, T], BF16))
            vsb = s1.enter_context(nc.sbuf_tensor("vsb", [128, 32 * 130], BF16))
            pt = s1.enter_context(nc.sbuf_tensor("pt", [128, 4096], BF16))
            wq_sb = s1.enter_context(nc.sbuf_tensor("wq", [128, 1024], BF16))
            wk_sb = s1.enter_context(nc.sbuf_tensor("wk", [128, 1024], BF16))
            wv_sb = s1.enter_context(nc.sbuf_tensor("wv", [128, 1024], BF16))
            qraw = [s1.enter_context(nc.sbuf_tensor(f"qraw{j}", [128, 512], F32)) for j in range(2)]
            qswp = [s1.enter_context(nc.sbuf_tensor(f"qswp{j}", [128, 512], F32)) for j in range(2)]
            rt1 = [s1.enter_context(nc.sbuf_tensor(f"rt1_{j}", [128, 512], F32)) for j in range(2)]
            cs_sb = s1.enter_context(nc.sbuf_tensor("cs_sb", [128, 2048], BF16))
            den_sb = s1.enter_context(nc.sbuf_tensor("den_sb", [65, 1024], F32))
            bc_sb = s1.enter_context(nc.sbuf_tensor("bc_sb", [128, 1024], F32))
            bnd = s1.enter_context(nc.sbuf_tensor("bnd", [33, 8], F32))
            c8col = s1.enter_context(nc.sbuf_tensor("c8col", [128, 2], F32))
            MA = s1.enter_context(nc.sbuf_tensor("MA", [128, 1024], BF16))
            MB = s1.enter_context(nc.sbuf_tensor("MB", [128, 1024], BF16))

            E("vector", lambda v: v.memset(vsb[:, :], 0.0), inc=cVE)
            vsb_r = vsb.rearrange("p (vt c) -> p vt c", c=130)
            E("vector", lambda v: v.memset(vsb_r[:, :, 64:65], 1.0), inc=cVE)
            E("vector", lambda v: v.memset(kpA[64:65, :], 1.0), inc=cVE)
            E("vector", lambda v: v.memset(bc_sb[:, :], 0.0), inc=cVE)
            E("vector", lambda v: v.memset(den_sb[:, :], 1.0), inc=cVE)
            E("vector", lambda v: v.memset(kpB[64:65, :], 1.0), inc=cVE)

            with (
                nc.psum_tensor("pA", [128, 2048], F32) as pA,
                nc.psum_tensor("pB", [128, 2048], F32) as pB,
            ):
                def wload(g, dstT, row0):
                    return g.dma_start(out=dstT[:, :], in_=dwqkv[row0 * 128:(row0 + 1) * 128, :])

                prev_norm = 0
                last_swp = 0
                swp_mark = [0, 0]
                qs_mark = [0, 0]
                rope_tile_idx = [0]

                for p in range(NPAIR):
                    pw = [(cTE, cTE.n)]
                    E("gpsimd", lambda g, p=p: wload(g, wq_sb, p * 3 + 0), waits=pw, inc=cWQ[0], inck=16)
                    E("gpsimd", lambda g, p=p: wload(g, wk_sb, p * 3 + 1), inc=cWQ[0], inck=16)
                    E("gpsimd", lambda g, p=p: wload(g, wv_sb, p * 3 + 2), inc=cWQ[0], inck=16)
                    wt_ready = cWQ[0].n

                    def rope_tiles(wsb, n_tiles, xsrc_fn, dstA, dstB, first_wait):
                        nonlocal last_swp, prev_norm
                        ve_tile = []
                        for j in range(n_tiles):
                            ps = pA[:, (j % 2) * 512:(j % 2 + 1) * 512]
                            gpar = rope_tile_idx[0] % 2
                            gw = [(cVE, ve_tile[j - 2])] if j >= 2 else [(cVE, cVE.n)]
                            E("sync", lambda s, j=j, gpar=gpar: s.dma_start(
                                out=cs_sb[:, gpar * 1024:gpar * 1024 + 512],
                                in_=dcos[:, j * 512:(j + 1) * 512]),
                                waits=gw, inc=cCS[gpar], inck=16)
                            E("sync", lambda s, j=j, gpar=gpar: s.dma_start(
                                out=cs_sb[:, gpar * 1024 + 512:gpar * 1024 + 1024],
                                in_=dsin[:, j * 512:(j + 1) * 512]),
                                inc=cCS[gpar], inck=16)
                            cs_ready = cCS[gpar].n
                            rope_tile_idx[0] += 1
                            w0 = [(cVE, xn_done), (cWQ[0], wt_ready)]
                            if j >= 2:
                                w0.append((cVE, ve_tile[j - 2]))
                            elif first_wait:
                                w0.append((cVE, first_wait))
                            for ic in range(NIC):
                                E("tensor", lambda t, ic=ic, j=j, ps=ps, wsb=wsb: t.matmul(
                                    ps, wsb[:, ic * 128:(ic + 1) * 128], xsrc_fn(ic, j),
                                    start=(ic == 0), stop=(ic == NIC - 1)),
                                    waits=w0 if ic == 0 else [],
                                    inc=cTE if ic == NIC - 1 else None)
                            tm = cTE.n
                            qr, qs, r1 = qraw[gpar], qswp[gpar], rt1[gpar]
                            E("vector", lambda v, ps=ps, qr=qr: v.tensor_copy(qr[:, :], ps),
                                waits=[(cTE, tm), (cSW, swp_mark[gpar]), (cQS, qs_mark[gpar])], inc=cVE)
                            vcp = cVE.n
                            for (a, b2) in ((0, 32), (32, 0), (64, 96), (96, 64)):
                                E("sync", lambda s, a=a, b2=b2, qr=qr, qs=qs: s.dma_start(
                                    out=qs[a:a + 32, :], in_=qr[b2:b2 + 32, :]),
                                    waits=[(cVE, vcp)] if a == 0 else [], inc=cSW, inck=16)
                            swp_mark[gpar] = last_swp = cSW.n
                            E("vector", lambda v, gpar=gpar, r1=r1, qr=qr: v.tensor_tensor(
                                r1[:, :], qr[:, :], cs_sb[:, gpar * 1024:gpar * 1024 + 512], ALU.mult),
                                waits=[(cCS[gpar], cs_ready)], inc=cVE)
                            E("vector", lambda v, gpar=gpar, qs=qs: v.tensor_tensor(
                                qs[:, :], qs[:, :], cs_sb[:, gpar * 1024 + 512:gpar * 1024 + 1024], ALU.mult),
                                waits=[(cSW, last_swp)], inc=cVE)
                            E("vector", lambda v, r1=r1, qs=qs: v.tensor_tensor(
                                r1[:, :], r1[:, :], qs[:, :], ALU.add), inc=cVE)
                            vq = cVE.n
                            sl = slice(j * 512, (j + 1) * 512)
                            E("gpsimd", lambda g, sl=sl, dstA=dstA, r1=r1: g.dma_start(
                                out=dstA[0:64, sl], in_=r1[0:64, :]),
                                waits=[(cVE, vq)], inc=cQS, inck=16)
                            E("gpsimd", lambda g, sl=sl, dstB=dstB, r1=r1: g.dma_start(
                                out=dstB[0:64, sl], in_=r1[64:128, :]),
                                inc=cQS, inck=16)
                            qs_mark[gpar] = cQS.n
                            ve_tile.append(cVE.n)
                        return ve_tile

                    rope_q = rope_tiles(
                        wq_sb, 4,
                        lambda ic, j: xn_own[:, ic * T + j * 512: ic * T + (j + 1) * 512],
                        qpA, qpB, prev_norm)
                    rope_k = rope_tiles(
                        wk_sb, 8,
                        lambda ic, j: (xn_own if j < 4 else xn_prt)[:, ic * T + (j % 4) * 512: ic * T + (j % 4 + 1) * 512],
                        kpA, kpB, rope_q[-1])
                    qk_dmas = cQS.n

                    E("vector", lambda v: v.memset(bnd[:, :], 0.0),
                        waits=[(cTE, cTE.n)], inc=cVE)
                    bnd_ve0 = cVE.n
                    sqm = [0, 0]
                    mmm = [cTE.n, cTE.n]
                    rdm = []
                    ti = 0
                    for srcs, col in (((qpA, qpB), 0), ((kpA, kpB), 1)):
                        n_tiles = 4 if col == 0 else 8
                        for j in range(n_tiles):
                            for hh, srch in enumerate(srcs):
                                par = ti % 2
                                sqb = MA[0:64, par * 512:(par + 1) * 512]
                                E("scalar", lambda s, srch=srch, j=j, sqb=sqb: s.activation(
                                    out=sqb, in_=srch[0:64, j * 512:(j + 1) * 512], func=AF.Square),
                                    waits=[(cQS, qk_dmas), (cTE, mmm[par])], inc=cSC)
                                sqm[par] = cSC.n
                                row = hh * 32
                                pslot = pA[row:row + 1, par * 512:(par + 1) * 512]
                                wmm = [(cSC, sqm[par])]
                                if ti >= 2:
                                    wmm.append((cVE, rdm[ti - 2]))
                                elif ti == 0:
                                    wmm.append((cVE, bnd_ve0))
                                E("tensor", lambda t, row=row, pslot=pslot, sqb=sqb: t.matmul(
                                    pslot, onescol_bf[0:64, :], sqb,
                                    start=True, stop=True),
                                    waits=wmm, inc=cTE)
                                mmm[par] = cTE.n
                                E("vector", lambda v, row=row, pslot=pslot: v.tensor_reduce(
                                    bnd[row:row + 1, 5:6], pslot, AX.X, ALU.max),
                                    waits=[(cTE, mmm[par])], inc=cVE)
                                E("vector", lambda v, col=col, row=row: v.tensor_tensor(
                                    bnd[row:row + 1, col:col + 1], bnd[row:row + 1, col:col + 1],
                                    bnd[row:row + 1, 5:6], ALU.max), inc=cVE)
                                rdm.append(cVE.n)
                                ti += 1
                    E("vector", lambda v: v.tensor_tensor(
                        bnd[:, 2:3], bnd[:, 0:1], bnd[:, 1:2], ALU.mult), inc=cVE)
                    E("scalar", lambda s: s.activation(
                        out=bnd[:, 3:4], in_=bnd[:, 2:3], func=AF.Sqrt),
                        waits=[(cVE, cVE.n)], inc=cSC)
                    E("vector", lambda v: v.tensor_scalar(
                        bnd[:, 4:5], bnd[:, 3:4], -0.1275 / 8.0, None, ALU.mult),
                        waits=[(cSC, cSC.n)], inc=cVE)
                    E("vector", lambda v: v.tensor_scalar(
                        bnd[:, 6:7], bnd[:, 3:4], -0.1275 * 8.0, None, ALU.mult), inc=cVE)
                    bnd_ve = cVE.n
                    E("tensor", lambda t: t.matmul(
                        pA[0:128, 0:1], ones_f32[0:1, :], bnd[0:1, 4:5], start=True, stop=True),
                        waits=[(cVE, bnd_ve)])
                    E("tensor", lambda t: t.matmul(
                        pA[0:128, 512:513], ones_f32[32:33, :], bnd[32:33, 4:5], start=True, stop=True),
                        inc=cTE)
                    E("vector", lambda v: v.tensor_copy(c8col[:, 0:1], pA[0:128, 0:1]),
                        waits=[(cTE, cTE.n)], inc=cVE)
                    E("vector", lambda v: v.tensor_copy(c8col[:, 1:2], pA[0:128, 512:513]), inc=cVE)
                    c8_ve = cVE.n

                    v_marks = []
                    for vt in range(32):
                        ps = pA[:, (vt % 2) * 512: (vt % 2) * 512 + 128]
                        xsb = xn_own if vt < 16 else xn_prt
                        tcol = (vt % 16) * 128
                        w0 = [(cVE, c8_ve), (cWQ[0], wt_ready)]
                        if vt >= 2:
                            w0.append((cVE, v_marks[vt - 2]))
                        for ic in range(NIC):
                            E("tensor", lambda t, ic=ic, ps=ps, xsb=xsb, tcol=tcol: t.matmul(
                                ps, xsb[:, ic * T + tcol: ic * T + tcol + 128],
                                wv_sb[:, ic * 128:(ic + 1) * 128],
                                start=(ic == 0), stop=(ic == NIC - 1)),
                                waits=w0 if ic == 0 else [],
                                inc=cTE if ic == NIC - 1 else None)
                        tm = cTE.n
                        E("vector", lambda v, ps=ps, vt=vt: v.tensor_copy(
                            vsb[:, vt * 130:vt * 130 + 64], ps[:, 0:64]),
                            waits=[(cTE, tm)], inc=cVE)
                        E("vector", lambda v, ps=ps, vt=vt: v.tensor_copy(
                            vsb[:, vt * 130 + 65:vt * 130 + 129], ps[:, 64:128]), inc=cVE)
                        v_marks.append(cVE.n)
                    v_done = cVE.n

                    if debug and p == 0:
                        E("gpsimd", lambda g: g.dma_start(out=dbg["q"][0:64, :], in_=qpA[0:64, :]),
                            waits=[(cVE, v_done)], inc=cDBG, inck=16)
                        E("gpsimd", lambda g: g.dma_start(out=dbg["q"][64:128, :], in_=qpB[0:64, :]), inc=cDBG, inck=16)
                        E("gpsimd", lambda g: g.dma_start(out=dbg["k"][0:64, :], in_=kpA[0:64, :]), inc=cDBG, inck=16)
                        E("gpsimd", lambda g: g.dma_start(out=dbg["k"][64:128, :], in_=kpB[0:64, :]), inc=cDBG, inck=16)
                        E("gpsimd", lambda g: g.dma_start(out=dbg["v"][:, :], in_=vsb[:, :]), inc=cDBG, inck=16)

                    for qc in range(2):
                        qoff = qc * 1024
                        maxm = []
                        for kt in range(32):
                            par = kt % 2
                            for head, (kph, qph, psh, Mh) in enumerate(
                                    ((kpA, qpA, pA, MA), (kpB, qpB, pB, MB))):
                                pslice = psh[:, par * 1024:(par + 1) * 1024]
                                for half in range(2):
                                    w0 = []
                                    if half == 0 and head == 0:
                                        if kt >= 2:
                                            w0 = [(cVE, maxm[kt - 2])]
                                        else:
                                            w0 = [(cVE, prev_norm)] if (qc > 0 or p > 0) else [(cVE, v_done)]
                                        if kt == 0:
                                            w0.append((cVE, v_done))
                                    E("tensor", lambda t, kt=kt, half=half, qoff=qoff,
                                        kph=kph, qph=qph, pslice=pslice: t.matmul(
                                        pslice[:, half * 512:(half + 1) * 512],
                                        kph[0:64, kt * 128:(kt + 1) * 128],
                                        qph[0:64, qoff + half * 512: qoff + (half + 1) * 512],
                                        start=True, stop=True),
                                        waits=w0, inc=cTE if half == 1 else None)
                                tmh = cTE.n
                                if kt == 0:
                                    E("vector", lambda v, Mh=Mh, pslice=pslice: v.tensor_copy(
                                        Mh[:, :], pslice),
                                        waits=[(cTE, tmh), (cSC, cSC.n)], inc=cVE)
                                else:
                                    E("vector", lambda v, Mh=Mh, pslice=pslice: v.tensor_tensor(
                                        Mh[:, :], pslice, Mh[:, :], ALU.max),
                                        waits=[(cTE, tmh)], inc=cVE)
                            maxm.append(cVE.n)
                        for head, (Mh, qph, row) in enumerate(((MA, qpA, 0), (MB, qpB, 32))):
                            E("scalar", lambda s, Mh=Mh, head=head: s.activation(
                                out=Mh[:, :], in_=Mh[:, :], func=AF.Exp,
                                bias=c8col[:, head:head + 1], scale=0.125 / 8.0),
                                waits=[(cVE, maxm[-1]), (cTE, cTE.n), (cSC, cSC.n)], inc=cSC)
                            scm = cSC.n
                            for half in range(2):
                                E("tensor", lambda t, row=row, Mh=Mh, half=half: t.matmul(
                                    pA[row:row + 1, half * 512:(half + 1) * 512],
                                    onescol_bf[:, :], Mh[:, half * 512:(half + 1) * 512],
                                    start=True, stop=True),
                                    waits=[(cSC, scm)] if half == 0 else [],
                                    inc=cTE if half == 1 else None)
                            E("vector", lambda v, row=row: v.tensor_single_scalar(
                                bc_sb[row:row + 1, 0:1024], pA[row:row + 1, 0:1024], 1e-38, ALU.add),
                                waits=[(cTE, cTE.n)], inc=cVE)
                            E("scalar", lambda s, row=row: s.activation(
                                out=bc_sb[row:row + 1, 0:1024], in_=bc_sb[row:row + 1, 0:1024],
                                func=AF.Ln),
                                waits=[(cVE, cVE.n)], inc=cSC)
                        E("vector", lambda v: v.tensor_scalar(
                            bc_sb[0:33, 0:1024], bc_sb[0:33, 0:1024], -64.0, bnd[0:33, 6:7], ALU.mult, ALU.add),
                            waits=[(cSC, cSC.n)], inc=cVE)
                        lnm = cVE.n
                        E("gpsimd", lambda g, qoff=qoff: g.dma_start(
                            out=qpA[64:65, qoff:qoff + 1024], in_=bc_sb[0:1, 0:1024]),
                            waits=[(cVE, lnm)], inc=cU, inck=16)
                        E("gpsimd", lambda g, qoff=qoff: g.dma_start(
                            out=qpB[64:65, qoff:qoff + 1024], in_=bc_sb[32:33, 0:1024]),
                            inc=cU, inck=16)
                        u_dma = cU.n

                        expA, expB = [], []
                        for kt in range(32):
                            par = kt % 2
                            for half in range(2):
                                w0 = []
                                if half == 0:
                                    w0 = [(cSC, expA[kt - 1])] if kt >= 1 else [(cU, u_dma), (cSC, cSC.n)]
                                E("tensor", lambda t, kt=kt, half=half, qoff=qoff: t.matmul(
                                    pA[:, half * 512:(half + 1) * 512],
                                    kpA[0:65, kt * 128:(kt + 1) * 128],
                                    qpA[0:65, qoff + half * 512: qoff + (half + 1) * 512],
                                    start=True, stop=True),
                                    waits=w0, inc=cTE if half == 1 else None)
                            qkA = cTE.n
                            for half in range(2):
                                w0 = []
                                if half == 0 and kt >= 1:
                                    w0 = [(cSC, expB[kt - 1])]
                                E("tensor", lambda t, kt=kt, half=half, qoff=qoff: t.matmul(
                                    pB[:, half * 512:(half + 1) * 512],
                                    kpB[0:65, kt * 128:(kt + 1) * 128],
                                    qpB[0:65, qoff + half * 512: qoff + (half + 1) * 512],
                                    start=True, stop=True),
                                    waits=w0, inc=cTE if half == 1 else None)
                            qkB = cTE.n
                            if kt > 0:
                                ktm = kt - 1
                                parm = ktm % 2
                                for half in range(2):
                                    E("tensor", lambda t, ktm=ktm, parm=parm, half=half: t.matmul(
                                        pA[0:65, 1024 + half * 512:1024 + (half + 1) * 512],
                                        vsb[:, ktm * 130:ktm * 130 + 65],
                                        pt[:, parm * 1024 + half * 512: parm * 1024 + (half + 1) * 512],
                                        start=(ktm == 0), stop=False))
                                    E("tensor", lambda t, ktm=ktm, parm=parm, half=half: t.matmul(
                                        pB[64:128, 1024 + half * 512:1024 + (half + 1) * 512],
                                        vsb[:, ktm * 130 + 65:ktm * 130 + 129],
                                        pt[:, 2048 + parm * 1024 + half * 512: 2048 + parm * 1024 + (half + 1) * 512],
                                        start=(ktm == 0), stop=False, tile_position=(0, 64)))
                                    E("tensor", lambda t, ktm=ktm, parm=parm, half=half: t.matmul(
                                        pB[0:1, 1024 + half * 512:1024 + (half + 1) * 512],
                                        onescol_bf[:, :],
                                        pt[:, 2048 + parm * 1024 + half * 512: 2048 + parm * 1024 + (half + 1) * 512],
                                        start=(ktm == 0), stop=False))
                            E("scalar", lambda s, par=par: s.activation(
                                out=pt[:, par * 1024:(par + 1) * 1024], in_=pA[:, 0:1024],
                                func=AF.Exp, scale=0.125),
                                waits=[(cTE, qkA)], inc=cSC)
                            expA.append(cSC.n)
                            E("scalar", lambda s, par=par: s.activation(
                                out=pt[:, 2048 + par * 1024:2048 + (par + 1) * 1024], in_=pB[:, 0:1024],
                                func=AF.Exp, scale=0.125),
                                waits=[(cTE, qkB)], inc=cSC)
                            expB.append(cSC.n)
                        ktm, parm = 31, 1
                        for half in range(2):
                            E("tensor", lambda t, half=half, parm=parm, ktm=ktm: t.matmul(
                                pA[0:65, 1024 + half * 512:1024 + (half + 1) * 512],
                                vsb[:, ktm * 130:ktm * 130 + 65],
                                pt[:, parm * 1024 + half * 512: parm * 1024 + (half + 1) * 512],
                                start=False, stop=True),
                                waits=[(cSC, expA[31])] if half == 0 else [])
                            E("tensor", lambda t, half=half, parm=parm, ktm=ktm: t.matmul(
                                pB[64:128, 1024 + half * 512:1024 + (half + 1) * 512],
                                vsb[:, ktm * 130 + 65:ktm * 130 + 129],
                                pt[:, 2048 + parm * 1024 + half * 512: 2048 + parm * 1024 + (half + 1) * 512],
                                start=False, stop=True, tile_position=(0, 64)),
                                waits=[(cSC, expB[31])] if half == 0 else [])
                            E("tensor", lambda t, half=half, parm=parm: t.matmul(
                                pB[0:1, 1024 + half * 512:1024 + (half + 1) * 512],
                                onescol_bf[:, :],
                                pt[:, 2048 + parm * 1024 + half * 512: 2048 + parm * 1024 + (half + 1) * 512],
                                start=False, stop=True),
                                inc=cTE if half == 1 else None)
                        pv_te = cTE.n
                        E("vector", lambda v: v.tensor_copy(den_sb[64:65, :], pA[64:65, 1024:2048]),
                            waits=[(cTE, pv_te)], inc=cVE)
                        E("vector", lambda v: v.tensor_copy(den_sb[0:1, :], pB[0:1, 1024:2048]), inc=cVE)
                        E("vector", lambda v: v.reciprocal(den_sb[0:65, :], den_sb[0:65, :]), inc=cVE)
                        den_ve = cVE.n
                        for half in range(2):
                            E("tensor", lambda t, half=half: t.matmul(
                                pA[:, half * 512:(half + 1) * 512],
                                ones_f32[64:65, :], den_sb[64:65, half * 512:(half + 1) * 512],
                                start=True, stop=True),
                                waits=[(cVE, den_ve), (cSC, expA[31])] if half == 0 else [],
                                inc=cTE if half == 1 else None)
                        E("vector", lambda v: v.tensor_copy(bc_sb[:, :], pA[:, 0:1024]),
                            waits=[(cTE, cTE.n)], inc=cVE)
                        E("vector", lambda v, p=p, qoff=qoff: v.tensor_tensor(
                            ao_sb[0:64, p * T + qoff: p * T + qoff + 1024],
                            pA[0:64, 1024:2048], bc_sb[0:64, :], ALU.mult),
                            waits=[(cVE, cVE.n)], inc=cVE)
                        for half in range(2):
                            E("tensor", lambda t, half=half: t.matmul(
                                pB[:, half * 512:(half + 1) * 512],
                                ones_f32[0:1, :], den_sb[0:1, half * 512:(half + 1) * 512],
                                start=True, stop=True),
                                waits=[(cVE, cVE.n), (cSC, expB[31])] if half == 0 else [],
                                inc=cTE if half == 1 else None)
                        E("vector", lambda v: v.tensor_copy(bc_sb[:, :], pB[:, 0:1024]),
                            waits=[(cTE, cTE.n), (cVE, cVE.n)], inc=cVE)
                        E("vector", lambda v, p=p, qoff=qoff: v.tensor_tensor(
                            ao_sb[64:128, p * T + qoff: p * T + qoff + 1024],
                            pB[64:128, 1024:2048], bc_sb[64:128, :], ALU.mult),
                            waits=[(cVE, cVE.n)], inc=cVE)
                        prev_norm = cVE.n

                ao_done = cVE.n
                if debug:
                    for ic in range(NIC):
                        E("gpsimd", lambda g, ic=ic: g.dma_start(
                            out=dbg["ao"][ic * 128:(ic + 1) * 128, :], in_=ao_sb[:, ic * T:(ic + 1) * T]),
                            waits=[(cVE, ao_done)], inc=cDBG, inck=16)

                sch.flush(nc)
                ap_done_box[0] = ao_done

    with contextlib.ExitStack() as s2:
        bufA = s2.enter_context(nc.sbuf_tensor("bufA", [128, NHC * TCT], BF16))
        bufB = s2.enter_context(nc.sbuf_tensor("bufB", [128, NHC * TCT], BF16))
        wh = [s2.enter_context(nc.sbuf_tensor(f"wh{j}", [128, 4096], BF16)) for j in range(3)]
        xres = [s2.enter_context(nc.sbuf_tensor(f"xres{j}", [128, 512], F32)) for j in range(2)]
        ytmp = s2.enter_context(nc.sbuf_tensor("ytmp", [128, 512], F32))
        ysb = [s2.enter_context(nc.sbuf_tensor(f"ysb{j}", [128, 512], F32)) for j in range(2)]

        with nc.psum_tensor("pF", [128, 2048], F32) as pF, \
             nc.psum_tensor("pF2", [128, 2048], F32) as pF2:
            prev_tail = [(cVE, ap_done_box[0])] * 4
            prev_wh_marks = [cTE.n, cTE.n, cTE.n]
            y_ve_marks = []
            st_marks = []

            opdone = [0]

            def gen_op():
                ao_done = ap_done_box[0]
                E("gpsimd", lambda g: g.dma_start(out=wo_sb[0][:, :], in_=dwout[0:128, :]),
                    waits=[(cTE, cTE.n)], inc=cWO[0], inck=16)
                op_marks = []
                for oc in range(NIC):
                    ob = oc % 2
                    wt_r = cWO[ob].n
                    if oc < NIC - 1:
                        E("gpsimd", lambda g, oc=oc: g.dma_start(
                            out=wo_sb[(oc + 1) % 2][:, :], in_=dwout[(oc + 1) * 128:(oc + 2) * 128, :]),
                            waits=[(cTE, op_marks[oc * 4 - 1][1]) if oc >= 1 else (cTE, cTE.n)],
                            inc=cWO[(oc + 1) % 2], inck=16)
                    for qt in range(4):
                        idx = oc * 4 + qt
                        ps = pF2[:, (idx % 4) * 512:(idx % 4 + 1) * 512]
                        w0 = [(cVE, ao_done), (cWO[ob], wt_r)]
                        if idx >= 4:
                            w0.append((cVE, op_marks[idx - 4][0]))
                        for dc in range(NIC):
                            E("tensor", lambda t, dc=dc, qt=qt, ps=ps, ob=ob: t.matmul(
                                ps, wo_sb[ob][:, dc * 128:(dc + 1) * 128],
                                ao_sb[:, dc * T + qt * 512:dc * T + (qt + 1) * 512],
                                start=(dc == 0), stop=(dc == NIC - 1)),
                                waits=w0 if dc == 0 else [],
                                inc=cTE if dc == NIC - 1 else None)
                        tm = cTE.n
                        E("vector", lambda v, oc=oc, qt=qt, ps=ps: v.tensor_copy(
                            attn_proj[:, oc * T + qt * 512:oc * T + (qt + 1) * 512], ps),
                            waits=[(cTE, tm)], inc=cVE)
                        op_marks.append((cVE.n, tm))
                        yield
                opdone[0] = cVE.n

            def gen_stage(dw, n_oc, n_ic, ncols, rhs_fn, evac_fn, first_extra=None,
                          dwc=None, first_pass=True):
                nonlocal prev_tail, prev_wh_marks
                te_marks = []
                ev = []
                for oc in range(n_oc):
                    wb = oc % 3
                    lw = [(cTE, te_marks[oc - 3])] if oc >= 3 else [(cTE, prev_wh_marks[oc])]
                    if dwc is not None and first_pass:
                        lw.append((cWB[wb % 2], cWB[wb % 2].n))
                    src_t = dw if (dwc is None or first_pass) else dwc
                    E("gpsimd", lambda g, oc=oc, wb=wb, src_t=src_t, ncols=ncols: g.dma_start(
                        out=wh[wb][:, 0:ncols], in_=src_t[oc * 128:(oc + 1) * 128, :]),
                        waits=lw, inc=cWH[wb], inck=16)
                    wt_r = cWH[wb].n
                    if dwc is not None and first_pass:
                        E("sync", lambda s, oc=oc, wb=wb, dwc=dwc, ncols=ncols: s.dma_start(
                            out=dwc[oc * 128:(oc + 1) * 128, :], in_=wh[wb][:, 0:ncols]),
                            waits=[(cWH[wb], wt_r)], inc=cWB[wb % 2], inck=16)
                    ps = pF[:, 512 * (oc % 4):512 * (oc % 4 + 1)]
                    w0 = [(cWH[wb], wt_r)]
                    if first_extra is not None and oc == 0:
                        w0.append(first_extra)
                    if oc >= 4:
                        w0.append(ev[oc - 4])
                    else:
                        w0.append(prev_tail[oc])
                    for ic in range(n_ic):
                        E("tensor", lambda t, ic=ic, ps=ps, wb=wb: t.matmul(
                            ps, wh[wb][:, ic * 128:(ic + 1) * 128], rhs_fn(ic),
                            start=(ic == 0), stop=(ic == n_ic - 1)),
                            waits=w0 if ic == 0 else [],
                            inc=cTE if ic == n_ic - 1 else None)
                    te_marks.append(cTE.n)
                    ev.append(evac_fn(oc, ps, cTE.n))
                    yield
                prev_tail = ev[-4:]
                prev_wh_marks = [te_marks[-3], te_marks[-2], te_marks[-1]]
                evout[0] = ev

            def stage(*a, **k):
                evout[0] = None
                for _ in gen_stage(*a, **k):
                    pass
                return evout[0]

            evout = [None]

            for tct in range(NTCT):
                tof = tct * TCT

                def evac_w2(oc, ps, tm):
                    E("vector", lambda v, oc=oc, ps=ps: v.tensor_scalar(
                        bufB[:, oc * TCT:(oc + 1) * TCT], ps, s2_sb[:, oc:oc + 1], None, ALU.mult),
                        waits=[(cTE, tm)], inc=cVE)
                    return (cVE, cVE.n)

                def evac_w1(oc, ps, tm):
                    E("scalar", lambda s, oc=oc, ps=ps: s.activation(
                        out=bufA[:, oc * TCT:(oc + 1) * TCT], in_=ps,
                        func=AF.Silu, scale=s1_sb[:, oc:oc + 1]),
                        waits=[(cTE, tm)], inc=cSC)
                    scm = cSC.n
                    E("vector", lambda v, oc=oc: v.tensor_tensor(
                        bufB[:, oc * TCT:(oc + 1) * TCT],
                        bufA[:, oc * TCT:(oc + 1) * TCT],
                        bufB[:, oc * TCT:(oc + 1) * TCT], ALU.mult),
                        waits=[(cSC, scm), (cVE, w2ev[oc][1])], inc=cVE)
                    return (cSC, scm)

                def evac_h1(oc, ps, tm):
                    E("scalar", lambda s, oc=oc, ps=ps: s.activation(
                        out=bufA[:, oc * TCT:(oc + 1) * TCT], in_=ps,
                        func=AF.Gelu, scale=h1s_sb[:, oc:oc + 1]),
                        waits=[(cTE, tm)], inc=cSC)
                    return (cSC, cSC.n)

                def evac_h2(oc, ps, tm):
                    E("scalar", lambda s, oc=oc, ps=ps: s.activation(
                        out=bufB[:, oc * TCT:(oc + 1) * TCT], in_=ps,
                        func=AF.Gelu, scale=h2s_sb[:, oc:oc + 1]),
                        waits=[(cTE, tm)], inc=cSC)
                    return (cSC, cSC.n)

                def evac_w3(oc, ps, tm):
                    jb = oc % 2
                    nst = len(st_marks)
                    lw = [(cVE, y_ve_marks[nst - 2])] if nst >= 2 else []
                    E("sync", lambda s, oc=oc, jb=jb, tof=tof: s.dma_start(
                        out=xres[jb][:, :], in_=dx[oc * 128:(oc + 1) * 128, tof:tof + TCT]),
                        waits=lw, inc=cXR[jb], inck=16)
                    ld = cXR[jb].n
                    E("vector", lambda v, oc=oc, ps=ps, tof=tof: v.scalar_tensor_tensor(
                        ytmp[:, :], ps, w3s_sb[:, oc:oc + 1],
                        attn_proj[:, oc * T + tof: oc * T + tof + TCT],
                        ALU.mult, ALU.add),
                        waits=[(cTE, tm), (cVE, opdone[0])], inc=cVE)
                    sw = [(cXR[jb], ld)]
                    if nst >= 2:
                        sw.append((cST[jb], st_marks[nst - 2]))
                    E("vector", lambda v, jb=jb: v.tensor_tensor(
                        ysb[jb][:, :], ytmp[:, :], xres[jb][:, :], ALU.add),
                        waits=sw, inc=cVE)
                    yv = cVE.n
                    E("sync", lambda s, oc=oc, jb=jb, tof=tof: s.dma_start(
                        out=dy[oc * 128:(oc + 1) * 128, tof:tof + TCT], in_=ysb[jb][:, :]),
                        waits=[(cVE, yv)], inc=cST[jb], inck=16)
                    y_ve_marks.append(yv)
                    st_marks.append(cST[jb].n)
                    return (cVE, cVE.n)

                if tct == 0:
                    evout[0] = None
                    g1 = gen_stage(dw2, NHC, NIC, 1024,
                                   lambda ic: xn_own[:, ic * T + tof: ic * T + tof + TCT],
                                   evac_w2, first_extra=(cTAB, scales_wt),
                                   dwc=dw2c, first_pass=True)
                    g2 = gen_op()
                    live = [g1, g2]
                    while live:
                        for g in live[:]:
                            try:
                                next(g)
                            except StopIteration:
                                live.remove(g)
                    ew2 = evout[0]
                else:
                    ew2 = stage(dw2, NHC, NIC, 1024,
                                lambda ic: xn_own[:, ic * T + tof: ic * T + tof + TCT],
                                evac_w2, first_extra=(cTAB, scales_wt),
                                dwc=dw2c, first_pass=False)
                w2ev = ew2
                ew1 = stage(dw1, NHC, NIC, 1024,
                            lambda ic: xn_own[:, ic * T + tof: ic * T + tof + TCT],
                            evac_w1, dwc=dw1c, first_pass=(tct == 0))
                h_done = cVE.n
                eh1 = stage(dh1, NHC, NHC, 4096,
                            lambda ic: bufB[:, ic * TCT:(ic + 1) * TCT],
                            evac_h1, first_extra=(cVE, h_done),
                            dwc=dh1c, first_pass=(tct == 0))
                eh2 = stage(dh2, NHC, NHC, 4096,
                            lambda ic: bufA[:, ic * TCT:(ic + 1) * TCT],
                            evac_h2, first_extra=(cSC, eh1[-1][1]),
                            dwc=dh2c, first_pass=(tct == 0))
                stage(dw3, NIC, NHC, 4096,
                      lambda ic: bufB[:, ic * TCT:(ic + 1) * TCT],
                      evac_w3, first_extra=(cSC, eh2[-1][1]),
                      dwc=dw3c, first_pass=(tct == 0))

            E("sync", lambda s: None, waits=[(cST[0], cST[0].n), (cST[1], cST[1].n)])
            sch.flush(nc)

    stack.close()
    return nc, dbg



def _strip_img(wT, n_ic):
    return wT.reshape(n_ic, 128, 128).transpose(1, 0, 2).reshape(128, n_ic * 128)


def _weight_img(wT, n_ic, n_oc):
    return np.ascontiguousarray(np.concatenate(
        [_strip_img(wT[:, oc * 128:(oc + 1) * 128], n_ic) for oc in range(n_oc)], axis=0))


def _rope_tables(h):
    pos = np.concatenate([np.arange(h * T, (h + 1) * T),
                          np.arange((1 - h) * T, (2 - h) * T)]).astype(np.float64)
    inv = ROPE_BASE ** (-np.arange(32, dtype=np.float64) / 32.0)
    ang = pos[:, None] * inv[None, :]
    c32 = np.cos(ang).astype(np.float32)
    s32 = np.sin(ang).astype(np.float32)
    cos_rows = np.zeros((128, L), np.float32)
    sin_rows = np.zeros((128, L), np.float32)
    for d in range(128):
        dd = d % 64
        j = dd % 32
        cos_rows[d] = c32[:, j]
        sin_rows[d] = (-1.0 if dd < 32 else 1.0) * s32[:, j]
    return (np.ascontiguousarray(cos_rows.astype(ml_dtypes.bfloat16)),
            np.ascontiguousarray(sin_rows.astype(ml_dtypes.bfloat16)))


def make_in_maps(x, wqkv, wout, w1w, w1s, w2w, w2s, h1w, h1s, h2w, h2s, w3w, w3s):
    x = np.asarray(x, np.float32)
    wqkvT = np.ascontiguousarray(np.asarray(wqkv, np.float32).T)
    woutT = np.ascontiguousarray(np.asarray(wout, np.float32).T)
    w1T = np.ascontiguousarray(np.asarray(w1w, np.float32).T)
    w2T = np.ascontiguousarray(np.asarray(w2w, np.float32).T)
    h1T = np.ascontiguousarray(np.asarray(h1w, np.float32).T)
    h2T = np.ascontiguousarray(np.asarray(h2w, np.float32).T)
    w3T = np.ascontiguousarray(np.asarray(w3w, np.float32).T)

    blocks = []
    for p in range(NPAIR):
        for r in range(3):
            blocks.append(_strip_img(wqkvT[:, r * D + p * 128: r * D + (p + 1) * 128], NIC))
    wqkv_img = np.ascontiguousarray(np.concatenate(blocks, axis=0))
    imgs = {
        "wqkv": wqkv_img,
        "wout": _weight_img(woutT, NIC, NIC),
        "w1": _weight_img(w1T, NIC, NHC),
        "w2": _weight_img(w2T, NIC, NHC),
        "h1": _weight_img(h1T, NHC, NHC),
        "h2": _weight_img(h2T, NHC, NHC),
        "w3": _weight_img(w3T, NHC, NIC),
        "s1": np.ascontiguousarray(np.asarray(w1s, np.float32).reshape(32, 128).T),
        "s2": np.ascontiguousarray(np.asarray(w2s, np.float32).reshape(32, 128).T),
        "h1s": np.ascontiguousarray(np.asarray(h1s, np.float32).reshape(32, 128).T),
        "h2s": np.ascontiguousarray(np.asarray(h2s, np.float32).reshape(32, 128).T),
        "w3s": np.ascontiguousarray(np.asarray(w3s, np.float32).reshape(8, 128).T),
    }
    tabs = [_rope_tables(0), _rope_tables(1)]
    in_maps = []
    for c in range(8):
        b, h = c // 2, c % 2
        xr = np.ascontiguousarray(
            np.concatenate([x[b, h * T:(h + 1) * T, :], x[b, (1 - h) * T:(2 - h) * T, :]], axis=0).T)
        m = {"x": xr, "costab": tabs[h][0], "sintab": tabs[h][1]}
        m.update(imgs)
        in_maps.append(m)
    return in_maps


def kernel(L0_attn_x, L0_attn_w_qkv, L0_attn_w_out,
           L0_ffn_w1_w, L0_ffn_w1_s, L0_ffn_w2_w, L0_ffn_w2_s,
           L0_ffn_h1_w, L0_ffn_h1_s, L0_ffn_h2_w, L0_ffn_h2_s,
           L0_ffn_w3_w, L0_ffn_w3_s):
    in_maps = make_in_maps(L0_attn_x, L0_attn_w_qkv, L0_attn_w_out,
                           L0_ffn_w1_w, L0_ffn_w1_s, L0_ffn_w2_w, L0_ffn_w2_s,
                           L0_ffn_h1_w, L0_ffn_h1_s, L0_ffn_h2_w, L0_ffn_h2_s,
                           L0_ffn_w3_w, L0_ffn_w3_s)
    nc, _ = build_nc(debug=False)
    res = run_bass_kernel_spmd(nc, in_maps, core_ids=list(range(8)))
    out = np.empty((B, L, D), np.float32)
    for c in range(8):
        b, h = c // 2, c % 2
        out[b, h * T:(h + 1) * T, :] = np.asarray(res.results[c]["y"]).T
    return out


# version 10
# speedup vs baseline: 1.0233x; 1.0077x over previous
import contextlib

import numpy as np
import ml_dtypes

import concourse.bass as bass
import concourse.mybir as mybir
from concourse.bass_utils import run_bass_kernel_spmd

F32 = mybir.dt.float32
BF16 = mybir.dt.bfloat16
AF = mybir.ActivationFunctionType
ALU = mybir.AluOpType
AX = mybir.AxisListType

B, L, D = 4, 4096, 1024
T = 2048
H, DH = 16, 64
HID = 4096
NIC = D // 128
NHC = HID // 128
NPAIR = H // 2
TCT = 512
NTCT = T // TCT
ROPE_BASE = 10000.0
EPS = 1e-5


class Ctr:

    def __init__(self, stack, nc, name):
        self.sem = stack.enter_context(nc.semaphore(name))
        self.n = 0


class Sched:

    ENGINES = ("tensor", "vector", "scalar", "sync", "gpsimd")

    def __init__(self):
        self.ops = {e: [] for e in self.ENGINES}

    def emit(self, engine, fn, waits=(), inc=None, inck=1):
        if inc is not None:
            inc.n += inck
            inc = (inc, inck)
        self.ops[engine].append((fn, tuple(waits), inc))

    def flush(self, nc):
        with nc.Block() as block:
            def mk(eng_name):
                def _(eng):
                    for fn, waits, inc in self.ops[eng_name]:
                        for ctr, val in waits:
                            if val > 0:
                                eng.wait_ge(ctr.sem, val)
                        inst = fn(eng)
                        if inc is not None:
                            ctr, k = inc
                            inst.then_inc(ctr.sem, k)
                return _
            for name in self.ENGINES:
                if self.ops[name]:
                    getattr(block, name)(mk(name))
        self.ops = {e: [] for e in self.ENGINES}


def build_nc(debug=False):
    stack = contextlib.ExitStack()
    nc = bass.Bass("TRN2", target_bir_lowering=False, detect_race_conditions=False)

    dx = nc.dram_tensor("x", [D, L], F32, kind="ExternalInput")
    dwqkv = nc.dram_tensor("wqkv", [NPAIR * 3 * 128, NIC * 128], F32, kind="ExternalInput")
    dwout = nc.dram_tensor("wout", [NIC * 128, NIC * 128], F32, kind="ExternalInput")
    dw1 = nc.dram_tensor("w1", [NHC * 128, NIC * 128], F32, kind="ExternalInput")
    dw2 = nc.dram_tensor("w2", [NHC * 128, NIC * 128], F32, kind="ExternalInput")
    dh1 = nc.dram_tensor("h1", [NHC * 128, NHC * 128], F32, kind="ExternalInput")
    dh2 = nc.dram_tensor("h2", [NHC * 128, NHC * 128], F32, kind="ExternalInput")
    dw3 = nc.dram_tensor("w3", [NIC * 128, NHC * 128], F32, kind="ExternalInput")
    ds1 = nc.dram_tensor("s1", [128, 32], F32, kind="ExternalInput")
    ds2 = nc.dram_tensor("s2", [128, 32], F32, kind="ExternalInput")
    dh1s = nc.dram_tensor("h1s", [128, 32], F32, kind="ExternalInput")
    dh2s = nc.dram_tensor("h2s", [128, 32], F32, kind="ExternalInput")
    dw3s = nc.dram_tensor("w3s", [128, 8], F32, kind="ExternalInput")
    dao = nc.dram_tensor("dao", [NPAIR * 128, T], BF16, kind="Internal")
    dw1c = nc.dram_tensor("w1c", [NHC * 128, NIC * 128], BF16, kind="Internal")
    dw2c = nc.dram_tensor("w2c", [NHC * 128, NIC * 128], BF16, kind="Internal")
    dh1c = nc.dram_tensor("h1c", [NHC * 128, NHC * 128], BF16, kind="Internal")
    dh2c = nc.dram_tensor("h2c", [NHC * 128, NHC * 128], BF16, kind="Internal")
    dw3c = nc.dram_tensor("w3c", [NIC * 128, NHC * 128], BF16, kind="Internal")
    dcos = nc.dram_tensor("costab", [128, L], BF16, kind="ExternalInput")
    dsin = nc.dram_tensor("sintab", [128, L], BF16, kind="ExternalInput")
    dy = nc.dram_tensor("y", [D, T], F32, kind="ExternalOutput")
    dbg = {}
    if debug:
        dbg["xn"] = nc.dram_tensor("dbg_xn", [D, L], F32, kind="ExternalOutput")
        dbg["q"] = nc.dram_tensor("dbg_q", [128, T], F32, kind="ExternalOutput")
        dbg["k"] = nc.dram_tensor("dbg_k", [128, L], F32, kind="ExternalOutput")
        dbg["v"] = nc.dram_tensor("dbg_v", [128, 32 * 130], F32, kind="ExternalOutput")
        dbg["ao"] = nc.dram_tensor("dbg_ao", [D, T], F32, kind="ExternalOutput")
        dbg["ap"] = nc.dram_tensor("dbg_ap", [D, T], F32, kind="ExternalOutput")

    cX = [Ctr(stack, nc, f"cX{j}") for j in range(2)]
    cSW = Ctr(stack, nc, "cSW")
    cTAB = Ctr(stack, nc, "cTAB")
    cWQ = [Ctr(stack, nc, f"cWQ{j}") for j in range(2)]
    cWO = [Ctr(stack, nc, f"cWO{j}") for j in range(2)]
    cWH = [Ctr(stack, nc, f"cWH{j}") for j in range(3)]
    cXR = [Ctr(stack, nc, f"cXR{j}") for j in range(2)]
    cWB = [Ctr(stack, nc, f"cWB{j}") for j in range(2)]
    cST = [Ctr(stack, nc, f"cST{j}") for j in range(2)]
    cQS = Ctr(stack, nc, "cQS")
    cU = Ctr(stack, nc, "cU")
    cAO = Ctr(stack, nc, "cAO")
    cCS = [Ctr(stack, nc, f"cCS{j}") for j in range(2)]
    cDBG = Ctr(stack, nc, "cDBG")
    cTE = Ctr(stack, nc, "cTE")
    cSC = Ctr(stack, nc, "cSC")
    cVE = Ctr(stack, nc, "cVE")

    sch = Sched()
    E = sch.emit

    xn_own = stack.enter_context(nc.sbuf_tensor("xn_own", [128, NIC * T], BF16))

    attn_proj = stack.enter_context(nc.sbuf_tensor("attn_proj", [128, NIC * T], BF16))
    s1_sb = stack.enter_context(nc.sbuf_tensor("s1_sb", [128, 32], F32))
    s2_sb = stack.enter_context(nc.sbuf_tensor("s2_sb", [128, 32], F32))
    h1s_sb = stack.enter_context(nc.sbuf_tensor("h1s_sb", [128, 32], F32))
    h2s_sb = stack.enter_context(nc.sbuf_tensor("h2s_sb", [128, 32], F32))
    w3s_sb = stack.enter_context(nc.sbuf_tensor("w3s_sb", [128, 8], F32))
    ones_f32 = stack.enter_context(nc.sbuf_tensor("ones_f32", [65, 128], F32))
    onescol_f32 = stack.enter_context(nc.sbuf_tensor("onescol_f32", [128, 1], F32))
    onescol_bf = stack.enter_context(nc.sbuf_tensor("onescol_bf", [128, 1], BF16))

    E("vector", lambda v: v.memset(ones_f32[:, :], 1.0), inc=cVE)
    E("vector", lambda v: v.memset(onescol_f32[:, :], 1.0), inc=cVE)
    E("vector", lambda v: v.memset(onescol_bf[:, :], 1.0), inc=cVE)
    const_ve = cVE.n
    for dsrc, dst in ((ds1, s1_sb), (ds2, s2_sb), (dh1s, h1s_sb), (dh2s, h2s_sb), (dw3s, w3s_sb)):
        E("gpsimd", lambda g, s=dsrc, d=dst: g.dma_start(out=d[:, :], in_=s[:, :]), inc=cTAB, inck=16)
    scales_wt = cTAB.n

    ap_done_box = [0]

    with contextlib.ExitStack() as sA:
        xn_prt = sA.enter_context(nc.sbuf_tensor("xn_prt", [128, NIC * T], BF16))

        with contextlib.ExitStack() as s0:
            xch = [s0.enter_context(nc.sbuf_tensor(f"xch{j}", [128, L], F32)) for j in range(2)]
            xsq = [s0.enter_context(nc.sbuf_tensor(f"xsq{j}", [128, 512], F32)) for j in range(2)]
            es_sb = s0.enter_context(nc.sbuf_tensor("es_sb", [33, L], F32))
            mean_bc = s0.enter_context(nc.sbuf_tensor("mean_bc", [128, L], BF16))
            rstd_bc = s0.enter_context(nc.sbuf_tensor("rstd_bc", [128, L], BF16))
            t1 = s0.enter_context(nc.sbuf_tensor("t1", [128, T // 2], F32))
            v1 = t1[:, 0:512]
            v2 = t1[:, 512:1024]

            with nc.psum_tensor("stats_ps", [33, L], F32) as stats_ps:
                te_ic = []
                sc_ic = []
                for ic in range(NIC):
                    bj = ic % 2
                    lw = []
                    if ic >= 2:
                        lw = [(cTE, te_ic[ic - 2]), (cSC, sc_ic[ic - 2])]
                    E("sync", lambda s, ic=ic, bj=bj: s.dma_start(
                        out=xch[bj][:, :], in_=dx[ic * 128:(ic + 1) * 128, :]),
                        waits=lw, inc=cX[bj], inck=16)
                    ldc, ld = cX[bj], cX[bj].n
                    for tc in range(8):
                        sq = tc % 2
                        sqw = [(ldc, ld)]
                        if ic > 0 or tc >= 2:
                            sqw.append((cTE, cTE.n))
                        E("scalar", lambda s, bj=bj, tc=tc, sq=sq: s.activation(
                            out=xsq[sq][:, :], in_=xch[bj][:, tc * 512:(tc + 1) * 512], func=AF.Square),
                            waits=sqw, inc=cSC)
                        scm = cSC.n
                        E("tensor", lambda t, ic=ic, bj=bj, tc=tc: t.matmul(
                            stats_ps[0:1, tc * 512:(tc + 1) * 512], onescol_f32[:, :],
                            xch[bj][:, tc * 512:(tc + 1) * 512],
                            start=(ic == 0), stop=(ic == NIC - 1)),
                            waits=([(ldc, ld), (cVE, const_ve)] if ic == 0 else [(ldc, ld)]) if tc == 0 else [])
                        E("tensor", lambda t, ic=ic, tc=tc, sq=sq: t.matmul(
                            stats_ps[32:33, tc * 512:(tc + 1) * 512], onescol_f32[:, :],
                            xsq[sq][:, :],
                            start=(ic == 0), stop=(ic == NIC - 1)),
                            waits=[(cSC, scm)],
                            inc=cTE)
                    te_ic.append(cTE.n)
                    sc_ic.append(cSC.n)
                E("vector", lambda v: v.tensor_single_scalar(
                    es_sb[0:1, :], stats_ps[0:1, :], 1.0 / D, ALU.mult),
                    waits=[(cTE, cTE.n), (cVE, const_ve)], inc=cVE)
                E("vector", lambda v: v.tensor_single_scalar(
                    es_sb[32:33, :], stats_ps[32:33, :], 1.0 / D, ALU.mult), inc=cVE)
                sch.flush(nc)
            p0a_te, p0a_sc, p0a_ve = cTE.n, cSC.n, cVE.n

            with nc.psum_tensor("bc_ps", [128, 2048], F32) as bc_ps:
                vtc = []
                for tc in range(8):
                    par = tc % 2
                    mslot = bc_ps[:, (2 * par) * 512:(2 * par + 1) * 512]
                    eslot = bc_ps[:, (2 * par + 1) * 512:(2 * par + 2) * 512]
                    w0 = [(cVE, p0a_ve)]
                    if tc >= 2:
                        w0.append((cVE, vtc[tc - 2]))
                    sl = slice(tc * 512, (tc + 1) * 512)
                    E("tensor", lambda t, m=mslot, sl=sl: t.matmul(
                        m, ones_f32[0:1, :], es_sb[0:1, sl], start=True, stop=True), waits=w0)
                    E("tensor", lambda t, e=eslot, sl=sl: t.matmul(
                        e, ones_f32[32:33, :], es_sb[32:33, sl], start=True, stop=True), inc=cTE)
                    tm = cTE.n
                    E("vector", lambda v, m=mslot, sl=sl: v.tensor_copy(mean_bc[:, sl], m),
                        waits=[(cTE, tm)], inc=cVE)
                    E("vector", lambda v, m=mslot: v.tensor_copy(v1, m), inc=cVE)
                    E("vector", lambda v, e=eslot: v.tensor_tensor(v2, v1, v1, ALU.mult), inc=cVE)
                    E("vector", lambda v, e=eslot: v.tensor_tensor(v2, e, v2, ALU.subtract), inc=cVE)
                    E("vector", lambda v: v.tensor_single_scalar(v2, v2, EPS, ALU.add), inc=cVE)
                    vsq = cVE.n
                    E("scalar", lambda s: s.activation(out=v1, in_=v2, func=AF.Sqrt),
                        waits=[(cVE, vsq), (cSC, p0a_sc)], inc=cSC)
                    E("vector", lambda v: v.reciprocal(v2, v1),
                        waits=[(cSC, cSC.n)], inc=cVE)
                    E("vector", lambda v, sl=sl: v.tensor_copy(rstd_bc[:, sl], v2), inc=cVE)
                    vtc.append(cVE.n)
                sch.flush(nc)
            p0b_ve = cVE.n

            va = []
            for ic in range(NIC):
                bj = ic % 2
                lw = [(cVE, p0b_ve), (cTE, cTE.n), (cSC, cSC.n)]
                if ic >= 2:
                    lw.append((cVE, va[ic - 2]))
                E("sync", lambda s, ic=ic, bj=bj: s.dma_start(
                    out=xch[bj][:, :], in_=dx[ic * 128:(ic + 1) * 128, :]),
                    waits=lw, inc=cX[bj], inck=16)
                ldc, ld = cX[bj], cX[bj].n
                for quarter in range(4):
                    dst = xn_own if quarter < 2 else xn_prt
                    hs = slice(quarter * (T // 2), (quarter + 1) * (T // 2))
                    ds = slice(ic * T + (quarter % 2) * (T // 2), ic * T + (quarter % 2 + 1) * (T // 2))
                    E("vector", lambda v, bj=bj, hs=hs: v.tensor_tensor(
                        t1[:, :], xch[bj][:, hs], mean_bc[:, hs], ALU.subtract),
                        waits=[(ldc, ld)] if quarter == 0 else [], inc=cVE)
                    E("vector", lambda v, hs=hs, ds=ds, dst=dst: v.tensor_tensor(
                        dst[:, ds], t1[:, :], rstd_bc[:, hs], ALU.mult), inc=cVE)
                va.append(cVE.n)
            xn_done = cVE.n
            if debug:
                for ic in range(NIC):
                    E("gpsimd", lambda g, ic=ic: g.dma_start(
                        out=dbg["xn"][ic * 128:(ic + 1) * 128, 0:T], in_=xn_own[:, ic * T:(ic + 1) * T]),
                        waits=[(cVE, xn_done)], inc=cDBG, inck=16)
                    E("gpsimd", lambda g, ic=ic: g.dma_start(
                        out=dbg["xn"][ic * 128:(ic + 1) * 128, T:L], in_=xn_prt[:, ic * T:(ic + 1) * T]),
                        inc=cDBG, inck=16)
            sch.flush(nc)

        with contextlib.ExitStack() as s1:
            kp_A = [s1.enter_context(nc.sbuf_tensor(f"kpA{j}", [65, L], BF16)) for j in range(2)]
            kp_B = [s1.enter_context(nc.sbuf_tensor(f"kpB{j}", [65, L], BF16)) for j in range(2)]
            qp_A = [s1.enter_context(nc.sbuf_tensor(f"qpA{j}", [65, T], BF16)) for j in range(2)]
            qp_B = [s1.enter_context(nc.sbuf_tensor(f"qpB{j}", [65, T], BF16)) for j in range(2)]
            vsb2 = [s1.enter_context(nc.sbuf_tensor(f"vsb{j}", [128, 32 * 130], BF16)) for j in range(2)]
            pt = s1.enter_context(nc.sbuf_tensor("pt", [128, 4096], BF16))
            wq_sb = s1.enter_context(nc.sbuf_tensor("wq", [128, 1024], BF16))
            wk_sb = s1.enter_context(nc.sbuf_tensor("wk", [128, 1024], BF16))
            wv_sb = s1.enter_context(nc.sbuf_tensor("wv", [128, 1024], BF16))
            qraw = [s1.enter_context(nc.sbuf_tensor(f"qraw{j}", [128, 512], BF16)) for j in range(2)]
            qswp = [s1.enter_context(nc.sbuf_tensor(f"qswp{j}", [128, 512], BF16)) for j in range(2)]
            rt1 = [s1.enter_context(nc.sbuf_tensor(f"rt1_{j}", [128, 512], F32)) for j in range(2)]
            cs_sb = s1.enter_context(nc.sbuf_tensor("cs_sb", [128, 2048], BF16))
            den_sb = s1.enter_context(nc.sbuf_tensor("den_sb", [65, 1024], F32))
            bc_sb = s1.enter_context(nc.sbuf_tensor("bc_sb", [128, 1024], F32))
            bnd = s1.enter_context(nc.sbuf_tensor("bnd", [33, 8], F32))
            c8col = s1.enter_context(nc.sbuf_tensor("c8col", [128, 2], F32))
            MA = s1.enter_context(nc.sbuf_tensor("MA", [128, 1024], BF16))
            MB = s1.enter_context(nc.sbuf_tensor("MB", [128, 1024], BF16))
            ao_scr = s1.enter_context(nc.sbuf_tensor("ao_scr", [128, 1024], BF16))

            for j in range(2):
                E("vector", lambda v, j=j: v.memset(vsb2[j][:, :], 0.0), inc=cVE)
                vr = vsb2[j].rearrange("p (vt c) -> p vt c", c=130)
                E("vector", lambda v, vr=vr: v.memset(vr[:, :, 64:65], 1.0), inc=cVE)
                E("vector", lambda v, j=j: v.memset(kp_A[j][64:65, :], 1.0), inc=cVE)
                E("vector", lambda v, j=j: v.memset(kp_B[j][64:65, :], 1.0), inc=cVE)
            E("vector", lambda v: v.memset(bc_sb[:, :], 0.0), inc=cVE)
            E("vector", lambda v: v.memset(den_sb[:, :], 1.0), inc=cVE)

            with (
                nc.psum_tensor("pA", [128, 2048], F32) as pA,
                nc.psum_tensor("pB", [128, 2048], F32) as pB,
            ):
                def wload(g, dstT, row0):
                    return g.dma_start(out=dstT[:, :], in_=dwqkv[row0 * 128:(row0 + 1) * 128, :])

                prev_norm = 0
                swp_mark = [0, 0]
                qs_mark = [0, 0]
                rope_tile_idx = [0]
                rope_done = {}
                last_swp_box = [0]

                def gen_rope(p):
                    st = p % 2
                    pw = [(cTE, cTE.n), (cSC, cSC.n)]
                    E("gpsimd", lambda g, p=p: wload(g, wq_sb, p * 3 + 0), waits=pw, inc=cWQ[0], inck=16)
                    E("gpsimd", lambda g, p=p: wload(g, wk_sb, p * 3 + 1), inc=cWQ[0], inck=16)
                    E("gpsimd", lambda g, p=p: wload(g, wv_sb, p * 3 + 2), inc=cWQ[0], inck=16)
                    wt_ready = cWQ[0].n
                    info = {"wt": wt_ready}
                    ve_tile = []
                    for tile in range(12):
                        if tile < 4:
                            wsb, dstA, dstB, j = wq_sb, qp_A[st], qp_B[st], tile
                            xs = lambda ic, j=j: xn_own[:, ic * T + j * 512: ic * T + (j + 1) * 512]
                        else:
                            j = tile - 4
                            wsb, dstA, dstB = wk_sb, kp_A[st], kp_B[st]
                            xs = lambda ic, j=j: (xn_own if j < 4 else xn_prt)[:, ic * T + (j % 4) * 512: ic * T + (j % 4 + 1) * 512]
                        ps = pB[:, (tile % 2) * 512:(tile % 2 + 1) * 512]
                        gpar = rope_tile_idx[0] % 2
                        gw = [(cVE, ve_tile[tile - 2])] if tile >= 2 else [(cVE, cVE.n)]
                        E("sync", lambda s, j=j, gpar=gpar: s.dma_start(
                            out=cs_sb[:, gpar * 1024:gpar * 1024 + 512],
                            in_=dcos[:, j * 512:(j + 1) * 512]),
                            waits=gw, inc=cCS[gpar], inck=16)
                        E("sync", lambda s, j=j, gpar=gpar: s.dma_start(
                            out=cs_sb[:, gpar * 1024 + 512:gpar * 1024 + 1024],
                            in_=dsin[:, j * 512:(j + 1) * 512]),
                            inc=cCS[gpar], inck=16)
                        cs_ready = cCS[gpar].n
                        rope_tile_idx[0] += 1
                        w0 = [(cVE, xn_done), (cWQ[0], wt_ready)]
                        if tile >= 2:
                            w0.append((cVE, ve_tile[tile - 2]))
                        else:
                            w0.append((cVE, prev_norm))
                        for ic in range(NIC):
                            E("tensor", lambda t, ic=ic, ps=ps, wsb=wsb, xs=xs: t.matmul(
                                ps, wsb[:, ic * 128:(ic + 1) * 128], xs(ic),
                                start=(ic == 0), stop=(ic == NIC - 1)),
                                waits=w0 if ic == 0 else [],
                                inc=cTE if ic == NIC - 1 else None)
                        tm = cTE.n
                        qr, qs, r1 = qraw[gpar], qswp[gpar], rt1[gpar]
                        E("scalar", lambda s, ps=ps, qr=qr: s.activation(
                            out=qr[:, :], in_=ps, func=AF.Copy),
                            waits=[(cTE, tm), (cSW, swp_mark[gpar]), (cQS, qs_mark[gpar]), (cSC, cSC.n)], inc=cSC)
                        vcp = cSC.n
                        for (a, b2) in ((0, 32), (32, 0), (64, 96), (96, 64)):
                            E("sync", lambda s, a=a, b2=b2, qr=qr, qs=qs: s.dma_start(
                                out=qs[a:a + 32, :], in_=qr[b2:b2 + 32, :]),
                                waits=[(cSC, vcp)] if a == 0 else [], inc=cSW, inck=16)
                        swp_mark[gpar] = last_swp = cSW.n
                        E("vector", lambda v, gpar=gpar, r1=r1, qr=qr: v.tensor_tensor(
                            r1[:, :], qr[:, :], cs_sb[:, gpar * 1024:gpar * 1024 + 512], ALU.mult),
                            waits=[(cCS[gpar], cs_ready), (cSC, vcp)], inc=cVE)
                        E("vector", lambda v, gpar=gpar, qs=qs: v.tensor_tensor(
                            qs[:, :], qs[:, :], cs_sb[:, gpar * 1024 + 512:gpar * 1024 + 1024], ALU.mult),
                            waits=[(cSW, last_swp)], inc=cVE)
                        E("vector", lambda v, r1=r1, qs=qs: v.tensor_tensor(
                            r1[:, :], r1[:, :], qs[:, :], ALU.add), inc=cVE)
                        vq = cVE.n
                        sl = slice(j * 512, (j + 1) * 512)
                        E("gpsimd", lambda g, sl=sl, dstA=dstA, r1=r1: g.dma_start(
                            out=dstA[0:64, sl], in_=r1[0:64, :]),
                            waits=[(cVE, vq)], inc=cQS, inck=16)
                        E("gpsimd", lambda g, sl=sl, dstB=dstB, r1=r1: g.dma_start(
                            out=dstB[0:64, sl], in_=r1[64:128, :]),
                            inc=cQS, inck=16)
                        qs_mark[gpar] = cQS.n
                        ve_tile.append(cVE.n)
                        yield
                    info["dmas"] = cQS.n
                    info["ve"] = cVE.n
                    rope_done[p] = info

                for p in range(NPAIR):
                    st = p % 2
                    kpA, kpB, qpA, qpB, vsb = kp_A[st], kp_B[st], qp_A[st], qp_B[st], vsb2[st]
                    if p == 0:
                        for _ in gen_rope(0):
                            pass
                    qk_dmas = rope_done[p]["dmas"]
                    wt_ready = rope_done[p]["wt"]

                    E("vector", lambda v: v.memset(bnd[:, :], 0.0),
                        waits=[(cTE, cTE.n)], inc=cVE)
                    bnd_ve0 = cVE.n
                    sqm = [0, 0]
                    mmm = [cTE.n, cTE.n]
                    rdm = []
                    ti = 0
                    for srcs, col in (((qpA, qpB), 0), ((kpA, kpB), 1)):
                        n_tiles = 4 if col == 0 else 8
                        for j in range(n_tiles):
                            for hh, srch in enumerate(srcs):
                                par = ti % 2
                                sqb = MA[0:64, par * 512:(par + 1) * 512]
                                E("scalar", lambda s, srch=srch, j=j, sqb=sqb: s.activation(
                                    out=sqb, in_=srch[0:64, j * 512:(j + 1) * 512], func=AF.Square),
                                    waits=[(cQS, qk_dmas), (cTE, mmm[par]), (cSC, cSC.n)], inc=cSC)
                                sqm[par] = cSC.n
                                row = hh * 32
                                pslot = pA[row:row + 1, par * 512:(par + 1) * 512]
                                wmm = [(cSC, sqm[par])]
                                if ti >= 2:
                                    wmm.append((cVE, rdm[ti - 2]))
                                else:
                                    wmm.append((cVE, bnd_ve0))
                                E("tensor", lambda t, pslot=pslot, sqb=sqb: t.matmul(
                                    pslot, onescol_bf[0:64, :], sqb, start=True, stop=True),
                                    waits=wmm, inc=cTE)
                                mmm[par] = cTE.n
                                E("vector", lambda v, row=row, pslot=pslot: v.tensor_reduce(
                                    bnd[row:row + 1, 5:6], pslot, AX.X, ALU.max),
                                    waits=[(cTE, mmm[par])], inc=cVE)
                                E("vector", lambda v, col=col, row=row: v.tensor_tensor(
                                    bnd[row:row + 1, col:col + 1], bnd[row:row + 1, col:col + 1],
                                    bnd[row:row + 1, 5:6], ALU.max), inc=cVE)
                                rdm.append(cVE.n)
                                ti += 1
                    E("vector", lambda v: v.tensor_tensor(
                        bnd[:, 2:3], bnd[:, 0:1], bnd[:, 1:2], ALU.mult), inc=cVE)
                    E("scalar", lambda s: s.activation(
                        out=bnd[:, 3:4], in_=bnd[:, 2:3], func=AF.Sqrt),
                        waits=[(cVE, cVE.n)], inc=cSC)
                    E("vector", lambda v: v.tensor_scalar(
                        bnd[:, 4:5], bnd[:, 3:4], -0.1275 / 8.0, None, ALU.mult),
                        waits=[(cSC, cSC.n)], inc=cVE)
                    E("vector", lambda v: v.tensor_scalar(
                        bnd[:, 6:7], bnd[:, 3:4], -0.1275 * 8.0, None, ALU.mult), inc=cVE)
                    bnd_ve = cVE.n
                    E("tensor", lambda t: t.matmul(
                        pA[0:128, 0:1], ones_f32[0:1, :], bnd[0:1, 4:5], start=True, stop=True),
                        waits=[(cVE, bnd_ve)])
                    E("tensor", lambda t: t.matmul(
                        pA[0:128, 512:513], ones_f32[32:33, :], bnd[32:33, 4:5], start=True, stop=True),
                        inc=cTE)
                    E("vector", lambda v: v.tensor_copy(c8col[:, 0:1], pA[0:128, 0:1]),
                        waits=[(cTE, cTE.n)], inc=cVE)
                    E("vector", lambda v: v.tensor_copy(c8col[:, 1:2], pA[0:128, 512:513]), inc=cVE)
                    c8_ve = cVE.n

                    v_marks = []
                    for vt in range(32):
                        ps = pA[:, (vt % 2) * 512: (vt % 2) * 512 + 128]
                        xsb = xn_own if vt < 16 else xn_prt
                        tcol = (vt % 16) * 128
                        w0 = [(cVE, c8_ve), (cWQ[0], wt_ready)]
                        if vt >= 2:
                            w0 = [(cVE, v_marks[vt - 2])]
                        for ic in range(NIC):
                            E("tensor", lambda t, ic=ic, ps=ps, xsb=xsb, tcol=tcol: t.matmul(
                                ps, xsb[:, ic * T + tcol: ic * T + tcol + 128],
                                wv_sb[:, ic * 128:(ic + 1) * 128],
                                start=(ic == 0), stop=(ic == NIC - 1)),
                                waits=w0 if ic == 0 else [],
                                inc=cTE if ic == NIC - 1 else None)
                        tm = cTE.n
                        E("vector", lambda v, ps=ps, vt=vt, vsb=vsb: v.tensor_copy(
                            vsb[:, vt * 130:vt * 130 + 64], ps[:, 0:64]),
                            waits=[(cTE, tm)], inc=cVE)
                        E("vector", lambda v, ps=ps, vt=vt, vsb=vsb: v.tensor_copy(
                            vsb[:, vt * 130 + 65:vt * 130 + 129], ps[:, 64:128]), inc=cVE)
                        v_marks.append(cVE.n)
                    v_done = cVE.n

                    if debug and p == 0:
                        E("gpsimd", lambda g, qpA=qpA: g.dma_start(out=dbg["q"][0:64, :], in_=qpA[0:64, :]),
                            waits=[(cVE, v_done)], inc=cDBG, inck=16)
                        E("gpsimd", lambda g, qpB=qpB: g.dma_start(out=dbg["q"][64:128, :], in_=qpB[0:64, :]), inc=cDBG, inck=16)
                        E("gpsimd", lambda g, kpA=kpA: g.dma_start(out=dbg["k"][0:64, :], in_=kpA[0:64, :]), inc=cDBG, inck=16)
                        E("gpsimd", lambda g, kpB=kpB: g.dma_start(out=dbg["k"][64:128, :], in_=kpB[0:64, :]), inc=cDBG, inck=16)
                        E("gpsimd", lambda g, vsb=vsb: g.dma_start(out=dbg["v"][:, :], in_=vsb[:, :]), inc=cDBG, inck=16)

                    def passA(qc, entry_sc):
                        qoff = qc * 1024
                        maxm = []
                        for kt in range(32):
                            pst = (pA, pB)[kt % 2]
                            for head, (kph, qph, Mh) in enumerate(((kpA, qpA, MA), (kpB, qpB, MB))):
                                pslice = pst[:, head * 1024:(head + 1) * 1024]
                                for half in range(2):
                                    w0 = []
                                    if half == 0 and head == 0:
                                        if kt >= 2:
                                            w0 = [(cVE, maxm[kt - 2])]
                                        else:
                                            w0 = [(cVE, prev_norm), (cVE, v_done)] if qc == 0 else [(cVE, prev_norm)]
                                    E("tensor", lambda t, kt=kt, half=half, qoff=qoff,
                                        kph=kph, qph=qph, pslice=pslice: t.matmul(
                                        pslice[:, half * 512:(half + 1) * 512],
                                        kph[0:64, kt * 128:(kt + 1) * 128],
                                        qph[0:64, qoff + half * 512: qoff + (half + 1) * 512],
                                        start=True, stop=True),
                                        waits=w0, inc=cTE if half == 1 else None)
                                tmh = cTE.n
                                if kt == 0:
                                    E("vector", lambda v, Mh=Mh, pslice=pslice: v.tensor_copy(
                                        Mh[:, :], pslice),
                                        waits=[(cTE, tmh), (cSC, entry_sc)], inc=cVE)
                                else:
                                    E("vector", lambda v, Mh=Mh, pslice=pslice: v.tensor_tensor(
                                        Mh[:, :], pslice, Mh[:, :], ALU.max),
                                        waits=[(cTE, tmh)], inc=cVE)
                            maxm.append(cVE.n)
                        return cVE.n

                    def fold(qc, maxdone):
                        qoff = qc * 1024
                        for head, (Mh, qph, row) in enumerate(((MA, qpA, 0), (MB, qpB, 32))):
                            E("scalar", lambda s, Mh=Mh, head=head: s.activation(
                                out=Mh[:, :], in_=Mh[:, :], func=AF.Exp,
                                bias=c8col[:, head:head + 1], scale=0.125 / 8.0),
                                waits=[(cVE, maxdone), (cTE, cTE.n), (cSC, cSC.n)], inc=cSC)
                            scm = cSC.n
                            for half in range(2):
                                E("tensor", lambda t, row=row, Mh=Mh, half=half: t.matmul(
                                    pA[row:row + 1, half * 512:(half + 1) * 512],
                                    onescol_bf[:, :], Mh[:, half * 512:(half + 1) * 512],
                                    start=True, stop=True),
                                    waits=[(cSC, scm)] if half == 0 else [],
                                    inc=cTE if half == 1 else None)
                            E("vector", lambda v, row=row: v.tensor_single_scalar(
                                bc_sb[row:row + 1, 0:1024], pA[row:row + 1, 0:1024], 1e-38, ALU.add),
                                waits=[(cTE, cTE.n)], inc=cVE)
                            E("scalar", lambda s, row=row: s.activation(
                                out=bc_sb[row:row + 1, 0:1024], in_=bc_sb[row:row + 1, 0:1024],
                                func=AF.Ln),
                                waits=[(cVE, cVE.n)], inc=cSC)
                        E("vector", lambda v: v.tensor_scalar(
                            bc_sb[0:33, 0:1024], bc_sb[0:33, 0:1024], -64.0, bnd[0:33, 6:7], ALU.mult, ALU.add),
                            waits=[(cSC, cSC.n)], inc=cVE)
                        lnm = cVE.n
                        E("gpsimd", lambda g, qoff=qoff, qph=qpA: g.dma_start(
                            out=qph[64:65, qoff:qoff + 1024], in_=bc_sb[0:1, 0:1024]),
                            waits=[(cVE, lnm)], inc=cU, inck=16)
                        E("gpsimd", lambda g, qoff=qoff, qph=qpB: g.dma_start(
                            out=qph[64:65, qoff:qoff + 1024], in_=bc_sb[32:33, 0:1024]),
                            inc=cU, inck=16)
                        return cU.n, cSC.n

                    def passB0(qc, u_dma, lnsc):
                        qoff = qc * 1024
                        expA, expB = [], []
                        for kt in range(32):
                            par = kt % 2
                            for half in range(2):
                                w0 = []
                                if half == 0:
                                    w0 = [(cSC, expA[kt - 1])] if kt >= 1 else [(cU, u_dma), (cSC, lnsc)]
                                E("tensor", lambda t, kt=kt, half=half, qoff=qoff, kpA=kpA, qpA=qpA: t.matmul(
                                    pA[:, half * 512:(half + 1) * 512],
                                    kpA[0:65, kt * 128:(kt + 1) * 128],
                                    qpA[0:65, qoff + half * 512: qoff + (half + 1) * 512],
                                    start=True, stop=True),
                                    waits=w0, inc=cTE if half == 1 else None)
                            qkA = cTE.n
                            for half in range(2):
                                w0 = []
                                if half == 0 and kt >= 1:
                                    w0 = [(cSC, expB[kt - 1])]
                                E("tensor", lambda t, kt=kt, half=half, qoff=qoff, kpB=kpB, qpB=qpB: t.matmul(
                                    pB[:, half * 512:(half + 1) * 512],
                                    kpB[0:65, kt * 128:(kt + 1) * 128],
                                    qpB[0:65, qoff + half * 512: qoff + (half + 1) * 512],
                                    start=True, stop=True),
                                    waits=w0, inc=cTE if half == 1 else None)
                            qkB = cTE.n
                            if kt > 0:
                                ktm = kt - 1
                                parm = ktm % 2
                                for half in range(2):
                                    E("tensor", lambda t, ktm=ktm, parm=parm, half=half, vsb=vsb: t.matmul(
                                        pA[0:65, 1024 + half * 512:1024 + (half + 1) * 512],
                                        vsb[:, ktm * 130:ktm * 130 + 65],
                                        pt[:, parm * 1024 + half * 512: parm * 1024 + (half + 1) * 512],
                                        start=(ktm == 0), stop=False))
                                    E("tensor", lambda t, ktm=ktm, parm=parm, half=half, vsb=vsb: t.matmul(
                                        pB[64:128, 1024 + half * 512:1024 + (half + 1) * 512],
                                        vsb[:, ktm * 130 + 65:ktm * 130 + 129],
                                        pt[:, 2048 + parm * 1024 + half * 512: 2048 + parm * 1024 + (half + 1) * 512],
                                        start=(ktm == 0), stop=False, tile_position=(0, 64)))
                                    E("tensor", lambda t, ktm=ktm, parm=parm, half=half: t.matmul(
                                        pB[0:1, 1024 + half * 512:1024 + (half + 1) * 512],
                                        onescol_bf[:, :],
                                        pt[:, 2048 + parm * 1024 + half * 512: 2048 + parm * 1024 + (half + 1) * 512],
                                        start=(ktm == 0), stop=False))
                            E("scalar", lambda s, par=par: s.activation(
                                out=pt[:, par * 1024:(par + 1) * 1024], in_=pA[:, 0:1024],
                                func=AF.Exp, scale=0.125),
                                waits=[(cTE, qkA)], inc=cSC)
                            expA.append(cSC.n)
                            E("scalar", lambda s, par=par: s.activation(
                                out=pt[:, 2048 + par * 1024:2048 + (par + 1) * 1024], in_=pB[:, 0:1024],
                                func=AF.Exp, scale=0.125),
                                waits=[(cTE, qkB)], inc=cSC)
                            expB.append(cSC.n)
                        _pv_tail(vsb, expA, expB)
                        return expA[31], expB[31], cTE.n

                    def _pv_tail(vsb, expA, expB):
                        ktm, parm = 31, 1
                        for half in range(2):
                            E("tensor", lambda t, half=half, parm=parm, ktm=ktm, vsb=vsb: t.matmul(
                                pA[0:65, 1024 + half * 512:1024 + (half + 1) * 512],
                                vsb[:, ktm * 130:ktm * 130 + 65],
                                pt[:, parm * 1024 + half * 512: parm * 1024 + (half + 1) * 512],
                                start=False, stop=True),
                                waits=[(cSC, expA[31])] if half == 0 else [])
                            E("tensor", lambda t, half=half, parm=parm, ktm=ktm, vsb=vsb: t.matmul(
                                pB[64:128, 1024 + half * 512:1024 + (half + 1) * 512],
                                vsb[:, ktm * 130 + 65:ktm * 130 + 129],
                                pt[:, 2048 + parm * 1024 + half * 512: 2048 + parm * 1024 + (half + 1) * 512],
                                start=False, stop=True, tile_position=(0, 64)),
                                waits=[(cSC, expB[31])] if half == 0 else [])
                            E("tensor", lambda t, half=half, parm=parm: t.matmul(
                                pB[0:1, 1024 + half * 512:1024 + (half + 1) * 512],
                                onescol_bf[:, :],
                                pt[:, 2048 + parm * 1024 + half * 512: 2048 + parm * 1024 + (half + 1) * 512],
                                start=False, stop=True),
                                inc=cTE if half == 1 else None)

                    def gen_passB1(qc, u_dma, lnsc, entry_ve):
                        qoff = qc * 1024
                        exps = [[], []]
                        for i in range(64):
                            head, kt = i % 2, i // 2
                            kph, qph = ((kpA, qpA), (kpB, qpB))[head]
                            par = kt % 2
                            if i == 0:
                                w0 = [(cU, u_dma), (cSC, lnsc), (cVE, entry_ve)]
                            else:
                                w0 = [(cSC, exps[1 - head][-1])]
                            for half in range(2):
                                E("tensor", lambda t, kt=kt, half=half, qoff=qoff, kph=kph, qph=qph: t.matmul(
                                    pA[:, half * 512:(half + 1) * 512],
                                    kph[0:65, kt * 128:(kt + 1) * 128],
                                    qph[0:65, qoff + half * 512: qoff + (half + 1) * 512],
                                    start=True, stop=True),
                                    waits=w0 if half == 0 else [],
                                    inc=cTE if half == 1 else None)
                            qk = cTE.n
                            if kt > 0:
                                ktm = kt - 1
                                parm = ktm % 2
                                for half in range(2):
                                    if head == 0:
                                        E("tensor", lambda t, ktm=ktm, parm=parm, half=half, vsb=vsb: t.matmul(
                                            pA[0:65, 1024 + half * 512:1024 + (half + 1) * 512],
                                            vsb[:, ktm * 130:ktm * 130 + 65],
                                            pt[:, parm * 1024 + half * 512: parm * 1024 + (half + 1) * 512],
                                            start=(ktm == 0), stop=False))
                                    else:
                                        E("tensor", lambda t, ktm=ktm, parm=parm, half=half, vsb=vsb: t.matmul(
                                            pB[64:128, 1024 + half * 512:1024 + (half + 1) * 512],
                                            vsb[:, ktm * 130 + 65:ktm * 130 + 129],
                                            pt[:, 2048 + parm * 1024 + half * 512: 2048 + parm * 1024 + (half + 1) * 512],
                                            start=(ktm == 0), stop=False, tile_position=(0, 64)))
                                        E("tensor", lambda t, ktm=ktm, parm=parm, half=half: t.matmul(
                                            pB[0:1, 1024 + half * 512:1024 + (half + 1) * 512],
                                            onescol_bf[:, :],
                                            pt[:, 2048 + parm * 1024 + half * 512: 2048 + parm * 1024 + (half + 1) * 512],
                                            start=(ktm == 0), stop=False))
                            E("scalar", lambda s, par=par, head=head: s.activation(
                                out=pt[:, head * 2048 + par * 1024:head * 2048 + (par + 1) * 1024],
                                in_=pA[:, 0:1024],
                                func=AF.Exp, scale=0.125),
                                waits=[(cTE, qk)], inc=cSC)
                            exps[head].append(cSC.n)
                            yield
                        _pv_tail(vsb, exps[0], exps[1])
                        pb1_out[0] = (exps[0][31], exps[1][31], cTE.n)

                    def normalize(qc, expAl, expBl, pv_te):
                        nonlocal prev_norm
                        qoff = qc * 1024
                        E("vector", lambda v: v.tensor_copy(den_sb[64:65, :], pA[64:65, 1024:2048]),
                            waits=[(cTE, pv_te), (cAO, cAO.n)], inc=cVE)
                        E("vector", lambda v: v.tensor_copy(den_sb[0:1, :], pB[0:1, 1024:2048]), inc=cVE)
                        E("vector", lambda v: v.reciprocal(den_sb[0:65, :], den_sb[0:65, :]), inc=cVE)
                        den_ve = cVE.n
                        for half in range(2):
                            E("tensor", lambda t, half=half: t.matmul(
                                pA[:, half * 512:(half + 1) * 512],
                                ones_f32[64:65, :], den_sb[64:65, half * 512:(half + 1) * 512],
                                start=True, stop=True),
                                waits=[(cVE, den_ve), (cSC, expAl)] if half == 0 else [],
                                inc=cTE if half == 1 else None)
                        E("vector", lambda v: v.tensor_copy(bc_sb[:, :], pA[:, 0:1024]),
                            waits=[(cTE, cTE.n)], inc=cVE)
                        E("vector", lambda v: v.tensor_tensor(
                            ao_scr[0:64, :], pA[0:64, 1024:2048], bc_sb[0:64, :], ALU.mult),
                            waits=[(cVE, cVE.n)], inc=cVE)
                        for half in range(2):
                            E("tensor", lambda t, half=half: t.matmul(
                                pA[:, half * 512:(half + 1) * 512],
                                ones_f32[0:1, :], den_sb[0:1, half * 512:(half + 1) * 512],
                                start=True, stop=True),
                                waits=[(cVE, cVE.n), (cSC, expBl)] if half == 0 else [],
                                inc=cTE if half == 1 else None)
                        E("vector", lambda v: v.tensor_copy(bc_sb[:, :], pA[:, 0:1024]),
                            waits=[(cTE, cTE.n), (cVE, cVE.n)], inc=cVE)
                        E("vector", lambda v: v.tensor_tensor(
                            ao_scr[64:128, :], pB[64:128, 1024:2048], bc_sb[64:128, :], ALU.mult),
                            waits=[(cVE, cVE.n)], inc=cVE)
                        aov = cVE.n
                        E("sync", lambda s, p=p, qoff=qoff: s.dma_start(
                            out=dao[p * 128:(p + 1) * 128, qoff:qoff + 1024], in_=ao_scr[:, :]),
                            waits=[(cVE, aov)], inc=cAO, inck=16)
                        prev_norm = cVE.n

                    m0 = passA(0, cSC.n)
                    u0, ls0 = fold(0, m0)
                    eA0, eB0, pv0 = passB0(0, u0, ls0)
                    normalize(0, eA0, eB0, pv0)
                    m1 = passA(1, cSC.n)
                    u1, ls1 = fold(1, m1)
                    pb1_out = [None]
                    gens = [(gen_passB1(1, u1, ls1, cVE.n), 6)]
                    if p + 1 < NPAIR:
                        gens.append((gen_rope(p + 1), 1))
                    live = [[g, k] for g, k in gens]
                    while live:
                        for item in live[:]:
                            g, k = item
                            try:
                                for _ in range(k):
                                    next(g)
                            except StopIteration:
                                live.remove(item)
                    eA1, eB1, pv1 = pb1_out[0]
                    normalize(1, eA1, eB1, pv1)

                sch.flush(nc)
                ap_done_box[0] = cAO.n

    with contextlib.ExitStack() as s2:
        bufA = s2.enter_context(nc.sbuf_tensor("bufA", [128, NHC * TCT], BF16))
        bufB = s2.enter_context(nc.sbuf_tensor("bufB", [128, NHC * TCT], BF16))
        wh = [s2.enter_context(nc.sbuf_tensor(f"wh{j}", [128, 4096], BF16)) for j in range(3)]
        ao_sb = s2.enter_context(nc.sbuf_tensor("ao_p2", [128, NIC * T], BF16))
        wo_sb = [s2.enter_context(nc.sbuf_tensor(f"wo{j}", [128, 1024], BF16)) for j in range(2)]
        xres = [s2.enter_context(nc.sbuf_tensor(f"xres{j}", [128, 512], F32)) for j in range(2)]
        ytmp = s2.enter_context(nc.sbuf_tensor("ytmp", [128, 512], F32))
        ysb = [s2.enter_context(nc.sbuf_tensor(f"ysb{j}", [128, 512], F32)) for j in range(2)]

        with nc.psum_tensor("pF", [128, 2048], F32) as pF, \
             nc.psum_tensor("pF2", [128, 2048], F32) as pF2:
            for dc in range(NIC):
                E("sync", lambda s, dc=dc: s.dma_start(
                    out=ao_sb[:, dc * T:(dc + 1) * T], in_=dao[dc * 128:(dc + 1) * 128, :]),
                    waits=[(cAO, ap_done_box[0])] if dc == 0 else [], inc=cAO, inck=16)
            ao_ready = cAO.n
            prev_tail = [(cVE, cVE.n)] * 4
            prev_wh_marks = [cTE.n, cTE.n, cTE.n]
            y_ve_marks = []
            st_marks = []

            opdone = [0]

            def gen_op():
                pass
                E("gpsimd", lambda g: g.dma_start(out=wo_sb[0][:, :], in_=dwout[0:128, :]),
                    waits=[(cTE, cTE.n)], inc=cWO[0], inck=16)
                op_marks = []
                for oc in range(NIC):
                    ob = oc % 2
                    wt_r = cWO[ob].n
                    if oc < NIC - 1:
                        E("gpsimd", lambda g, oc=oc: g.dma_start(
                            out=wo_sb[(oc + 1) % 2][:, :], in_=dwout[(oc + 1) * 128:(oc + 2) * 128, :]),
                            waits=[(cTE, op_marks[oc * 4 - 1][1]) if oc >= 1 else (cTE, cTE.n)],
                            inc=cWO[(oc + 1) % 2], inck=16)
                    for qt in range(4):
                        idx = oc * 4 + qt
                        ps = pF2[:, (idx % 4) * 512:(idx % 4 + 1) * 512]
                        w0 = [(cAO, ao_ready), (cWO[ob], wt_r)]
                        if idx >= 4:
                            w0.append((cVE, op_marks[idx - 4][0]))
                        for dc in range(NIC):
                            E("tensor", lambda t, dc=dc, qt=qt, ps=ps, ob=ob: t.matmul(
                                ps, wo_sb[ob][:, dc * 128:(dc + 1) * 128],
                                ao_sb[:, dc * T + qt * 512:dc * T + (qt + 1) * 512],
                                start=(dc == 0), stop=(dc == NIC - 1)),
                                waits=w0 if dc == 0 else [],
                                inc=cTE if dc == NIC - 1 else None)
                        tm = cTE.n
                        E("vector", lambda v, oc=oc, qt=qt, ps=ps: v.tensor_copy(
                            attn_proj[:, oc * T + qt * 512:oc * T + (qt + 1) * 512], ps),
                            waits=[(cTE, tm)], inc=cVE)
                        op_marks.append((cVE.n, tm))
                        yield
                opdone[0] = cVE.n

            def gen_stage(dw, n_oc, n_ic, ncols, rhs_fn, evac_fn, first_extra=None,
                          dwc=None, first_pass=True):
                nonlocal prev_tail, prev_wh_marks
                te_marks = []
                ev = []
                for oc in range(n_oc):
                    wb = oc % 3
                    lw = [(cTE, te_marks[oc - 3])] if oc >= 3 else [(cTE, prev_wh_marks[oc])]
                    if dwc is not None and first_pass:
                        lw.append((cWB[wb % 2], cWB[wb % 2].n))
                    src_t = dw if (dwc is None or first_pass) else dwc
                    E("gpsimd", lambda g, oc=oc, wb=wb, src_t=src_t, ncols=ncols: g.dma_start(
                        out=wh[wb][:, 0:ncols], in_=src_t[oc * 128:(oc + 1) * 128, :]),
                        waits=lw, inc=cWH[wb], inck=16)
                    wt_r = cWH[wb].n
                    if dwc is not None and first_pass:
                        E("sync", lambda s, oc=oc, wb=wb, dwc=dwc, ncols=ncols: s.dma_start(
                            out=dwc[oc * 128:(oc + 1) * 128, :], in_=wh[wb][:, 0:ncols]),
                            waits=[(cWH[wb], wt_r)], inc=cWB[wb % 2], inck=16)
                    ps = pF[:, 512 * (oc % 4):512 * (oc % 4 + 1)]
                    w0 = [(cWH[wb], wt_r)]
                    if first_extra is not None and oc == 0:
                        w0.append(first_extra)
                    if oc >= 4:
                        w0.append(ev[oc - 4])
                    else:
                        w0.append(prev_tail[oc])
                    for ic in range(n_ic):
                        E("tensor", lambda t, ic=ic, ps=ps, wb=wb: t.matmul(
                            ps, wh[wb][:, ic * 128:(ic + 1) * 128], rhs_fn(ic),
                            start=(ic == 0), stop=(ic == n_ic - 1)),
                            waits=w0 if ic == 0 else [],
                            inc=cTE if ic == n_ic - 1 else None)
                    te_marks.append(cTE.n)
                    ev.append(evac_fn(oc, ps, cTE.n))
                    yield
                prev_tail = ev[-4:]
                prev_wh_marks = [te_marks[-3], te_marks[-2], te_marks[-1]]
                evout[0] = ev

            def stage(*a, **k):
                evout[0] = None
                for _ in gen_stage(*a, **k):
                    pass
                return evout[0]

            evout = [None]

            for tct in range(NTCT):
                tof = tct * TCT

                def evac_w2(oc, ps, tm):
                    E("vector", lambda v, oc=oc, ps=ps: v.tensor_scalar(
                        bufB[:, oc * TCT:(oc + 1) * TCT], ps, s2_sb[:, oc:oc + 1], None, ALU.mult),
                        waits=[(cTE, tm)], inc=cVE)
                    return (cVE, cVE.n)

                def evac_w1(oc, ps, tm):
                    E("scalar", lambda s, oc=oc, ps=ps: s.activation(
                        out=bufA[:, oc * TCT:(oc + 1) * TCT], in_=ps,
                        func=AF.Silu, scale=s1_sb[:, oc:oc + 1]),
                        waits=[(cTE, tm)], inc=cSC)
                    scm = cSC.n
                    E("vector", lambda v, oc=oc: v.tensor_tensor(
                        bufB[:, oc * TCT:(oc + 1) * TCT],
                        bufA[:, oc * TCT:(oc + 1) * TCT],
                        bufB[:, oc * TCT:(oc + 1) * TCT], ALU.mult),
                        waits=[(cSC, scm), (cVE, w2ev[oc][1])], inc=cVE)
                    return (cSC, scm)

                def evac_h1(oc, ps, tm):
                    E("scalar", lambda s, oc=oc, ps=ps: s.activation(
                        out=bufA[:, oc * TCT:(oc + 1) * TCT], in_=ps,
                        func=AF.Gelu, scale=h1s_sb[:, oc:oc + 1]),
                        waits=[(cTE, tm)], inc=cSC)
                    return (cSC, cSC.n)

                def evac_h2(oc, ps, tm):
                    E("scalar", lambda s, oc=oc, ps=ps: s.activation(
                        out=bufB[:, oc * TCT:(oc + 1) * TCT], in_=ps,
                        func=AF.Gelu, scale=h2s_sb[:, oc:oc + 1]),
                        waits=[(cTE, tm)], inc=cSC)
                    return (cSC, cSC.n)

                def evac_w3(oc, ps, tm):
                    jb = oc % 2
                    nst = len(st_marks)
                    lw = [(cVE, y_ve_marks[nst - 2])] if nst >= 2 else []
                    E("sync", lambda s, oc=oc, jb=jb, tof=tof: s.dma_start(
                        out=xres[jb][:, :], in_=dx[oc * 128:(oc + 1) * 128, tof:tof + TCT]),
                        waits=lw, inc=cXR[jb], inck=16)
                    ld = cXR[jb].n
                    E("vector", lambda v, oc=oc, ps=ps, tof=tof: v.scalar_tensor_tensor(
                        ytmp[:, :], ps, w3s_sb[:, oc:oc + 1],
                        attn_proj[:, oc * T + tof: oc * T + tof + TCT],
                        ALU.mult, ALU.add),
                        waits=[(cTE, tm), (cVE, opdone[0])], inc=cVE)
                    sw = [(cXR[jb], ld)]
                    if nst >= 2:
                        sw.append((cST[jb], st_marks[nst - 2]))
                    E("vector", lambda v, jb=jb: v.tensor_tensor(
                        ysb[jb][:, :], ytmp[:, :], xres[jb][:, :], ALU.add),
                        waits=sw, inc=cVE)
                    yv = cVE.n
                    E("sync", lambda s, oc=oc, jb=jb, tof=tof: s.dma_start(
                        out=dy[oc * 128:(oc + 1) * 128, tof:tof + TCT], in_=ysb[jb][:, :]),
                        waits=[(cVE, yv)], inc=cST[jb], inck=16)
                    y_ve_marks.append(yv)
                    st_marks.append(cST[jb].n)
                    return (cVE, cVE.n)

                if tct == 0:
                    evout[0] = None
                    g1 = gen_stage(dw2, NHC, NIC, 1024,
                                   lambda ic: xn_own[:, ic * T + tof: ic * T + tof + TCT],
                                   evac_w2, first_extra=(cTAB, scales_wt),
                                   dwc=dw2c, first_pass=True)
                    g2 = gen_op()
                    live = [g1, g2]
                    while live:
                        for g in live[:]:
                            try:
                                next(g)
                            except StopIteration:
                                live.remove(g)
                    ew2 = evout[0]
                else:
                    ew2 = stage(dw2, NHC, NIC, 1024,
                                lambda ic: xn_own[:, ic * T + tof: ic * T + tof + TCT],
                                evac_w2, first_extra=(cTAB, scales_wt),
                                dwc=dw2c, first_pass=False)
                w2ev = ew2
                ew1 = stage(dw1, NHC, NIC, 1024,
                            lambda ic: xn_own[:, ic * T + tof: ic * T + tof + TCT],
                            evac_w1, dwc=dw1c, first_pass=(tct == 0))
                h_done = cVE.n
                eh1 = stage(dh1, NHC, NHC, 4096,
                            lambda ic: bufB[:, ic * TCT:(ic + 1) * TCT],
                            evac_h1, first_extra=(cVE, h_done),
                            dwc=dh1c, first_pass=(tct == 0))
                eh2 = stage(dh2, NHC, NHC, 4096,
                            lambda ic: bufA[:, ic * TCT:(ic + 1) * TCT],
                            evac_h2, first_extra=(cSC, eh1[-1][1]),
                            dwc=dh2c, first_pass=(tct == 0))
                stage(dw3, NIC, NHC, 4096,
                      lambda ic: bufB[:, ic * TCT:(ic + 1) * TCT],
                      evac_w3, first_extra=(cSC, eh2[-1][1]),
                      dwc=dw3c, first_pass=(tct == 0))

            E("sync", lambda s: None, waits=[(cST[0], cST[0].n), (cST[1], cST[1].n)])
            sch.flush(nc)

    stack.close()
    return nc, dbg



def _strip_img(wT, n_ic):
    return wT.reshape(n_ic, 128, 128).transpose(1, 0, 2).reshape(128, n_ic * 128)


def _weight_img(wT, n_ic, n_oc):
    return np.ascontiguousarray(np.concatenate(
        [_strip_img(wT[:, oc * 128:(oc + 1) * 128], n_ic) for oc in range(n_oc)], axis=0))


def _rope_tables(h):
    pos = np.concatenate([np.arange(h * T, (h + 1) * T),
                          np.arange((1 - h) * T, (2 - h) * T)]).astype(np.float64)
    inv = ROPE_BASE ** (-np.arange(32, dtype=np.float64) / 32.0)
    ang = pos[:, None] * inv[None, :]
    c32 = np.cos(ang).astype(np.float32)
    s32 = np.sin(ang).astype(np.float32)
    cos_rows = np.zeros((128, L), np.float32)
    sin_rows = np.zeros((128, L), np.float32)
    for d in range(128):
        dd = d % 64
        j = dd % 32
        cos_rows[d] = c32[:, j]
        sin_rows[d] = (-1.0 if dd < 32 else 1.0) * s32[:, j]
    return (np.ascontiguousarray(cos_rows.astype(ml_dtypes.bfloat16)),
            np.ascontiguousarray(sin_rows.astype(ml_dtypes.bfloat16)))


def make_in_maps(x, wqkv, wout, w1w, w1s, w2w, w2s, h1w, h1s, h2w, h2s, w3w, w3s):
    x = np.asarray(x, np.float32)
    wqkvT = np.ascontiguousarray(np.asarray(wqkv, np.float32).T)
    woutT = np.ascontiguousarray(np.asarray(wout, np.float32).T)
    w1T = np.ascontiguousarray(np.asarray(w1w, np.float32).T)
    w2T = np.ascontiguousarray(np.asarray(w2w, np.float32).T)
    h1T = np.ascontiguousarray(np.asarray(h1w, np.float32).T)
    h2T = np.ascontiguousarray(np.asarray(h2w, np.float32).T)
    w3T = np.ascontiguousarray(np.asarray(w3w, np.float32).T)

    blocks = []
    for p in range(NPAIR):
        for r in range(3):
            blocks.append(_strip_img(wqkvT[:, r * D + p * 128: r * D + (p + 1) * 128], NIC))
    wqkv_img = np.ascontiguousarray(np.concatenate(blocks, axis=0))
    imgs = {
        "wqkv": wqkv_img,
        "wout": _weight_img(woutT, NIC, NIC),
        "w1": _weight_img(w1T, NIC, NHC),
        "w2": _weight_img(w2T, NIC, NHC),
        "h1": _weight_img(h1T, NHC, NHC),
        "h2": _weight_img(h2T, NHC, NHC),
        "w3": _weight_img(w3T, NHC, NIC),
        "s1": np.ascontiguousarray(np.asarray(w1s, np.float32).reshape(32, 128).T),
        "s2": np.ascontiguousarray(np.asarray(w2s, np.float32).reshape(32, 128).T),
        "h1s": np.ascontiguousarray(np.asarray(h1s, np.float32).reshape(32, 128).T),
        "h2s": np.ascontiguousarray(np.asarray(h2s, np.float32).reshape(32, 128).T),
        "w3s": np.ascontiguousarray(np.asarray(w3s, np.float32).reshape(8, 128).T),
    }
    tabs = [_rope_tables(0), _rope_tables(1)]
    in_maps = []
    for c in range(8):
        b, h = c // 2, c % 2
        xr = np.ascontiguousarray(
            np.concatenate([x[b, h * T:(h + 1) * T, :], x[b, (1 - h) * T:(2 - h) * T, :]], axis=0).T)
        m = {"x": xr, "costab": tabs[h][0], "sintab": tabs[h][1]}
        m.update(imgs)
        in_maps.append(m)
    return in_maps


def kernel(L0_attn_x, L0_attn_w_qkv, L0_attn_w_out,
           L0_ffn_w1_w, L0_ffn_w1_s, L0_ffn_w2_w, L0_ffn_w2_s,
           L0_ffn_h1_w, L0_ffn_h1_s, L0_ffn_h2_w, L0_ffn_h2_s,
           L0_ffn_w3_w, L0_ffn_w3_s):
    in_maps = make_in_maps(L0_attn_x, L0_attn_w_qkv, L0_attn_w_out,
                           L0_ffn_w1_w, L0_ffn_w1_s, L0_ffn_w2_w, L0_ffn_w2_s,
                           L0_ffn_h1_w, L0_ffn_h1_s, L0_ffn_h2_w, L0_ffn_h2_s,
                           L0_ffn_w3_w, L0_ffn_w3_s)
    nc, _ = build_nc(debug=False)
    res = run_bass_kernel_spmd(nc, in_maps, core_ids=list(range(8)))
    out = np.empty((B, L, D), np.float32)
    for c in range(8):
        b, h = c // 2, c % 2
        out[b, h * T:(h + 1) * T, :] = np.asarray(res.results[c]["y"]).T
    return out


# version 11
# speedup vs baseline: 1.0355x; 1.0119x over previous
import contextlib

import numpy as np
import ml_dtypes

import concourse.bass as bass
import concourse.mybir as mybir
from concourse.bass_utils import run_bass_kernel_spmd

F32 = mybir.dt.float32
BF16 = mybir.dt.bfloat16
AF = mybir.ActivationFunctionType
ALU = mybir.AluOpType
AX = mybir.AxisListType

B, L, D = 4, 4096, 1024
T = 2048
H, DH = 16, 64
HID = 4096
NIC = D // 128
NHC = HID // 128
NPAIR = H // 2
TCT = 512
NTCT = T // TCT
ROPE_BASE = 10000.0
EPS = 1e-5


class Ctr:

    def __init__(self, stack, nc, name):
        self.sem = stack.enter_context(nc.semaphore(name))
        self.n = 0


class Sched:

    ENGINES = ("tensor", "vector", "scalar", "sync", "gpsimd")

    def __init__(self):
        self.ops = {e: [] for e in self.ENGINES}

    def emit(self, engine, fn, waits=(), inc=None, inck=1):
        if inc is not None:
            inc.n += inck
            inc = (inc, inck)
        self.ops[engine].append((fn, tuple(waits), inc))

    def flush(self, nc):
        with nc.Block() as block:
            def mk(eng_name):
                def _(eng):
                    for fn, waits, inc in self.ops[eng_name]:
                        for ctr, val in waits:
                            if val > 0:
                                eng.wait_ge(ctr.sem, val)
                        inst = fn(eng)
                        if inc is not None:
                            ctr, k = inc
                            inst.then_inc(ctr.sem, k)
                return _
            for name in self.ENGINES:
                if self.ops[name]:
                    getattr(block, name)(mk(name))
        self.ops = {e: [] for e in self.ENGINES}


def build_nc(debug=False):
    stack = contextlib.ExitStack()
    nc = bass.Bass("TRN2", target_bir_lowering=False, detect_race_conditions=False)

    dx = nc.dram_tensor("x", [D, L], F32, kind="ExternalInput")
    dwqkv = nc.dram_tensor("wqkv", [NPAIR * 3 * 128, NIC * 128], F32, kind="ExternalInput")
    dwout = nc.dram_tensor("wout", [NIC * 128, NIC * 128], F32, kind="ExternalInput")
    dw1 = nc.dram_tensor("w1", [NHC * 128, NIC * 128], F32, kind="ExternalInput")
    dw2 = nc.dram_tensor("w2", [NHC * 128, NIC * 128], F32, kind="ExternalInput")
    dh1 = nc.dram_tensor("h1", [NHC * 128, NHC * 128], F32, kind="ExternalInput")
    dh2 = nc.dram_tensor("h2", [NHC * 128, NHC * 128], F32, kind="ExternalInput")
    dw3 = nc.dram_tensor("w3", [NIC * 128, NHC * 128], F32, kind="ExternalInput")
    ds1 = nc.dram_tensor("s1", [128, 32], F32, kind="ExternalInput")
    ds2 = nc.dram_tensor("s2", [128, 32], F32, kind="ExternalInput")
    dh1s = nc.dram_tensor("h1s", [128, 32], F32, kind="ExternalInput")
    dh2s = nc.dram_tensor("h2s", [128, 32], F32, kind="ExternalInput")
    dw3s = nc.dram_tensor("w3s", [128, 8], F32, kind="ExternalInput")
    dao = nc.dram_tensor("dao", [NPAIR * 128, T], BF16, kind="Internal")
    dw1c = nc.dram_tensor("w1c", [NHC * 128, NIC * 128], BF16, kind="Internal")
    dw2c = nc.dram_tensor("w2c", [NHC * 128, NIC * 128], BF16, kind="Internal")
    dh1c = nc.dram_tensor("h1c", [NHC * 128, NHC * 128], BF16, kind="Internal")
    dh2c = nc.dram_tensor("h2c", [NHC * 128, NHC * 128], BF16, kind="Internal")
    dw3c = nc.dram_tensor("w3c", [NIC * 128, NHC * 128], BF16, kind="Internal")
    dcos = nc.dram_tensor("costab", [128, L], BF16, kind="ExternalInput")
    dsin = nc.dram_tensor("sintab", [128, L], BF16, kind="ExternalInput")
    dy = nc.dram_tensor("y", [D, T], F32, kind="ExternalOutput")
    dbg = {}
    if debug:
        dbg["xn"] = nc.dram_tensor("dbg_xn", [D, L], F32, kind="ExternalOutput")
        dbg["q"] = nc.dram_tensor("dbg_q", [128, T], F32, kind="ExternalOutput")
        dbg["k"] = nc.dram_tensor("dbg_k", [128, L], F32, kind="ExternalOutput")
        dbg["v"] = nc.dram_tensor("dbg_v", [128, 32 * 130], F32, kind="ExternalOutput")
        dbg["ao"] = nc.dram_tensor("dbg_ao", [D, T], F32, kind="ExternalOutput")
        dbg["ap"] = nc.dram_tensor("dbg_ap", [D, T], F32, kind="ExternalOutput")

    cX = [Ctr(stack, nc, f"cX{j}") for j in range(2)]
    cSW = Ctr(stack, nc, "cSW")
    cTAB = Ctr(stack, nc, "cTAB")
    cWQ = [Ctr(stack, nc, f"cWQ{j}") for j in range(2)]
    cWO = [Ctr(stack, nc, f"cWO{j}") for j in range(2)]
    cWH = [Ctr(stack, nc, f"cWH{j}") for j in range(3)]
    cXR = [Ctr(stack, nc, f"cXR{j}") for j in range(2)]
    cWB = [Ctr(stack, nc, f"cWB{j}") for j in range(2)]
    cST = [Ctr(stack, nc, f"cST{j}") for j in range(2)]
    cQS = Ctr(stack, nc, "cQS")
    cU = Ctr(stack, nc, "cU")
    cAO = Ctr(stack, nc, "cAO")
    cCS = [Ctr(stack, nc, f"cCS{j}") for j in range(2)]
    cDBG = Ctr(stack, nc, "cDBG")
    cTE = Ctr(stack, nc, "cTE")
    cSC = Ctr(stack, nc, "cSC")
    cVE = Ctr(stack, nc, "cVE")

    sch = Sched()
    E = sch.emit

    xn_own = stack.enter_context(nc.sbuf_tensor("xn_own", [128, NIC * T], BF16))

    attn_proj = stack.enter_context(nc.sbuf_tensor("attn_proj", [128, NIC * T], BF16))
    s1_sb = stack.enter_context(nc.sbuf_tensor("s1_sb", [128, 32], F32))
    s2_sb = stack.enter_context(nc.sbuf_tensor("s2_sb", [128, 32], F32))
    h1s_sb = stack.enter_context(nc.sbuf_tensor("h1s_sb", [128, 32], F32))
    h2s_sb = stack.enter_context(nc.sbuf_tensor("h2s_sb", [128, 32], F32))
    w3s_sb = stack.enter_context(nc.sbuf_tensor("w3s_sb", [128, 8], F32))
    ones_f32 = stack.enter_context(nc.sbuf_tensor("ones_f32", [65, 128], F32))
    onescol_f32 = stack.enter_context(nc.sbuf_tensor("onescol_f32", [128, 1], F32))
    onescol_bf = stack.enter_context(nc.sbuf_tensor("onescol_bf", [128, 1], BF16))

    E("vector", lambda v: v.memset(ones_f32[:, :], 1.0), inc=cVE)
    E("vector", lambda v: v.memset(onescol_f32[:, :], 1.0), inc=cVE)
    E("vector", lambda v: v.memset(onescol_bf[:, :], 1.0), inc=cVE)
    const_ve = cVE.n
    for dsrc, dst in ((ds1, s1_sb), (ds2, s2_sb), (dh1s, h1s_sb), (dh2s, h2s_sb), (dw3s, w3s_sb)):
        E("gpsimd", lambda g, s=dsrc, d=dst: g.dma_start(out=d[:, :], in_=s[:, :]), inc=cTAB, inck=16)
    scales_wt = cTAB.n

    ap_done_box = [0]

    with contextlib.ExitStack() as sA:
        xn_prt = sA.enter_context(nc.sbuf_tensor("xn_prt", [128, NIC * T], BF16))

        with contextlib.ExitStack() as s0:
            xch = [s0.enter_context(nc.sbuf_tensor(f"xch{j}", [128, L], F32)) for j in range(2)]
            xsq = [s0.enter_context(nc.sbuf_tensor(f"xsq{j}", [128, 512], F32)) for j in range(2)]
            es_sb = s0.enter_context(nc.sbuf_tensor("es_sb", [33, L], F32))
            mean_bc = s0.enter_context(nc.sbuf_tensor("mean_bc", [128, L], BF16))
            rstd_bc = s0.enter_context(nc.sbuf_tensor("rstd_bc", [128, L], BF16))
            t1 = s0.enter_context(nc.sbuf_tensor("t1", [128, T // 2], F32))
            v1 = t1[:, 0:512]
            v2 = t1[:, 512:1024]

            with nc.psum_tensor("stats_ps", [33, L], F32) as stats_ps:
                te_ic = []
                sc_ic = []
                for ic in range(NIC):
                    bj = ic % 2
                    lw = []
                    if ic >= 2:
                        lw = [(cTE, te_ic[ic - 2]), (cSC, sc_ic[ic - 2])]
                    E("sync", lambda s, ic=ic, bj=bj: s.dma_start(
                        out=xch[bj][:, :], in_=dx[ic * 128:(ic + 1) * 128, :]),
                        waits=lw, inc=cX[bj], inck=16)
                    ldc, ld = cX[bj], cX[bj].n
                    for tc in range(8):
                        sq = tc % 2
                        sqw = [(ldc, ld)]
                        if ic > 0 or tc >= 2:
                            sqw.append((cTE, cTE.n))
                        E("scalar", lambda s, bj=bj, tc=tc, sq=sq: s.activation(
                            out=xsq[sq][:, :], in_=xch[bj][:, tc * 512:(tc + 1) * 512], func=AF.Square),
                            waits=sqw, inc=cSC)
                        scm = cSC.n
                        E("tensor", lambda t, ic=ic, bj=bj, tc=tc: t.matmul(
                            stats_ps[0:1, tc * 512:(tc + 1) * 512], onescol_f32[:, :],
                            xch[bj][:, tc * 512:(tc + 1) * 512],
                            start=(ic == 0), stop=(ic == NIC - 1)),
                            waits=([(ldc, ld), (cVE, const_ve)] if ic == 0 else [(ldc, ld)]) if tc == 0 else [])
                        E("tensor", lambda t, ic=ic, tc=tc, sq=sq: t.matmul(
                            stats_ps[32:33, tc * 512:(tc + 1) * 512], onescol_f32[:, :],
                            xsq[sq][:, :],
                            start=(ic == 0), stop=(ic == NIC - 1)),
                            waits=[(cSC, scm)],
                            inc=cTE)
                    te_ic.append(cTE.n)
                    sc_ic.append(cSC.n)
                E("vector", lambda v: v.tensor_single_scalar(
                    es_sb[0:1, :], stats_ps[0:1, :], 1.0 / D, ALU.mult),
                    waits=[(cTE, cTE.n), (cVE, const_ve)], inc=cVE)
                E("vector", lambda v: v.tensor_single_scalar(
                    es_sb[32:33, :], stats_ps[32:33, :], 1.0 / D, ALU.mult), inc=cVE)
                sch.flush(nc)
            p0a_te, p0a_sc, p0a_ve = cTE.n, cSC.n, cVE.n

            with nc.psum_tensor("bc_ps", [128, 2048], F32) as bc_ps:
                vtc = []
                for tc in range(8):
                    par = tc % 2
                    mslot = bc_ps[:, (2 * par) * 512:(2 * par + 1) * 512]
                    eslot = bc_ps[:, (2 * par + 1) * 512:(2 * par + 2) * 512]
                    w0 = [(cVE, p0a_ve)]
                    if tc >= 2:
                        w0.append((cVE, vtc[tc - 2]))
                    sl = slice(tc * 512, (tc + 1) * 512)
                    E("tensor", lambda t, m=mslot, sl=sl: t.matmul(
                        m, ones_f32[0:1, :], es_sb[0:1, sl], start=True, stop=True), waits=w0)
                    E("tensor", lambda t, e=eslot, sl=sl: t.matmul(
                        e, ones_f32[32:33, :], es_sb[32:33, sl], start=True, stop=True), inc=cTE)
                    tm = cTE.n
                    E("vector", lambda v, m=mslot, sl=sl: v.tensor_copy(mean_bc[:, sl], m),
                        waits=[(cTE, tm)], inc=cVE)
                    E("vector", lambda v, m=mslot: v.tensor_copy(v1, m), inc=cVE)
                    E("vector", lambda v, e=eslot: v.tensor_tensor(v2, v1, v1, ALU.mult), inc=cVE)
                    E("vector", lambda v, e=eslot: v.tensor_tensor(v2, e, v2, ALU.subtract), inc=cVE)
                    E("vector", lambda v: v.tensor_single_scalar(v2, v2, EPS, ALU.add), inc=cVE)
                    vsq = cVE.n
                    E("scalar", lambda s: s.activation(out=v1, in_=v2, func=AF.Sqrt),
                        waits=[(cVE, vsq), (cSC, p0a_sc)], inc=cSC)
                    E("vector", lambda v: v.reciprocal(v2, v1),
                        waits=[(cSC, cSC.n)], inc=cVE)
                    E("vector", lambda v, sl=sl: v.tensor_copy(rstd_bc[:, sl], v2), inc=cVE)
                    vtc.append(cVE.n)
                sch.flush(nc)
            p0b_ve = cVE.n

            va = []
            for ic in range(NIC):
                bj = ic % 2
                lw = [(cVE, p0b_ve), (cTE, cTE.n), (cSC, cSC.n)]
                if ic >= 2:
                    lw.append((cVE, va[ic - 2]))
                E("sync", lambda s, ic=ic, bj=bj: s.dma_start(
                    out=xch[bj][:, :], in_=dx[ic * 128:(ic + 1) * 128, :]),
                    waits=lw, inc=cX[bj], inck=16)
                ldc, ld = cX[bj], cX[bj].n
                for quarter in range(4):
                    dst = xn_own if quarter < 2 else xn_prt
                    hs = slice(quarter * (T // 2), (quarter + 1) * (T // 2))
                    ds = slice(ic * T + (quarter % 2) * (T // 2), ic * T + (quarter % 2 + 1) * (T // 2))
                    E("vector", lambda v, bj=bj, hs=hs: v.tensor_tensor(
                        t1[:, :], xch[bj][:, hs], mean_bc[:, hs], ALU.subtract),
                        waits=[(ldc, ld)] if quarter == 0 else [], inc=cVE)
                    E("vector", lambda v, hs=hs, ds=ds, dst=dst: v.tensor_tensor(
                        dst[:, ds], t1[:, :], rstd_bc[:, hs], ALU.mult), inc=cVE)
                va.append(cVE.n)
            xn_done = cVE.n
            if debug:
                for ic in range(NIC):
                    E("gpsimd", lambda g, ic=ic: g.dma_start(
                        out=dbg["xn"][ic * 128:(ic + 1) * 128, 0:T], in_=xn_own[:, ic * T:(ic + 1) * T]),
                        waits=[(cVE, xn_done)], inc=cDBG, inck=16)
                    E("gpsimd", lambda g, ic=ic: g.dma_start(
                        out=dbg["xn"][ic * 128:(ic + 1) * 128, T:L], in_=xn_prt[:, ic * T:(ic + 1) * T]),
                        inc=cDBG, inck=16)
            sch.flush(nc)

        with contextlib.ExitStack() as s1:
            kp_A = [s1.enter_context(nc.sbuf_tensor(f"kpA{j}", [65, L], BF16)) for j in range(2)]
            kp_B = [s1.enter_context(nc.sbuf_tensor(f"kpB{j}", [65, L], BF16)) for j in range(2)]
            qp_A = [s1.enter_context(nc.sbuf_tensor(f"qpA{j}", [65, T], BF16)) for j in range(2)]
            qp_B = [s1.enter_context(nc.sbuf_tensor(f"qpB{j}", [65, T], BF16)) for j in range(2)]
            vsb2 = [s1.enter_context(nc.sbuf_tensor(f"vsb{j}", [128, 32 * 130], BF16)) for j in range(2)]
            pt = s1.enter_context(nc.sbuf_tensor("pt", [128, 4096], BF16))
            wq_sb = s1.enter_context(nc.sbuf_tensor("wq", [128, 1024], BF16))
            wk_sb = s1.enter_context(nc.sbuf_tensor("wk", [128, 1024], BF16))
            wv_sb = s1.enter_context(nc.sbuf_tensor("wv", [128, 1024], BF16))
            qraw = [s1.enter_context(nc.sbuf_tensor(f"qraw{j}", [128, 512], BF16)) for j in range(2)]
            qswp = [s1.enter_context(nc.sbuf_tensor(f"qswp{j}", [128, 512], BF16)) for j in range(2)]
            rt1 = [s1.enter_context(nc.sbuf_tensor(f"rt1_{j}", [128, 512], F32)) for j in range(2)]
            cs_sb = s1.enter_context(nc.sbuf_tensor("cs_sb", [128, 2048], BF16))
            den_sb = s1.enter_context(nc.sbuf_tensor("den_sb", [65, 1024], F32))
            bc_sb = s1.enter_context(nc.sbuf_tensor("bc_sb", [128, 1024], F32))
            bnd = s1.enter_context(nc.sbuf_tensor("bnd", [33, 8], F32))
            c8col = s1.enter_context(nc.sbuf_tensor("c8col", [128, 2], F32))
            MA = s1.enter_context(nc.sbuf_tensor("MA", [128, 1024], BF16))
            MB = s1.enter_context(nc.sbuf_tensor("MB", [128, 1024], BF16))
            ao_scr = s1.enter_context(nc.sbuf_tensor("ao_scr", [128, 1024], BF16))

            for j in range(2):
                E("vector", lambda v, j=j: v.memset(vsb2[j][:, :], 0.0), inc=cVE)
                vr = vsb2[j].rearrange("p (vt c) -> p vt c", c=130)
                E("vector", lambda v, vr=vr: v.memset(vr[:, :, 64:65], 1.0), inc=cVE)
                E("vector", lambda v, j=j: v.memset(kp_A[j][64:65, :], 1.0), inc=cVE)
                E("vector", lambda v, j=j: v.memset(kp_B[j][64:65, :], 1.0), inc=cVE)
            E("vector", lambda v: v.memset(bc_sb[:, :], 0.0), inc=cVE)
            E("vector", lambda v: v.memset(den_sb[:, :], 1.0), inc=cVE)

            with (
                nc.psum_tensor("pA", [128, 2048], F32) as pA,
                nc.psum_tensor("pB", [128, 2048], F32) as pB,
            ):
                def wload(g, dstT, row0):
                    return g.dma_start(out=dstT[:, :], in_=dwqkv[row0 * 128:(row0 + 1) * 128, :])

                prev_norm = 0
                swp_mark = [0, 0]
                qs_mark = [0, 0]
                rope_tile_idx = [0]
                rope_done = {}
                last_swp_box = [0]

                def gen_rope(p):
                    st = p % 2
                    pw = [(cTE, cTE.n), (cSC, cSC.n)]
                    E("gpsimd", lambda g, p=p: wload(g, wq_sb, p * 3 + 0), waits=pw, inc=cWQ[0], inck=16)
                    E("gpsimd", lambda g, p=p: wload(g, wk_sb, p * 3 + 1), inc=cWQ[0], inck=16)
                    E("gpsimd", lambda g, p=p: wload(g, wv_sb, p * 3 + 2), inc=cWQ[0], inck=16)
                    wt_ready = cWQ[0].n
                    info = {"wt": wt_ready}
                    ve_tile = []
                    for tile in range(12):
                        if tile < 4:
                            wsb, dstA, dstB, j = wq_sb, qp_A[st], qp_B[st], tile
                            xs = lambda ic, j=j: xn_own[:, ic * T + j * 512: ic * T + (j + 1) * 512]
                        else:
                            j = tile - 4
                            wsb, dstA, dstB = wk_sb, kp_A[st], kp_B[st]
                            xs = lambda ic, j=j: (xn_own if j < 4 else xn_prt)[:, ic * T + (j % 4) * 512: ic * T + (j % 4 + 1) * 512]
                        ps = pB[:, (tile % 2) * 512:(tile % 2 + 1) * 512]
                        gpar = rope_tile_idx[0] % 2
                        gw = [(cVE, ve_tile[tile - 2])] if tile >= 2 else [(cVE, cVE.n)]
                        E("sync", lambda s, j=j, gpar=gpar: s.dma_start(
                            out=cs_sb[:, gpar * 1024:gpar * 1024 + 512],
                            in_=dcos[:, j * 512:(j + 1) * 512]),
                            waits=gw, inc=cCS[gpar], inck=16)
                        E("sync", lambda s, j=j, gpar=gpar: s.dma_start(
                            out=cs_sb[:, gpar * 1024 + 512:gpar * 1024 + 1024],
                            in_=dsin[:, j * 512:(j + 1) * 512]),
                            inc=cCS[gpar], inck=16)
                        cs_ready = cCS[gpar].n
                        rope_tile_idx[0] += 1
                        w0 = [(cVE, xn_done), (cWQ[0], wt_ready)]
                        if tile >= 2:
                            w0.append((cVE, ve_tile[tile - 2]))
                        else:
                            w0.append((cVE, prev_norm))
                        for ic in range(NIC):
                            E("tensor", lambda t, ic=ic, ps=ps, wsb=wsb, xs=xs: t.matmul(
                                ps, wsb[:, ic * 128:(ic + 1) * 128], xs(ic),
                                start=(ic == 0), stop=(ic == NIC - 1)),
                                waits=w0 if ic == 0 else [],
                                inc=cTE if ic == NIC - 1 else None)
                        tm = cTE.n
                        qr, qs, r1 = qraw[gpar], qswp[gpar], rt1[gpar]
                        E("scalar", lambda s, ps=ps, qr=qr: s.activation(
                            out=qr[:, :], in_=ps, func=AF.Copy),
                            waits=[(cTE, tm), (cSW, swp_mark[gpar]), (cQS, qs_mark[gpar]), (cSC, cSC.n)], inc=cSC)
                        vcp = cSC.n
                        for (a, b2) in ((0, 32), (32, 0), (64, 96), (96, 64)):
                            E("sync", lambda s, a=a, b2=b2, qr=qr, qs=qs: s.dma_start(
                                out=qs[a:a + 32, :], in_=qr[b2:b2 + 32, :]),
                                waits=[(cSC, vcp)] if a == 0 else [], inc=cSW, inck=16)
                        swp_mark[gpar] = last_swp = cSW.n
                        E("vector", lambda v, gpar=gpar, r1=r1, qr=qr: v.tensor_tensor(
                            r1[:, :], qr[:, :], cs_sb[:, gpar * 1024:gpar * 1024 + 512], ALU.mult),
                            waits=[(cCS[gpar], cs_ready), (cSC, vcp)], inc=cVE)
                        E("vector", lambda v, gpar=gpar, qs=qs: v.tensor_tensor(
                            qs[:, :], qs[:, :], cs_sb[:, gpar * 1024 + 512:gpar * 1024 + 1024], ALU.mult),
                            waits=[(cSW, last_swp)], inc=cVE)
                        E("vector", lambda v, r1=r1, qs=qs: v.tensor_tensor(
                            r1[:, :], r1[:, :], qs[:, :], ALU.add), inc=cVE)
                        vq = cVE.n
                        sl = slice(j * 512, (j + 1) * 512)
                        E("gpsimd", lambda g, sl=sl, dstA=dstA, r1=r1: g.dma_start(
                            out=dstA[0:64, sl], in_=r1[0:64, :]),
                            waits=[(cVE, vq)], inc=cQS, inck=16)
                        E("gpsimd", lambda g, sl=sl, dstB=dstB, r1=r1: g.dma_start(
                            out=dstB[0:64, sl], in_=r1[64:128, :]),
                            inc=cQS, inck=16)
                        qs_mark[gpar] = cQS.n
                        ve_tile.append(cVE.n)
                        yield
                    info["dmas"] = cQS.n
                    info["ve"] = cVE.n
                    rope_done[p] = info


                def gen_vb(pp):
                    st = pp % 2
                    kpA, kpB, qpA, qpB, vsb = kp_A[st], kp_B[st], qp_A[st], qp_B[st], vsb2[st]
                    qk_dmas = rope_done[pp]["dmas"]
                    wt_ready = rope_done[pp]["wt"]
                    sc_rope = cSC.n
                    v_marks = []
                    for vt in range(32):
                        ps = pB[:, (vt % 2) * 512: (vt % 2) * 512 + 128]
                        xsb = xn_own if vt < 16 else xn_prt
                        tcol = (vt % 16) * 128
                        w0 = [(cSC, sc_rope), (cVE, prev_norm), (cWQ[0], wt_ready), (cQS, qk_dmas)]
                        if vt >= 2:
                            w0 = [(cVE, v_marks[vt - 2])]
                        for ic in range(NIC):
                            E("tensor", lambda t, ic=ic, ps=ps, xsb=xsb, tcol=tcol: t.matmul(
                                ps, xsb[:, ic * T + tcol: ic * T + tcol + 128],
                                wv_sb[:, ic * 128:(ic + 1) * 128],
                                start=(ic == 0), stop=(ic == NIC - 1)),
                                waits=w0 if ic == 0 else [],
                                inc=cTE if ic == NIC - 1 else None)
                        tm = cTE.n
                        E("vector", lambda v, ps=ps, vt=vt, vsb=vsb: v.tensor_copy(
                            vsb[:, vt * 130:vt * 130 + 64], ps[:, 0:64]),
                            waits=[(cTE, tm)], inc=cVE)
                        E("vector", lambda v, ps=ps, vt=vt, vsb=vsb: v.tensor_copy(
                            vsb[:, vt * 130 + 65:vt * 130 + 129], ps[:, 64:128]), inc=cVE)
                        v_marks.append(cVE.n)
                        yield
                    v_done = cVE.n

                    E("vector", lambda v: v.memset(bnd[:, :], 0.0),
                        waits=[(cTE, cTE.n)], inc=cVE)
                    bnd_ve0 = cVE.n
                    sqm = [0, 0]
                    mmm = [cTE.n, cTE.n]
                    rdm = []
                    ti = 0
                    for srcs, col in (((qpA, qpB), 0), ((kpA, kpB), 1)):
                        n_tiles = 4 if col == 0 else 8
                        for j in range(n_tiles):
                            for hh, srch in enumerate(srcs):
                                par = ti % 2
                                sqb = MA[0:64, par * 512:(par + 1) * 512]
                                E("scalar", lambda s, srch=srch, j=j, sqb=sqb: s.activation(
                                    out=sqb, in_=srch[0:64, j * 512:(j + 1) * 512], func=AF.Square),
                                    waits=[(cQS, qk_dmas), (cTE, mmm[par]), (cSC, cSC.n)], inc=cSC)
                                sqm[par] = cSC.n
                                row = hh * 32
                                pslot = pB[row:row + 1, par * 512:(par + 1) * 512]
                                wmm = [(cSC, sqm[par])]
                                if ti >= 2:
                                    wmm.append((cVE, rdm[ti - 2]))
                                else:
                                    wmm.append((cVE, bnd_ve0))
                                E("tensor", lambda t, pslot=pslot, sqb=sqb: t.matmul(
                                    pslot, onescol_bf[0:64, :], sqb, start=True, stop=True),
                                    waits=wmm, inc=cTE)
                                mmm[par] = cTE.n
                                E("vector", lambda v, row=row, pslot=pslot: v.tensor_reduce(
                                    bnd[row:row + 1, 5:6], pslot, AX.X, ALU.max),
                                    waits=[(cTE, mmm[par])], inc=cVE)
                                E("vector", lambda v, col=col, row=row: v.tensor_tensor(
                                    bnd[row:row + 1, col:col + 1], bnd[row:row + 1, col:col + 1],
                                    bnd[row:row + 1, 5:6], ALU.max), inc=cVE)
                                rdm.append(cVE.n)
                                ti += 1
                            yield
                    E("vector", lambda v: v.tensor_tensor(
                        bnd[:, 2:3], bnd[:, 0:1], bnd[:, 1:2], ALU.mult), inc=cVE)
                    E("scalar", lambda s: s.activation(
                        out=bnd[:, 3:4], in_=bnd[:, 2:3], func=AF.Sqrt),
                        waits=[(cVE, cVE.n)], inc=cSC)
                    E("vector", lambda v: v.tensor_scalar(
                        bnd[:, 4:5], bnd[:, 3:4], -0.1275 / 8.0, None, ALU.mult),
                        waits=[(cSC, cSC.n)], inc=cVE)
                    E("vector", lambda v: v.tensor_scalar(
                        bnd[:, 6:7], bnd[:, 3:4], -0.1275 * 8.0, None, ALU.mult), inc=cVE)
                    bnd_ve = cVE.n
                    E("tensor", lambda t: t.matmul(
                        pB[0:128, 0:1], ones_f32[0:1, :], bnd[0:1, 4:5], start=True, stop=True),
                        waits=[(cVE, bnd_ve)])
                    E("tensor", lambda t: t.matmul(
                        pB[0:128, 512:513], ones_f32[32:33, :], bnd[32:33, 4:5], start=True, stop=True),
                        inc=cTE)
                    E("vector", lambda v: v.tensor_copy(c8col[:, 0:1], pB[0:128, 0:1]),
                        waits=[(cTE, cTE.n)], inc=cVE)
                    E("vector", lambda v: v.tensor_copy(c8col[:, 1:2], pB[0:128, 512:513]), inc=cVE)
                    c8_ve = cVE.n


                    rope_done[pp]["v_done"] = v_done
                    rope_done[pp]["c8_ve"] = cVE.n

                for p in range(NPAIR):
                    st = p % 2
                    kpA, kpB, qpA, qpB, vsb = kp_A[st], kp_B[st], qp_A[st], qp_B[st], vsb2[st]
                    if p == 0:
                        for _ in gen_rope(0):
                            pass
                        for _ in gen_vb(0):
                            pass
                    v_done = rope_done[p]["v_done"]
                    c8_ve = rope_done[p]["c8_ve"]

                    if debug and p == 0:
                        E("gpsimd", lambda g, qpA=qpA: g.dma_start(out=dbg["q"][0:64, :], in_=qpA[0:64, :]),
                            waits=[(cVE, v_done)], inc=cDBG, inck=16)
                        E("gpsimd", lambda g, qpB=qpB: g.dma_start(out=dbg["q"][64:128, :], in_=qpB[0:64, :]), inc=cDBG, inck=16)
                        E("gpsimd", lambda g, kpA=kpA: g.dma_start(out=dbg["k"][0:64, :], in_=kpA[0:64, :]), inc=cDBG, inck=16)
                        E("gpsimd", lambda g, kpB=kpB: g.dma_start(out=dbg["k"][64:128, :], in_=kpB[0:64, :]), inc=cDBG, inck=16)
                        E("gpsimd", lambda g, vsb=vsb: g.dma_start(out=dbg["v"][:, :], in_=vsb[:, :]), inc=cDBG, inck=16)

                    def passA(qc, entry_sc):
                        qoff = qc * 1024
                        maxm = []
                        for kt in range(32):
                            pst = (pA, pB)[kt % 2]
                            for head, (kph, qph, Mh) in enumerate(((kpA, qpA, MA), (kpB, qpB, MB))):
                                pslice = pst[:, head * 1024:(head + 1) * 1024]
                                for half in range(2):
                                    w0 = []
                                    if half == 0 and head == 0:
                                        if kt >= 2:
                                            w0 = [(cVE, maxm[kt - 2])]
                                        else:
                                            w0 = [(cVE, prev_norm), (cVE, v_done)] if qc == 0 else [(cVE, prev_norm)]
                                    E("tensor", lambda t, kt=kt, half=half, qoff=qoff,
                                        kph=kph, qph=qph, pslice=pslice: t.matmul(
                                        pslice[:, half * 512:(half + 1) * 512],
                                        kph[0:64, kt * 128:(kt + 1) * 128],
                                        qph[0:64, qoff + half * 512: qoff + (half + 1) * 512],
                                        start=True, stop=True),
                                        waits=w0, inc=cTE if half == 1 else None)
                                tmh = cTE.n
                                if kt == 0:
                                    E("vector", lambda v, Mh=Mh, pslice=pslice: v.tensor_copy(
                                        Mh[:, :], pslice),
                                        waits=[(cTE, tmh), (cSC, entry_sc)], inc=cVE)
                                else:
                                    E("vector", lambda v, Mh=Mh, pslice=pslice: v.tensor_tensor(
                                        Mh[:, :], pslice, Mh[:, :], ALU.max),
                                        waits=[(cTE, tmh)], inc=cVE)
                            maxm.append(cVE.n)
                        return cVE.n

                    def fold(qc, maxdone):
                        qoff = qc * 1024
                        for head, (Mh, qph, row) in enumerate(((MA, qpA, 0), (MB, qpB, 32))):
                            E("scalar", lambda s, Mh=Mh, head=head: s.activation(
                                out=Mh[:, :], in_=Mh[:, :], func=AF.Exp,
                                bias=c8col[:, head:head + 1], scale=0.125 / 8.0),
                                waits=[(cVE, maxdone), (cTE, cTE.n), (cSC, cSC.n)], inc=cSC)
                            scm = cSC.n
                            for half in range(2):
                                E("tensor", lambda t, row=row, Mh=Mh, half=half: t.matmul(
                                    pA[row:row + 1, half * 512:(half + 1) * 512],
                                    onescol_bf[:, :], Mh[:, half * 512:(half + 1) * 512],
                                    start=True, stop=True),
                                    waits=[(cSC, scm)] if half == 0 else [],
                                    inc=cTE if half == 1 else None)
                            E("vector", lambda v, row=row: v.tensor_single_scalar(
                                bc_sb[row:row + 1, 0:1024], pA[row:row + 1, 0:1024], 1e-38, ALU.add),
                                waits=[(cTE, cTE.n)], inc=cVE)
                            E("scalar", lambda s, row=row: s.activation(
                                out=bc_sb[row:row + 1, 0:1024], in_=bc_sb[row:row + 1, 0:1024],
                                func=AF.Ln),
                                waits=[(cVE, cVE.n)], inc=cSC)
                        E("vector", lambda v: v.tensor_scalar(
                            bc_sb[0:33, 0:1024], bc_sb[0:33, 0:1024], -64.0, bnd[0:33, 6:7], ALU.mult, ALU.add),
                            waits=[(cSC, cSC.n)], inc=cVE)
                        lnm = cVE.n
                        E("gpsimd", lambda g, qoff=qoff, qph=qpA: g.dma_start(
                            out=qph[64:65, qoff:qoff + 1024], in_=bc_sb[0:1, 0:1024]),
                            waits=[(cVE, lnm)], inc=cU, inck=16)
                        E("gpsimd", lambda g, qoff=qoff, qph=qpB: g.dma_start(
                            out=qph[64:65, qoff:qoff + 1024], in_=bc_sb[32:33, 0:1024]),
                            inc=cU, inck=16)
                        return cU.n, cSC.n

                    def passB0(qc, u_dma, lnsc):
                        qoff = qc * 1024
                        expA, expB = [], []
                        for kt in range(32):
                            par = kt % 2
                            for half in range(2):
                                w0 = []
                                if half == 0:
                                    w0 = [(cSC, expA[kt - 1])] if kt >= 1 else [(cU, u_dma), (cSC, lnsc), (cVE, v_done)]
                                E("tensor", lambda t, kt=kt, half=half, qoff=qoff, kpA=kpA, qpA=qpA: t.matmul(
                                    pA[:, half * 512:(half + 1) * 512],
                                    kpA[0:65, kt * 128:(kt + 1) * 128],
                                    qpA[0:65, qoff + half * 512: qoff + (half + 1) * 512],
                                    start=True, stop=True),
                                    waits=w0, inc=cTE if half == 1 else None)
                            qkA = cTE.n
                            for half in range(2):
                                w0 = []
                                if half == 0 and kt >= 1:
                                    w0 = [(cSC, expB[kt - 1])]
                                E("tensor", lambda t, kt=kt, half=half, qoff=qoff, kpB=kpB, qpB=qpB: t.matmul(
                                    pB[:, half * 512:(half + 1) * 512],
                                    kpB[0:65, kt * 128:(kt + 1) * 128],
                                    qpB[0:65, qoff + half * 512: qoff + (half + 1) * 512],
                                    start=True, stop=True),
                                    waits=w0, inc=cTE if half == 1 else None)
                            qkB = cTE.n
                            if kt > 0:
                                ktm = kt - 1
                                parm = ktm % 2
                                for half in range(2):
                                    E("tensor", lambda t, ktm=ktm, parm=parm, half=half, vsb=vsb: t.matmul(
                                        pA[0:65, 1024 + half * 512:1024 + (half + 1) * 512],
                                        vsb[:, ktm * 130:ktm * 130 + 65],
                                        pt[:, parm * 1024 + half * 512: parm * 1024 + (half + 1) * 512],
                                        start=(ktm == 0), stop=False))
                                    E("tensor", lambda t, ktm=ktm, parm=parm, half=half, vsb=vsb: t.matmul(
                                        pB[64:128, 1024 + half * 512:1024 + (half + 1) * 512],
                                        vsb[:, ktm * 130 + 65:ktm * 130 + 129],
                                        pt[:, 2048 + parm * 1024 + half * 512: 2048 + parm * 1024 + (half + 1) * 512],
                                        start=(ktm == 0), stop=False, tile_position=(0, 64)))
                                    E("tensor", lambda t, ktm=ktm, parm=parm, half=half: t.matmul(
                                        pB[0:1, 1024 + half * 512:1024 + (half + 1) * 512],
                                        onescol_bf[:, :],
                                        pt[:, 2048 + parm * 1024 + half * 512: 2048 + parm * 1024 + (half + 1) * 512],
                                        start=(ktm == 0), stop=False))
                            E("scalar", lambda s, par=par: s.activation(
                                out=pt[:, par * 1024:(par + 1) * 1024], in_=pA[:, 0:1024],
                                func=AF.Exp, scale=0.125),
                                waits=[(cTE, qkA)], inc=cSC)
                            expA.append(cSC.n)
                            E("scalar", lambda s, par=par: s.activation(
                                out=pt[:, 2048 + par * 1024:2048 + (par + 1) * 1024], in_=pB[:, 0:1024],
                                func=AF.Exp, scale=0.125),
                                waits=[(cTE, qkB)], inc=cSC)
                            expB.append(cSC.n)
                        _pv_tail(vsb, expA, expB)
                        return expA[31], expB[31], cTE.n

                    def _pv_tail(vsb, expA, expB):
                        ktm, parm = 31, 1
                        for half in range(2):
                            E("tensor", lambda t, half=half, parm=parm, ktm=ktm, vsb=vsb: t.matmul(
                                pA[0:65, 1024 + half * 512:1024 + (half + 1) * 512],
                                vsb[:, ktm * 130:ktm * 130 + 65],
                                pt[:, parm * 1024 + half * 512: parm * 1024 + (half + 1) * 512],
                                start=False, stop=True),
                                waits=[(cSC, expA[31])] if half == 0 else [])
                            E("tensor", lambda t, half=half, parm=parm, ktm=ktm, vsb=vsb: t.matmul(
                                pB[64:128, 1024 + half * 512:1024 + (half + 1) * 512],
                                vsb[:, ktm * 130 + 65:ktm * 130 + 129],
                                pt[:, 2048 + parm * 1024 + half * 512: 2048 + parm * 1024 + (half + 1) * 512],
                                start=False, stop=True, tile_position=(0, 64)),
                                waits=[(cSC, expB[31])] if half == 0 else [])
                            E("tensor", lambda t, half=half, parm=parm: t.matmul(
                                pB[0:1, 1024 + half * 512:1024 + (half + 1) * 512],
                                onescol_bf[:, :],
                                pt[:, 2048 + parm * 1024 + half * 512: 2048 + parm * 1024 + (half + 1) * 512],
                                start=False, stop=True),
                                inc=cTE if half == 1 else None)

                    def gen_passB1(qc, u_dma, lnsc, entry_ve):
                        qoff = qc * 1024
                        exps = [[], []]
                        for i in range(64):
                            head, kt = i % 2, i // 2
                            kph, qph = ((kpA, qpA), (kpB, qpB))[head]
                            par = kt % 2
                            if i == 0:
                                w0 = [(cU, u_dma), (cSC, lnsc), (cVE, entry_ve)]
                            else:
                                w0 = [(cSC, exps[1 - head][-1])]
                            for half in range(2):
                                E("tensor", lambda t, kt=kt, half=half, qoff=qoff, kph=kph, qph=qph: t.matmul(
                                    pA[:, half * 512:(half + 1) * 512],
                                    kph[0:65, kt * 128:(kt + 1) * 128],
                                    qph[0:65, qoff + half * 512: qoff + (half + 1) * 512],
                                    start=True, stop=True),
                                    waits=w0 if half == 0 else [],
                                    inc=cTE if half == 1 else None)
                            qk = cTE.n
                            if kt > 0:
                                ktm = kt - 1
                                parm = ktm % 2
                                for half in range(2):
                                    if head == 0:
                                        E("tensor", lambda t, ktm=ktm, parm=parm, half=half, vsb=vsb: t.matmul(
                                            pA[0:65, 1024 + half * 512:1024 + (half + 1) * 512],
                                            vsb[:, ktm * 130:ktm * 130 + 65],
                                            pt[:, parm * 1024 + half * 512: parm * 1024 + (half + 1) * 512],
                                            start=(ktm == 0), stop=False))
                                    else:
                                        E("tensor", lambda t, ktm=ktm, parm=parm, half=half, vsb=vsb: t.matmul(
                                            pB[64:128, 1024 + half * 512:1024 + (half + 1) * 512],
                                            vsb[:, ktm * 130 + 65:ktm * 130 + 129],
                                            pt[:, 2048 + parm * 1024 + half * 512: 2048 + parm * 1024 + (half + 1) * 512],
                                            start=(ktm == 0), stop=False, tile_position=(0, 64)))
                                        E("tensor", lambda t, ktm=ktm, parm=parm, half=half: t.matmul(
                                            pB[0:1, 1024 + half * 512:1024 + (half + 1) * 512],
                                            onescol_bf[:, :],
                                            pt[:, 2048 + parm * 1024 + half * 512: 2048 + parm * 1024 + (half + 1) * 512],
                                            start=(ktm == 0), stop=False))
                            E("scalar", lambda s, par=par, head=head: s.activation(
                                out=pt[:, head * 2048 + par * 1024:head * 2048 + (par + 1) * 1024],
                                in_=pA[:, 0:1024],
                                func=AF.Exp, scale=0.125),
                                waits=[(cTE, qk)], inc=cSC)
                            exps[head].append(cSC.n)
                            yield
                        _pv_tail(vsb, exps[0], exps[1])
                        pb1_out[0] = (exps[0][31], exps[1][31], cTE.n)

                    def normalize(qc, expAl, expBl, pv_te):
                        nonlocal prev_norm
                        qoff = qc * 1024
                        E("vector", lambda v: v.tensor_copy(den_sb[64:65, :], pA[64:65, 1024:2048]),
                            waits=[(cTE, pv_te), (cAO, cAO.n)], inc=cVE)
                        E("vector", lambda v: v.tensor_copy(den_sb[0:1, :], pB[0:1, 1024:2048]), inc=cVE)
                        E("vector", lambda v: v.reciprocal(den_sb[0:65, :], den_sb[0:65, :]), inc=cVE)
                        den_ve = cVE.n
                        for half in range(2):
                            E("tensor", lambda t, half=half: t.matmul(
                                pA[:, half * 512:(half + 1) * 512],
                                ones_f32[64:65, :], den_sb[64:65, half * 512:(half + 1) * 512],
                                start=True, stop=True),
                                waits=[(cVE, den_ve), (cSC, expAl)] if half == 0 else [],
                                inc=cTE if half == 1 else None)
                        E("vector", lambda v: v.tensor_copy(bc_sb[:, :], pA[:, 0:1024]),
                            waits=[(cTE, cTE.n)], inc=cVE)
                        E("vector", lambda v: v.tensor_tensor(
                            ao_scr[0:64, :], pA[0:64, 1024:2048], bc_sb[0:64, :], ALU.mult),
                            waits=[(cVE, cVE.n)], inc=cVE)
                        for half in range(2):
                            E("tensor", lambda t, half=half: t.matmul(
                                pA[:, half * 512:(half + 1) * 512],
                                ones_f32[0:1, :], den_sb[0:1, half * 512:(half + 1) * 512],
                                start=True, stop=True),
                                waits=[(cVE, cVE.n), (cSC, expBl)] if half == 0 else [],
                                inc=cTE if half == 1 else None)
                        E("vector", lambda v: v.tensor_copy(bc_sb[:, :], pA[:, 0:1024]),
                            waits=[(cTE, cTE.n), (cVE, cVE.n)], inc=cVE)
                        E("vector", lambda v: v.tensor_tensor(
                            ao_scr[64:128, :], pB[64:128, 1024:2048], bc_sb[64:128, :], ALU.mult),
                            waits=[(cVE, cVE.n)], inc=cVE)
                        aov = cVE.n
                        E("sync", lambda s, p=p, qoff=qoff: s.dma_start(
                            out=dao[p * 128:(p + 1) * 128, qoff:qoff + 1024], in_=ao_scr[:, :]),
                            waits=[(cVE, aov)], inc=cAO, inck=16)
                        prev_norm = cVE.n

                    m0 = passA(0, cSC.n)
                    u0, ls0 = fold(0, m0)
                    eA0, eB0, pv0 = passB0(0, u0, ls0)
                    normalize(0, eA0, eB0, pv0)
                    m1 = passA(1, cSC.n)
                    u1, ls1 = fold(1, m1)
                    pb1_out = [None]
                    def filler(p1):
                        yield from gen_rope(p1)
                        yield from gen_vb(p1)
                    gens = [(gen_passB1(1, u1, ls1, cVE.n), 2)]
                    if p + 1 < NPAIR:
                        gens.append((filler(p + 1), 1))
                    live = [[g, k] for g, k in gens]
                    while live:
                        for item in live[:]:
                            g, k = item
                            try:
                                for _ in range(k):
                                    next(g)
                            except StopIteration:
                                live.remove(item)
                    eA1, eB1, pv1 = pb1_out[0]
                    normalize(1, eA1, eB1, pv1)

                sch.flush(nc)
                ap_done_box[0] = cAO.n

    with contextlib.ExitStack() as s2:
        bufA = s2.enter_context(nc.sbuf_tensor("bufA", [128, NHC * TCT], BF16))
        bufB = s2.enter_context(nc.sbuf_tensor("bufB", [128, NHC * TCT], BF16))
        wh = [s2.enter_context(nc.sbuf_tensor(f"wh{j}", [128, 4096], BF16)) for j in range(3)]
        ao_sb = s2.enter_context(nc.sbuf_tensor("ao_p2", [128, NIC * T], BF16))
        wo_sb = [s2.enter_context(nc.sbuf_tensor(f"wo{j}", [128, 1024], BF16)) for j in range(2)]
        xres = [s2.enter_context(nc.sbuf_tensor(f"xres{j}", [128, 512], F32)) for j in range(2)]
        ytmp = s2.enter_context(nc.sbuf_tensor("ytmp", [128, 512], F32))
        ysb = [s2.enter_context(nc.sbuf_tensor(f"ysb{j}", [128, 512], F32)) for j in range(2)]

        with nc.psum_tensor("pF", [128, 2048], F32) as pF, \
             nc.psum_tensor("pF2", [128, 2048], F32) as pF2:
            for dc in range(NIC):
                E("sync", lambda s, dc=dc: s.dma_start(
                    out=ao_sb[:, dc * T:(dc + 1) * T], in_=dao[dc * 128:(dc + 1) * 128, :]),
                    waits=[(cAO, ap_done_box[0])] if dc == 0 else [], inc=cAO, inck=16)
            ao_ready = cAO.n
            prev_tail = [(cVE, cVE.n)] * 4
            prev_wh_marks = [cTE.n, cTE.n, cTE.n]
            y_ve_marks = []
            st_marks = []

            opdone = [0]

            def gen_op():
                pass
                E("gpsimd", lambda g: g.dma_start(out=wo_sb[0][:, :], in_=dwout[0:128, :]),
                    waits=[(cTE, cTE.n)], inc=cWO[0], inck=16)
                op_marks = []
                for oc in range(NIC):
                    ob = oc % 2
                    wt_r = cWO[ob].n
                    if oc < NIC - 1:
                        E("gpsimd", lambda g, oc=oc: g.dma_start(
                            out=wo_sb[(oc + 1) % 2][:, :], in_=dwout[(oc + 1) * 128:(oc + 2) * 128, :]),
                            waits=[(cTE, op_marks[oc * 4 - 1][1]) if oc >= 1 else (cTE, cTE.n)],
                            inc=cWO[(oc + 1) % 2], inck=16)
                    for qt in range(4):
                        idx = oc * 4 + qt
                        ps = pF2[:, (idx % 4) * 512:(idx % 4 + 1) * 512]
                        w0 = [(cAO, ao_ready), (cWO[ob], wt_r)]
                        if idx >= 4:
                            w0.append((cVE, op_marks[idx - 4][0]))
                        for dc in range(NIC):
                            E("tensor", lambda t, dc=dc, qt=qt, ps=ps, ob=ob: t.matmul(
                                ps, wo_sb[ob][:, dc * 128:(dc + 1) * 128],
                                ao_sb[:, dc * T + qt * 512:dc * T + (qt + 1) * 512],
                                start=(dc == 0), stop=(dc == NIC - 1)),
                                waits=w0 if dc == 0 else [],
                                inc=cTE if dc == NIC - 1 else None)
                        tm = cTE.n
                        E("vector", lambda v, oc=oc, qt=qt, ps=ps: v.tensor_copy(
                            attn_proj[:, oc * T + qt * 512:oc * T + (qt + 1) * 512], ps),
                            waits=[(cTE, tm)], inc=cVE)
                        op_marks.append((cVE.n, tm))
                        yield
                opdone[0] = cVE.n

            def gen_stage(dw, n_oc, n_ic, ncols, rhs_fn, evac_fn, first_extra=None,
                          dwc=None, first_pass=True):
                nonlocal prev_tail, prev_wh_marks
                te_marks = []
                ev = []
                for oc in range(n_oc):
                    wb = oc % 3
                    lw = [(cTE, te_marks[oc - 3])] if oc >= 3 else [(cTE, prev_wh_marks[oc])]
                    if dwc is not None and first_pass:
                        lw.append((cWB[wb % 2], cWB[wb % 2].n))
                    src_t = dw if (dwc is None or first_pass) else dwc
                    E("gpsimd", lambda g, oc=oc, wb=wb, src_t=src_t, ncols=ncols: g.dma_start(
                        out=wh[wb][:, 0:ncols], in_=src_t[oc * 128:(oc + 1) * 128, :]),
                        waits=lw, inc=cWH[wb], inck=16)
                    wt_r = cWH[wb].n
                    if dwc is not None and first_pass:
                        E("sync", lambda s, oc=oc, wb=wb, dwc=dwc, ncols=ncols: s.dma_start(
                            out=dwc[oc * 128:(oc + 1) * 128, :], in_=wh[wb][:, 0:ncols]),
                            waits=[(cWH[wb], wt_r)], inc=cWB[wb % 2], inck=16)
                    ps = pF[:, 512 * (oc % 4):512 * (oc % 4 + 1)]
                    w0 = [(cWH[wb], wt_r)]
                    if first_extra is not None and oc == 0:
                        w0.append(first_extra)
                    if oc >= 4:
                        w0.append(ev[oc - 4])
                    else:
                        w0.append(prev_tail[oc])
                    for ic in range(n_ic):
                        E("tensor", lambda t, ic=ic, ps=ps, wb=wb: t.matmul(
                            ps, wh[wb][:, ic * 128:(ic + 1) * 128], rhs_fn(ic),
                            start=(ic == 0), stop=(ic == n_ic - 1)),
                            waits=w0 if ic == 0 else [],
                            inc=cTE if ic == n_ic - 1 else None)
                    te_marks.append(cTE.n)
                    ev.append(evac_fn(oc, ps, cTE.n))
                    yield
                prev_tail = ev[-4:]
                prev_wh_marks = [te_marks[-3], te_marks[-2], te_marks[-1]]
                evout[0] = ev

            def stage(*a, **k):
                evout[0] = None
                for _ in gen_stage(*a, **k):
                    pass
                return evout[0]

            evout = [None]

            for tct in range(NTCT):
                tof = tct * TCT

                def evac_w2(oc, ps, tm):
                    E("vector", lambda v, oc=oc, ps=ps: v.tensor_scalar(
                        bufB[:, oc * TCT:(oc + 1) * TCT], ps, s2_sb[:, oc:oc + 1], None, ALU.mult),
                        waits=[(cTE, tm)], inc=cVE)
                    return (cVE, cVE.n)

                def evac_w1(oc, ps, tm):
                    E("scalar", lambda s, oc=oc, ps=ps: s.activation(
                        out=bufA[:, oc * TCT:(oc + 1) * TCT], in_=ps,
                        func=AF.Silu, scale=s1_sb[:, oc:oc + 1]),
                        waits=[(cTE, tm)], inc=cSC)
                    scm = cSC.n
                    E("vector", lambda v, oc=oc: v.tensor_tensor(
                        bufB[:, oc * TCT:(oc + 1) * TCT],
                        bufA[:, oc * TCT:(oc + 1) * TCT],
                        bufB[:, oc * TCT:(oc + 1) * TCT], ALU.mult),
                        waits=[(cSC, scm), (cVE, w2ev[oc][1])], inc=cVE)
                    return (cSC, scm)

                def evac_h1(oc, ps, tm):
                    E("scalar", lambda s, oc=oc, ps=ps: s.activation(
                        out=bufA[:, oc * TCT:(oc + 1) * TCT], in_=ps,
                        func=AF.Gelu, scale=h1s_sb[:, oc:oc + 1]),
                        waits=[(cTE, tm)], inc=cSC)
                    return (cSC, cSC.n)

                def evac_h2(oc, ps, tm):
                    E("scalar", lambda s, oc=oc, ps=ps: s.activation(
                        out=bufB[:, oc * TCT:(oc + 1) * TCT], in_=ps,
                        func=AF.Gelu, scale=h2s_sb[:, oc:oc + 1]),
                        waits=[(cTE, tm)], inc=cSC)
                    return (cSC, cSC.n)

                def evac_w3(oc, ps, tm):
                    jb = oc % 2
                    nst = len(st_marks)
                    lw = [(cVE, y_ve_marks[nst - 2])] if nst >= 2 else []
                    E("sync", lambda s, oc=oc, jb=jb, tof=tof: s.dma_start(
                        out=xres[jb][:, :], in_=dx[oc * 128:(oc + 1) * 128, tof:tof + TCT]),
                        waits=lw, inc=cXR[jb], inck=16)
                    ld = cXR[jb].n
                    E("vector", lambda v, oc=oc, ps=ps, tof=tof: v.scalar_tensor_tensor(
                        ytmp[:, :], ps, w3s_sb[:, oc:oc + 1],
                        attn_proj[:, oc * T + tof: oc * T + tof + TCT],
                        ALU.mult, ALU.add),
                        waits=[(cTE, tm), (cVE, opdone[0])], inc=cVE)
                    sw = [(cXR[jb], ld)]
                    if nst >= 2:
                        sw.append((cST[jb], st_marks[nst - 2]))
                    E("vector", lambda v, jb=jb: v.tensor_tensor(
                        ysb[jb][:, :], ytmp[:, :], xres[jb][:, :], ALU.add),
                        waits=sw, inc=cVE)
                    yv = cVE.n
                    E("sync", lambda s, oc=oc, jb=jb, tof=tof: s.dma_start(
                        out=dy[oc * 128:(oc + 1) * 128, tof:tof + TCT], in_=ysb[jb][:, :]),
                        waits=[(cVE, yv)], inc=cST[jb], inck=16)
                    y_ve_marks.append(yv)
                    st_marks.append(cST[jb].n)
                    return (cVE, cVE.n)

                if tct == 0:
                    evout[0] = None
                    g1 = gen_stage(dw2, NHC, NIC, 1024,
                                   lambda ic: xn_own[:, ic * T + tof: ic * T + tof + TCT],
                                   evac_w2, first_extra=(cTAB, scales_wt),
                                   dwc=dw2c, first_pass=True)
                    g2 = gen_op()
                    live = [g1, g2]
                    while live:
                        for g in live[:]:
                            try:
                                next(g)
                            except StopIteration:
                                live.remove(g)
                    ew2 = evout[0]
                else:
                    ew2 = stage(dw2, NHC, NIC, 1024,
                                lambda ic: xn_own[:, ic * T + tof: ic * T + tof + TCT],
                                evac_w2, first_extra=(cTAB, scales_wt),
                                dwc=dw2c, first_pass=False)
                w2ev = ew2
                ew1 = stage(dw1, NHC, NIC, 1024,
                            lambda ic: xn_own[:, ic * T + tof: ic * T + tof + TCT],
                            evac_w1, dwc=dw1c, first_pass=(tct == 0))
                h_done = cVE.n
                eh1 = stage(dh1, NHC, NHC, 4096,
                            lambda ic: bufB[:, ic * TCT:(ic + 1) * TCT],
                            evac_h1, first_extra=(cVE, h_done),
                            dwc=dh1c, first_pass=(tct == 0))
                eh2 = stage(dh2, NHC, NHC, 4096,
                            lambda ic: bufA[:, ic * TCT:(ic + 1) * TCT],
                            evac_h2, first_extra=(cSC, eh1[-1][1]),
                            dwc=dh2c, first_pass=(tct == 0))
                stage(dw3, NIC, NHC, 4096,
                      lambda ic: bufB[:, ic * TCT:(ic + 1) * TCT],
                      evac_w3, first_extra=(cSC, eh2[-1][1]),
                      dwc=dw3c, first_pass=(tct == 0))

            E("sync", lambda s: None, waits=[(cST[0], cST[0].n), (cST[1], cST[1].n)])
            sch.flush(nc)

    stack.close()
    return nc, dbg



def _strip_img(wT, n_ic):
    return wT.reshape(n_ic, 128, 128).transpose(1, 0, 2).reshape(128, n_ic * 128)


def _weight_img(wT, n_ic, n_oc):
    return np.ascontiguousarray(np.concatenate(
        [_strip_img(wT[:, oc * 128:(oc + 1) * 128], n_ic) for oc in range(n_oc)], axis=0))


def _rope_tables(h):
    pos = np.concatenate([np.arange(h * T, (h + 1) * T),
                          np.arange((1 - h) * T, (2 - h) * T)]).astype(np.float64)
    inv = ROPE_BASE ** (-np.arange(32, dtype=np.float64) / 32.0)
    ang = pos[:, None] * inv[None, :]
    c32 = np.cos(ang).astype(np.float32)
    s32 = np.sin(ang).astype(np.float32)
    cos_rows = np.zeros((128, L), np.float32)
    sin_rows = np.zeros((128, L), np.float32)
    for d in range(128):
        dd = d % 64
        j = dd % 32
        cos_rows[d] = c32[:, j]
        sin_rows[d] = (-1.0 if dd < 32 else 1.0) * s32[:, j]
    return (np.ascontiguousarray(cos_rows.astype(ml_dtypes.bfloat16)),
            np.ascontiguousarray(sin_rows.astype(ml_dtypes.bfloat16)))


def make_in_maps(x, wqkv, wout, w1w, w1s, w2w, w2s, h1w, h1s, h2w, h2s, w3w, w3s):
    x = np.asarray(x, np.float32)
    wqkvT = np.ascontiguousarray(np.asarray(wqkv, np.float32).T)
    woutT = np.ascontiguousarray(np.asarray(wout, np.float32).T)
    w1T = np.ascontiguousarray(np.asarray(w1w, np.float32).T)
    w2T = np.ascontiguousarray(np.asarray(w2w, np.float32).T)
    h1T = np.ascontiguousarray(np.asarray(h1w, np.float32).T)
    h2T = np.ascontiguousarray(np.asarray(h2w, np.float32).T)
    w3T = np.ascontiguousarray(np.asarray(w3w, np.float32).T)

    blocks = []
    for p in range(NPAIR):
        for r in range(3):
            blocks.append(_strip_img(wqkvT[:, r * D + p * 128: r * D + (p + 1) * 128], NIC))
    wqkv_img = np.ascontiguousarray(np.concatenate(blocks, axis=0))
    imgs = {
        "wqkv": wqkv_img,
        "wout": _weight_img(woutT, NIC, NIC),
        "w1": _weight_img(w1T, NIC, NHC),
        "w2": _weight_img(w2T, NIC, NHC),
        "h1": _weight_img(h1T, NHC, NHC),
        "h2": _weight_img(h2T, NHC, NHC),
        "w3": _weight_img(w3T, NHC, NIC),
        "s1": np.ascontiguousarray(np.asarray(w1s, np.float32).reshape(32, 128).T),
        "s2": np.ascontiguousarray(np.asarray(w2s, np.float32).reshape(32, 128).T),
        "h1s": np.ascontiguousarray(np.asarray(h1s, np.float32).reshape(32, 128).T),
        "h2s": np.ascontiguousarray(np.asarray(h2s, np.float32).reshape(32, 128).T),
        "w3s": np.ascontiguousarray(np.asarray(w3s, np.float32).reshape(8, 128).T),
    }
    tabs = [_rope_tables(0), _rope_tables(1)]
    in_maps = []
    for c in range(8):
        b, h = c // 2, c % 2
        xr = np.ascontiguousarray(
            np.concatenate([x[b, h * T:(h + 1) * T, :], x[b, (1 - h) * T:(2 - h) * T, :]], axis=0).T)
        m = {"x": xr, "costab": tabs[h][0], "sintab": tabs[h][1]}
        m.update(imgs)
        in_maps.append(m)
    return in_maps


def kernel(L0_attn_x, L0_attn_w_qkv, L0_attn_w_out,
           L0_ffn_w1_w, L0_ffn_w1_s, L0_ffn_w2_w, L0_ffn_w2_s,
           L0_ffn_h1_w, L0_ffn_h1_s, L0_ffn_h2_w, L0_ffn_h2_s,
           L0_ffn_w3_w, L0_ffn_w3_s):
    in_maps = make_in_maps(L0_attn_x, L0_attn_w_qkv, L0_attn_w_out,
                           L0_ffn_w1_w, L0_ffn_w1_s, L0_ffn_w2_w, L0_ffn_w2_s,
                           L0_ffn_h1_w, L0_ffn_h1_s, L0_ffn_h2_w, L0_ffn_h2_s,
                           L0_ffn_w3_w, L0_ffn_w3_s)
    nc, _ = build_nc(debug=False)
    res = run_bass_kernel_spmd(nc, in_maps, core_ids=list(range(8)))
    out = np.empty((B, L, D), np.float32)
    for c in range(8):
        b, h = c // 2, c % 2
        out[b, h * T:(h + 1) * T, :] = np.asarray(res.results[c]["y"]).T
    return out


# version 12
# speedup vs baseline: 1.0401x; 1.0044x over previous
import contextlib

import numpy as np
import ml_dtypes

import concourse.bass as bass
import concourse.mybir as mybir
from concourse.bass_utils import run_bass_kernel_spmd

F32 = mybir.dt.float32
BF16 = mybir.dt.bfloat16
AF = mybir.ActivationFunctionType
ALU = mybir.AluOpType
AX = mybir.AxisListType

B, L, D = 4, 4096, 1024
T = 2048
H, DH = 16, 64
HID = 4096
NIC = D // 128
NHC = HID // 128
NPAIR = H // 2
TCT = 512
NTCT = T // TCT
ROPE_BASE = 10000.0
EPS = 1e-5


class Ctr:

    def __init__(self, stack, nc, name):
        self.sem = stack.enter_context(nc.semaphore(name))
        self.n = 0


class Sched:

    ENGINES = ("tensor", "vector", "scalar", "sync", "gpsimd")

    def __init__(self):
        self.ops = {e: [] for e in self.ENGINES}

    def emit(self, engine, fn, waits=(), inc=None, inck=1):
        if inc is not None:
            inc.n += inck
            inc = (inc, inck)
        self.ops[engine].append((fn, tuple(waits), inc))

    def flush(self, nc):
        with nc.Block() as block:
            def mk(eng_name):
                def _(eng):
                    for fn, waits, inc in self.ops[eng_name]:
                        for ctr, val in waits:
                            if val > 0:
                                eng.wait_ge(ctr.sem, val)
                        inst = fn(eng)
                        if inc is not None:
                            ctr, k = inc
                            inst.then_inc(ctr.sem, k)
                return _
            for name in self.ENGINES:
                if self.ops[name]:
                    getattr(block, name)(mk(name))
        self.ops = {e: [] for e in self.ENGINES}


def build_nc(debug=False):
    stack = contextlib.ExitStack()
    nc = bass.Bass("TRN2", target_bir_lowering=False, detect_race_conditions=False)

    dx = nc.dram_tensor("x", [D, L], F32, kind="ExternalInput")
    dwqkv = nc.dram_tensor("wqkv", [NPAIR * 3 * 128, NIC * 128], F32, kind="ExternalInput")
    dwout = nc.dram_tensor("wout", [NIC * 128, NIC * 128], F32, kind="ExternalInput")
    dw1 = nc.dram_tensor("w1", [NHC * 128, NIC * 128], F32, kind="ExternalInput")
    dw2 = nc.dram_tensor("w2", [NHC * 128, NIC * 128], F32, kind="ExternalInput")
    dh1 = nc.dram_tensor("h1", [NHC * 128, NHC * 128], F32, kind="ExternalInput")
    dh2 = nc.dram_tensor("h2", [NHC * 128, NHC * 128], F32, kind="ExternalInput")
    dw3 = nc.dram_tensor("w3", [NIC * 128, NHC * 128], F32, kind="ExternalInput")
    ds1 = nc.dram_tensor("s1", [128, 32], F32, kind="ExternalInput")
    ds2 = nc.dram_tensor("s2", [128, 32], F32, kind="ExternalInput")
    dh1s = nc.dram_tensor("h1s", [128, 32], F32, kind="ExternalInput")
    dh2s = nc.dram_tensor("h2s", [128, 32], F32, kind="ExternalInput")
    dw3s = nc.dram_tensor("w3s", [128, 8], F32, kind="ExternalInput")
    dao = nc.dram_tensor("dao", [NPAIR * 128, T], BF16, kind="Internal")
    dw1c = nc.dram_tensor("w1c", [NHC * 128, NIC * 128], BF16, kind="Internal")
    dw2c = nc.dram_tensor("w2c", [NHC * 128, NIC * 128], BF16, kind="Internal")
    dh1c = nc.dram_tensor("h1c", [NHC * 128, NHC * 128], BF16, kind="Internal")
    dh2c = nc.dram_tensor("h2c", [NHC * 128, NHC * 128], BF16, kind="Internal")
    dw3c = nc.dram_tensor("w3c", [NIC * 128, NHC * 128], BF16, kind="Internal")
    dcos = nc.dram_tensor("costab", [128, L], F32, kind="ExternalInput")
    dsin = nc.dram_tensor("sintab", [128, L], F32, kind="ExternalInput")
    dy = nc.dram_tensor("y", [D, T], F32, kind="ExternalOutput")
    dbg = {}
    if debug:
        dbg["xn"] = nc.dram_tensor("dbg_xn", [D, L], F32, kind="ExternalOutput")
        dbg["q"] = nc.dram_tensor("dbg_q", [128, T], F32, kind="ExternalOutput")
        dbg["k"] = nc.dram_tensor("dbg_k", [128, L], F32, kind="ExternalOutput")
        dbg["v"] = nc.dram_tensor("dbg_v", [128, 32 * 130], F32, kind="ExternalOutput")
        dbg["ao"] = nc.dram_tensor("dbg_ao", [D, T], F32, kind="ExternalOutput")
        dbg["ap"] = nc.dram_tensor("dbg_ap", [D, T], F32, kind="ExternalOutput")

    cX = [Ctr(stack, nc, f"cX{j}") for j in range(2)]
    cSW = Ctr(stack, nc, "cSW")
    cTAB = Ctr(stack, nc, "cTAB")
    cWQ = [Ctr(stack, nc, f"cWQ{j}") for j in range(2)]
    cWO = [Ctr(stack, nc, f"cWO{j}") for j in range(2)]
    cWH = [Ctr(stack, nc, f"cWH{j}") for j in range(3)]
    cXR = [Ctr(stack, nc, f"cXR{j}") for j in range(2)]
    cWB = [Ctr(stack, nc, f"cWB{j}") for j in range(2)]
    cST = [Ctr(stack, nc, f"cST{j}") for j in range(2)]
    cQS = Ctr(stack, nc, "cQS")
    cU = Ctr(stack, nc, "cU")
    cAO = Ctr(stack, nc, "cAO")
    cCS = [Ctr(stack, nc, f"cCS{j}") for j in range(2)]
    cDBG = Ctr(stack, nc, "cDBG")
    cTE = Ctr(stack, nc, "cTE")
    cSC = Ctr(stack, nc, "cSC")
    cVE = Ctr(stack, nc, "cVE")

    sch = Sched()
    E = sch.emit

    xn_own = stack.enter_context(nc.sbuf_tensor("xn_own", [128, NIC * T], BF16))

    attn_proj = stack.enter_context(nc.sbuf_tensor("attn_proj", [128, NIC * T], BF16))
    s1_sb = stack.enter_context(nc.sbuf_tensor("s1_sb", [128, 32], F32))
    s2_sb = stack.enter_context(nc.sbuf_tensor("s2_sb", [128, 32], F32))
    h1s_sb = stack.enter_context(nc.sbuf_tensor("h1s_sb", [128, 32], F32))
    h2s_sb = stack.enter_context(nc.sbuf_tensor("h2s_sb", [128, 32], F32))
    w3s_sb = stack.enter_context(nc.sbuf_tensor("w3s_sb", [128, 8], F32))
    ones_f32 = stack.enter_context(nc.sbuf_tensor("ones_f32", [65, 128], F32))
    onescol_f32 = stack.enter_context(nc.sbuf_tensor("onescol_f32", [128, 1], F32))
    onescol_bf = stack.enter_context(nc.sbuf_tensor("onescol_bf", [128, 1], BF16))

    E("vector", lambda v: v.memset(ones_f32[:, :], 1.0), inc=cVE)
    E("vector", lambda v: v.memset(onescol_f32[:, :], 1.0), inc=cVE)
    E("vector", lambda v: v.memset(onescol_bf[:, :], 1.0), inc=cVE)
    const_ve = cVE.n
    for dsrc, dst in ((ds1, s1_sb), (ds2, s2_sb), (dh1s, h1s_sb), (dh2s, h2s_sb), (dw3s, w3s_sb)):
        E("gpsimd", lambda g, s=dsrc, d=dst: g.dma_start(out=d[:, :], in_=s[:, :]), inc=cTAB, inck=16)
    scales_wt = cTAB.n

    ap_done_box = [0]

    with contextlib.ExitStack() as sA:
        xn_prt = sA.enter_context(nc.sbuf_tensor("xn_prt", [128, NIC * T], BF16))

        with contextlib.ExitStack() as s0:
            xch = [s0.enter_context(nc.sbuf_tensor(f"xch{j}", [128, L], F32)) for j in range(2)]
            xsq = [s0.enter_context(nc.sbuf_tensor(f"xsq{j}", [128, 512], F32)) for j in range(2)]
            es_sb = s0.enter_context(nc.sbuf_tensor("es_sb", [33, L], F32))
            mean_bc = s0.enter_context(nc.sbuf_tensor("mean_bc", [128, L], BF16))
            rstd_bc = s0.enter_context(nc.sbuf_tensor("rstd_bc", [128, L], BF16))
            t1 = s0.enter_context(nc.sbuf_tensor("t1", [128, T // 2], F32))
            v1 = t1[:, 0:512]
            v2 = t1[:, 512:1024]

            with nc.psum_tensor("stats_ps", [33, L], F32) as stats_ps:
                te_ic = []
                sc_ic = []
                for ic in range(NIC):
                    bj = ic % 2
                    lw = []
                    if ic >= 2:
                        lw = [(cTE, te_ic[ic - 2]), (cSC, sc_ic[ic - 2])]
                    E("sync", lambda s, ic=ic, bj=bj: s.dma_start(
                        out=xch[bj][:, :], in_=dx[ic * 128:(ic + 1) * 128, :]),
                        waits=lw, inc=cX[bj], inck=16)
                    ldc, ld = cX[bj], cX[bj].n
                    for tc in range(8):
                        sq = tc % 2
                        sqw = [(ldc, ld)]
                        if ic > 0 or tc >= 2:
                            sqw.append((cTE, cTE.n))
                        E("scalar", lambda s, bj=bj, tc=tc, sq=sq: s.activation(
                            out=xsq[sq][:, :], in_=xch[bj][:, tc * 512:(tc + 1) * 512], func=AF.Square),
                            waits=sqw, inc=cSC)
                        scm = cSC.n
                        E("tensor", lambda t, ic=ic, bj=bj, tc=tc: t.matmul(
                            stats_ps[0:1, tc * 512:(tc + 1) * 512], onescol_f32[:, :],
                            xch[bj][:, tc * 512:(tc + 1) * 512],
                            start=(ic == 0), stop=(ic == NIC - 1)),
                            waits=([(ldc, ld), (cVE, const_ve)] if ic == 0 else [(ldc, ld)]) if tc == 0 else [])
                        E("tensor", lambda t, ic=ic, tc=tc, sq=sq: t.matmul(
                            stats_ps[32:33, tc * 512:(tc + 1) * 512], onescol_f32[:, :],
                            xsq[sq][:, :],
                            start=(ic == 0), stop=(ic == NIC - 1)),
                            waits=[(cSC, scm)],
                            inc=cTE)
                    te_ic.append(cTE.n)
                    sc_ic.append(cSC.n)
                E("vector", lambda v: v.tensor_single_scalar(
                    es_sb[0:1, :], stats_ps[0:1, :], 1.0 / D, ALU.mult),
                    waits=[(cTE, cTE.n), (cVE, const_ve)], inc=cVE)
                E("vector", lambda v: v.tensor_single_scalar(
                    es_sb[32:33, :], stats_ps[32:33, :], 1.0 / D, ALU.mult), inc=cVE)
                sch.flush(nc)
            p0a_te, p0a_sc, p0a_ve = cTE.n, cSC.n, cVE.n

            with nc.psum_tensor("bc_ps", [128, 2048], F32) as bc_ps:
                vtc = []
                for tc in range(8):
                    par = tc % 2
                    mslot = bc_ps[:, (2 * par) * 512:(2 * par + 1) * 512]
                    eslot = bc_ps[:, (2 * par + 1) * 512:(2 * par + 2) * 512]
                    w0 = [(cVE, p0a_ve)]
                    if tc >= 2:
                        w0.append((cVE, vtc[tc - 2]))
                    sl = slice(tc * 512, (tc + 1) * 512)
                    E("tensor", lambda t, m=mslot, sl=sl: t.matmul(
                        m, ones_f32[0:1, :], es_sb[0:1, sl], start=True, stop=True), waits=w0)
                    E("tensor", lambda t, e=eslot, sl=sl: t.matmul(
                        e, ones_f32[32:33, :], es_sb[32:33, sl], start=True, stop=True), inc=cTE)
                    tm = cTE.n
                    E("vector", lambda v, m=mslot, sl=sl: v.tensor_copy(mean_bc[:, sl], m),
                        waits=[(cTE, tm)], inc=cVE)
                    E("vector", lambda v, m=mslot: v.tensor_copy(v1, m), inc=cVE)
                    E("vector", lambda v, e=eslot: v.tensor_tensor(v2, v1, v1, ALU.mult), inc=cVE)
                    E("vector", lambda v, e=eslot: v.tensor_tensor(v2, e, v2, ALU.subtract), inc=cVE)
                    E("vector", lambda v: v.tensor_single_scalar(v2, v2, EPS, ALU.add), inc=cVE)
                    vsq = cVE.n
                    E("scalar", lambda s: s.activation(out=v1, in_=v2, func=AF.Sqrt),
                        waits=[(cVE, vsq), (cSC, p0a_sc)], inc=cSC)
                    E("vector", lambda v: v.reciprocal(v2, v1),
                        waits=[(cSC, cSC.n)], inc=cVE)
                    E("vector", lambda v, sl=sl: v.tensor_copy(rstd_bc[:, sl], v2), inc=cVE)
                    vtc.append(cVE.n)
                sch.flush(nc)
            p0b_ve = cVE.n

            va = []
            for ic in range(NIC):
                bj = ic % 2
                lw = [(cVE, p0b_ve), (cTE, cTE.n), (cSC, cSC.n)]
                if ic >= 2:
                    lw.append((cVE, va[ic - 2]))
                E("sync", lambda s, ic=ic, bj=bj: s.dma_start(
                    out=xch[bj][:, :], in_=dx[ic * 128:(ic + 1) * 128, :]),
                    waits=lw, inc=cX[bj], inck=16)
                ldc, ld = cX[bj], cX[bj].n
                for quarter in range(4):
                    dst = xn_own if quarter < 2 else xn_prt
                    hs = slice(quarter * (T // 2), (quarter + 1) * (T // 2))
                    ds = slice(ic * T + (quarter % 2) * (T // 2), ic * T + (quarter % 2 + 1) * (T // 2))
                    E("vector", lambda v, bj=bj, hs=hs: v.tensor_tensor(
                        t1[:, :], xch[bj][:, hs], mean_bc[:, hs], ALU.subtract),
                        waits=[(ldc, ld)] if quarter == 0 else [], inc=cVE)
                    E("vector", lambda v, hs=hs, ds=ds, dst=dst: v.tensor_tensor(
                        dst[:, ds], t1[:, :], rstd_bc[:, hs], ALU.mult), inc=cVE)
                va.append(cVE.n)
            xn_done = cVE.n
            if debug:
                for ic in range(NIC):
                    E("gpsimd", lambda g, ic=ic: g.dma_start(
                        out=dbg["xn"][ic * 128:(ic + 1) * 128, 0:T], in_=xn_own[:, ic * T:(ic + 1) * T]),
                        waits=[(cVE, xn_done)], inc=cDBG, inck=16)
                    E("gpsimd", lambda g, ic=ic: g.dma_start(
                        out=dbg["xn"][ic * 128:(ic + 1) * 128, T:L], in_=xn_prt[:, ic * T:(ic + 1) * T]),
                        inc=cDBG, inck=16)
            sch.flush(nc)

        with contextlib.ExitStack() as s1:
            kp_A = [s1.enter_context(nc.sbuf_tensor(f"kpA{j}", [65, L], BF16)) for j in range(2)]
            kp_B = [s1.enter_context(nc.sbuf_tensor(f"kpB{j}", [65, L], BF16)) for j in range(2)]
            qp_A = [s1.enter_context(nc.sbuf_tensor(f"qpA{j}", [65, T], BF16)) for j in range(2)]
            qp_B = [s1.enter_context(nc.sbuf_tensor(f"qpB{j}", [65, T], BF16)) for j in range(2)]
            vsb2 = [s1.enter_context(nc.sbuf_tensor(f"vsb{j}", [128, 32 * 130], BF16)) for j in range(2)]
            pt = s1.enter_context(nc.sbuf_tensor("pt", [128, 4096], BF16))
            wq_sb = s1.enter_context(nc.sbuf_tensor("wq", [128, 1024], BF16))
            wk_sb = s1.enter_context(nc.sbuf_tensor("wk", [128, 1024], BF16))
            wv_sb = s1.enter_context(nc.sbuf_tensor("wv", [128, 1024], BF16))
            qraw = [s1.enter_context(nc.sbuf_tensor(f"qraw{j}", [128, 512], BF16)) for j in range(2)]
            qswp = [s1.enter_context(nc.sbuf_tensor(f"qswp{j}", [128, 512], BF16)) for j in range(2)]
            rt1 = [s1.enter_context(nc.sbuf_tensor(f"rt1_{j}", [128, 512], F32)) for j in range(2)]
            cs_sb = s1.enter_context(nc.sbuf_tensor("cs_sb", [128, 2048], F32))
            den_sb = s1.enter_context(nc.sbuf_tensor("den_sb", [65, 1024], F32))
            bc_sb = s1.enter_context(nc.sbuf_tensor("bc_sb", [128, 1024], F32))
            bnd = s1.enter_context(nc.sbuf_tensor("bnd", [33, 8], F32))
            c8col = s1.enter_context(nc.sbuf_tensor("c8col", [128, 2], F32))
            MA = s1.enter_context(nc.sbuf_tensor("MA", [128, 1024], BF16))
            MB = s1.enter_context(nc.sbuf_tensor("MB", [128, 1024], BF16))
            ao_scr = s1.enter_context(nc.sbuf_tensor("ao_scr", [128, 1024], BF16))

            for j in range(2):
                E("vector", lambda v, j=j: v.memset(vsb2[j][:, :], 0.0), inc=cVE)
                vr = vsb2[j].rearrange("p (vt c) -> p vt c", c=130)
                E("vector", lambda v, vr=vr: v.memset(vr[:, :, 64:65], 1.0), inc=cVE)
                E("vector", lambda v, j=j: v.memset(kp_A[j][64:65, :], 1.0), inc=cVE)
                E("vector", lambda v, j=j: v.memset(kp_B[j][64:65, :], 1.0), inc=cVE)
            E("vector", lambda v: v.memset(bc_sb[:, :], 0.0), inc=cVE)
            E("vector", lambda v: v.memset(den_sb[:, :], 1.0), inc=cVE)

            with (
                nc.psum_tensor("pA", [128, 2048], F32) as pA,
                nc.psum_tensor("pB", [128, 2048], F32) as pB,
            ):
                def wload(g, dstT, row0):
                    return g.dma_start(out=dstT[:, :], in_=dwqkv[row0 * 128:(row0 + 1) * 128, :])

                prev_norm = 0
                swp_mark = [0, 0]
                qs_mark = [0, 0]
                rope_tile_idx = [0]
                rope_done = {}
                last_swp_box = [0]

                def gen_rope(p):
                    st = p % 2
                    pw = [(cTE, cTE.n), (cSC, cSC.n)]
                    E("gpsimd", lambda g, p=p: wload(g, wq_sb, p * 3 + 0), waits=pw, inc=cWQ[0], inck=16)
                    E("gpsimd", lambda g, p=p: wload(g, wk_sb, p * 3 + 1), inc=cWQ[0], inck=16)
                    E("gpsimd", lambda g, p=p: wload(g, wv_sb, p * 3 + 2), inc=cWQ[0], inck=16)
                    wt_ready = cWQ[0].n
                    info = {"wt": wt_ready}
                    ve_tile = []
                    for tile in range(12):
                        if tile < 4:
                            wsb, dstA, dstB, j = wq_sb, qp_A[st], qp_B[st], tile
                            xs = lambda ic, j=j: xn_own[:, ic * T + j * 512: ic * T + (j + 1) * 512]
                        else:
                            j = tile - 4
                            wsb, dstA, dstB = wk_sb, kp_A[st], kp_B[st]
                            xs = lambda ic, j=j: (xn_own if j < 4 else xn_prt)[:, ic * T + (j % 4) * 512: ic * T + (j % 4 + 1) * 512]
                        ps = pB[:, (tile % 2) * 512:(tile % 2 + 1) * 512]
                        gpar = rope_tile_idx[0] % 2
                        gw = [(cVE, ve_tile[tile - 2])] if tile >= 2 else [(cVE, cVE.n)]
                        E("sync", lambda s, j=j, gpar=gpar: s.dma_start(
                            out=cs_sb[:, gpar * 1024:gpar * 1024 + 512],
                            in_=dcos[:, j * 512:(j + 1) * 512]),
                            waits=gw, inc=cCS[gpar], inck=16)
                        E("sync", lambda s, j=j, gpar=gpar: s.dma_start(
                            out=cs_sb[:, gpar * 1024 + 512:gpar * 1024 + 1024],
                            in_=dsin[:, j * 512:(j + 1) * 512]),
                            inc=cCS[gpar], inck=16)
                        cs_ready = cCS[gpar].n
                        rope_tile_idx[0] += 1
                        w0 = [(cVE, xn_done), (cWQ[0], wt_ready)]
                        if tile >= 2:
                            w0.append((cVE, ve_tile[tile - 2]))
                        else:
                            w0.append((cVE, prev_norm))
                        for ic in range(NIC):
                            E("tensor", lambda t, ic=ic, ps=ps, wsb=wsb, xs=xs: t.matmul(
                                ps, wsb[:, ic * 128:(ic + 1) * 128], xs(ic),
                                start=(ic == 0), stop=(ic == NIC - 1)),
                                waits=w0 if ic == 0 else [],
                                inc=cTE if ic == NIC - 1 else None)
                        tm = cTE.n
                        qr, qs, r1 = qraw[gpar], qswp[gpar], rt1[gpar]
                        E("scalar", lambda s, ps=ps, qr=qr: s.activation(
                            out=qr[:, :], in_=ps, func=AF.Copy),
                            waits=[(cTE, tm), (cSW, swp_mark[gpar]), (cQS, qs_mark[gpar]), (cSC, cSC.n)], inc=cSC)
                        vcp = cSC.n
                        for (a, b2) in ((0, 32), (32, 0), (64, 96), (96, 64)):
                            E("sync", lambda s, a=a, b2=b2, qr=qr, qs=qs: s.dma_start(
                                out=qs[a:a + 32, :], in_=qr[b2:b2 + 32, :]),
                                waits=[(cSC, vcp)] if a == 0 else [], inc=cSW, inck=16)
                        swp_mark[gpar] = last_swp = cSW.n
                        E("vector", lambda v, gpar=gpar, r1=r1, qr=qr: v.tensor_tensor(
                            r1[:, :], qr[:, :], cs_sb[:, gpar * 1024:gpar * 1024 + 512], ALU.mult),
                            waits=[(cCS[gpar], cs_ready), (cSC, vcp)], inc=cVE)
                        E("vector", lambda v, gpar=gpar, qs=qs: v.tensor_tensor(
                            qs[:, :], qs[:, :], cs_sb[:, gpar * 1024 + 512:gpar * 1024 + 1024], ALU.mult),
                            waits=[(cSW, last_swp)], inc=cVE)
                        E("vector", lambda v, r1=r1, qs=qs: v.tensor_tensor(
                            r1[:, :], r1[:, :], qs[:, :], ALU.add), inc=cVE)
                        vq = cVE.n
                        sl = slice(j * 512, (j + 1) * 512)
                        E("gpsimd", lambda g, sl=sl, dstA=dstA, r1=r1: g.dma_start(
                            out=dstA[0:64, sl], in_=r1[0:64, :]),
                            waits=[(cVE, vq)], inc=cQS, inck=16)
                        E("gpsimd", lambda g, sl=sl, dstB=dstB, r1=r1: g.dma_start(
                            out=dstB[0:64, sl], in_=r1[64:128, :]),
                            inc=cQS, inck=16)
                        qs_mark[gpar] = cQS.n
                        ve_tile.append(cVE.n)
                        yield
                    info["dmas"] = cQS.n
                    info["ve"] = cVE.n
                    rope_done[p] = info

                for p in range(NPAIR):
                    st = p % 2
                    kpA, kpB, qpA, qpB, vsb = kp_A[st], kp_B[st], qp_A[st], qp_B[st], vsb2[st]
                    if p == 0:
                        for _ in gen_rope(0):
                            pass
                    qk_dmas = rope_done[p]["dmas"]
                    wt_ready = rope_done[p]["wt"]

                    E("vector", lambda v: v.memset(bnd[:, :], 0.0),
                        waits=[(cTE, cTE.n)], inc=cVE)
                    bnd_ve0 = cVE.n
                    sqm = [0, 0]
                    mmm = [cTE.n, cTE.n]
                    rdm = []
                    ti = 0
                    for srcs, col in (((qpA, qpB), 0), ((kpA, kpB), 1)):
                        n_tiles = 4 if col == 0 else 8
                        for j in range(n_tiles):
                            for hh, srch in enumerate(srcs):
                                par = ti % 2
                                sqb = MA[0:64, par * 512:(par + 1) * 512]
                                E("scalar", lambda s, srch=srch, j=j, sqb=sqb: s.activation(
                                    out=sqb, in_=srch[0:64, j * 512:(j + 1) * 512], func=AF.Square),
                                    waits=[(cQS, qk_dmas), (cTE, mmm[par]), (cSC, cSC.n)], inc=cSC)
                                sqm[par] = cSC.n
                                row = hh * 32
                                pslot = pA[row:row + 1, par * 512:(par + 1) * 512]
                                wmm = [(cSC, sqm[par])]
                                if ti >= 2:
                                    wmm.append((cVE, rdm[ti - 2]))
                                else:
                                    wmm.append((cVE, bnd_ve0))
                                E("tensor", lambda t, pslot=pslot, sqb=sqb: t.matmul(
                                    pslot, onescol_bf[0:64, :], sqb, start=True, stop=True),
                                    waits=wmm, inc=cTE)
                                mmm[par] = cTE.n
                                E("vector", lambda v, row=row, pslot=pslot: v.tensor_reduce(
                                    bnd[row:row + 1, 5:6], pslot, AX.X, ALU.max),
                                    waits=[(cTE, mmm[par])], inc=cVE)
                                E("vector", lambda v, col=col, row=row: v.tensor_tensor(
                                    bnd[row:row + 1, col:col + 1], bnd[row:row + 1, col:col + 1],
                                    bnd[row:row + 1, 5:6], ALU.max), inc=cVE)
                                rdm.append(cVE.n)
                                ti += 1
                    E("vector", lambda v: v.tensor_tensor(
                        bnd[:, 2:3], bnd[:, 0:1], bnd[:, 1:2], ALU.mult), inc=cVE)
                    E("scalar", lambda s: s.activation(
                        out=bnd[:, 3:4], in_=bnd[:, 2:3], func=AF.Sqrt),
                        waits=[(cVE, cVE.n)], inc=cSC)
                    E("vector", lambda v: v.tensor_scalar(
                        bnd[:, 4:5], bnd[:, 3:4], -0.1275 / 8.0, None, ALU.mult),
                        waits=[(cSC, cSC.n)], inc=cVE)
                    E("vector", lambda v: v.tensor_scalar(
                        bnd[:, 6:7], bnd[:, 3:4], -0.1275 * 8.0, None, ALU.mult), inc=cVE)
                    bnd_ve = cVE.n
                    E("tensor", lambda t: t.matmul(
                        pA[0:128, 0:1], ones_f32[0:1, :], bnd[0:1, 4:5], start=True, stop=True),
                        waits=[(cVE, bnd_ve)])
                    E("tensor", lambda t: t.matmul(
                        pA[0:128, 512:513], ones_f32[32:33, :], bnd[32:33, 4:5], start=True, stop=True),
                        inc=cTE)
                    E("vector", lambda v: v.tensor_copy(c8col[:, 0:1], pA[0:128, 0:1]),
                        waits=[(cTE, cTE.n)], inc=cVE)
                    E("vector", lambda v: v.tensor_copy(c8col[:, 1:2], pA[0:128, 512:513]), inc=cVE)
                    c8_ve = cVE.n

                    v_marks = []
                    for vt in range(32):
                        ps = pA[:, (vt % 2) * 512: (vt % 2) * 512 + 128]
                        xsb = xn_own if vt < 16 else xn_prt
                        tcol = (vt % 16) * 128
                        w0 = [(cVE, c8_ve), (cWQ[0], wt_ready)]
                        if vt >= 2:
                            w0 = [(cVE, v_marks[vt - 2])]
                        for ic in range(NIC):
                            E("tensor", lambda t, ic=ic, ps=ps, xsb=xsb, tcol=tcol: t.matmul(
                                ps, xsb[:, ic * T + tcol: ic * T + tcol + 128],
                                wv_sb[:, ic * 128:(ic + 1) * 128],
                                start=(ic == 0), stop=(ic == NIC - 1)),
                                waits=w0 if ic == 0 else [],
                                inc=cTE if ic == NIC - 1 else None)
                        tm = cTE.n
                        E("vector", lambda v, ps=ps, vt=vt, vsb=vsb: v.tensor_copy(
                            vsb[:, vt * 130:vt * 130 + 64], ps[:, 0:64]),
                            waits=[(cTE, tm)], inc=cVE)
                        E("vector", lambda v, ps=ps, vt=vt, vsb=vsb: v.tensor_copy(
                            vsb[:, vt * 130 + 65:vt * 130 + 129], ps[:, 64:128]), inc=cVE)
                        v_marks.append(cVE.n)
                    v_done = cVE.n

                    if debug and p == 0:
                        E("gpsimd", lambda g, qpA=qpA: g.dma_start(out=dbg["q"][0:64, :], in_=qpA[0:64, :]),
                            waits=[(cVE, v_done)], inc=cDBG, inck=16)
                        E("gpsimd", lambda g, qpB=qpB: g.dma_start(out=dbg["q"][64:128, :], in_=qpB[0:64, :]), inc=cDBG, inck=16)
                        E("gpsimd", lambda g, kpA=kpA: g.dma_start(out=dbg["k"][0:64, :], in_=kpA[0:64, :]), inc=cDBG, inck=16)
                        E("gpsimd", lambda g, kpB=kpB: g.dma_start(out=dbg["k"][64:128, :], in_=kpB[0:64, :]), inc=cDBG, inck=16)
                        E("gpsimd", lambda g, vsb=vsb: g.dma_start(out=dbg["v"][:, :], in_=vsb[:, :]), inc=cDBG, inck=16)

                    def passA(qc, entry_sc):
                        qoff = qc * 1024
                        maxm = []
                        for kt in range(32):
                            pst = (pA, pB)[kt % 2]
                            for head, (kph, qph, Mh) in enumerate(((kpA, qpA, MA), (kpB, qpB, MB))):
                                pslice = pst[:, head * 1024:(head + 1) * 1024]
                                for half in range(2):
                                    w0 = []
                                    if half == 0 and head == 0:
                                        if kt >= 2:
                                            w0 = [(cVE, maxm[kt - 2])]
                                        else:
                                            w0 = [(cVE, prev_norm), (cVE, v_done)] if qc == 0 else [(cVE, prev_norm)]
                                    E("tensor", lambda t, kt=kt, half=half, qoff=qoff,
                                        kph=kph, qph=qph, pslice=pslice: t.matmul(
                                        pslice[:, half * 512:(half + 1) * 512],
                                        kph[0:64, kt * 128:(kt + 1) * 128],
                                        qph[0:64, qoff + half * 512: qoff + (half + 1) * 512],
                                        start=True, stop=True),
                                        waits=w0, inc=cTE if half == 1 else None)
                                tmh = cTE.n
                                if kt == 0:
                                    E("vector", lambda v, Mh=Mh, pslice=pslice: v.tensor_copy(
                                        Mh[:, :], pslice),
                                        waits=[(cTE, tmh), (cSC, entry_sc)], inc=cVE)
                                else:
                                    E("vector", lambda v, Mh=Mh, pslice=pslice: v.tensor_tensor(
                                        Mh[:, :], pslice, Mh[:, :], ALU.max),
                                        waits=[(cTE, tmh)], inc=cVE)
                            maxm.append(cVE.n)
                        return cVE.n

                    def fold(qc, maxdone):
                        qoff = qc * 1024
                        for head, (Mh, qph, row) in enumerate(((MA, qpA, 0), (MB, qpB, 32))):
                            E("scalar", lambda s, Mh=Mh, head=head: s.activation(
                                out=Mh[:, :], in_=Mh[:, :], func=AF.Exp,
                                bias=c8col[:, head:head + 1], scale=0.125 / 8.0),
                                waits=[(cVE, maxdone), (cTE, cTE.n), (cSC, cSC.n)], inc=cSC)
                            scm = cSC.n
                            for half in range(2):
                                E("tensor", lambda t, row=row, Mh=Mh, half=half: t.matmul(
                                    pA[row:row + 1, half * 512:(half + 1) * 512],
                                    onescol_bf[:, :], Mh[:, half * 512:(half + 1) * 512],
                                    start=True, stop=True),
                                    waits=[(cSC, scm)] if half == 0 else [],
                                    inc=cTE if half == 1 else None)
                            E("vector", lambda v, row=row: v.tensor_single_scalar(
                                bc_sb[row:row + 1, 0:1024], pA[row:row + 1, 0:1024], 1e-38, ALU.add),
                                waits=[(cTE, cTE.n)], inc=cVE)
                            E("scalar", lambda s, row=row: s.activation(
                                out=bc_sb[row:row + 1, 0:1024], in_=bc_sb[row:row + 1, 0:1024],
                                func=AF.Ln),
                                waits=[(cVE, cVE.n)], inc=cSC)
                        E("vector", lambda v: v.tensor_scalar(
                            bc_sb[0:33, 0:1024], bc_sb[0:33, 0:1024], -64.0, bnd[0:33, 6:7], ALU.mult, ALU.add),
                            waits=[(cSC, cSC.n)], inc=cVE)
                        lnm = cVE.n
                        E("gpsimd", lambda g, qoff=qoff, qph=qpA: g.dma_start(
                            out=qph[64:65, qoff:qoff + 1024], in_=bc_sb[0:1, 0:1024]),
                            waits=[(cVE, lnm)], inc=cU, inck=16)
                        E("gpsimd", lambda g, qoff=qoff, qph=qpB: g.dma_start(
                            out=qph[64:65, qoff:qoff + 1024], in_=bc_sb[32:33, 0:1024]),
                            inc=cU, inck=16)
                        return cU.n, cSC.n

                    def passB0(qc, u_dma, lnsc):
                        qoff = qc * 1024
                        expA, expB = [], []
                        for kt in range(32):
                            par = kt % 2
                            for half in range(2):
                                w0 = []
                                if half == 0:
                                    w0 = [(cSC, expA[kt - 1])] if kt >= 1 else [(cU, u_dma), (cSC, lnsc)]
                                E("tensor", lambda t, kt=kt, half=half, qoff=qoff, kpA=kpA, qpA=qpA: t.matmul(
                                    pA[:, half * 512:(half + 1) * 512],
                                    kpA[0:65, kt * 128:(kt + 1) * 128],
                                    qpA[0:65, qoff + half * 512: qoff + (half + 1) * 512],
                                    start=True, stop=True),
                                    waits=w0, inc=cTE if half == 1 else None)
                            qkA = cTE.n
                            for half in range(2):
                                w0 = []
                                if half == 0 and kt >= 1:
                                    w0 = [(cSC, expB[kt - 1])]
                                E("tensor", lambda t, kt=kt, half=half, qoff=qoff, kpB=kpB, qpB=qpB: t.matmul(
                                    pB[:, half * 512:(half + 1) * 512],
                                    kpB[0:65, kt * 128:(kt + 1) * 128],
                                    qpB[0:65, qoff + half * 512: qoff + (half + 1) * 512],
                                    start=True, stop=True),
                                    waits=w0, inc=cTE if half == 1 else None)
                            qkB = cTE.n
                            if kt > 0:
                                ktm = kt - 1
                                parm = ktm % 2
                                for half in range(2):
                                    E("tensor", lambda t, ktm=ktm, parm=parm, half=half, vsb=vsb: t.matmul(
                                        pA[0:65, 1024 + half * 512:1024 + (half + 1) * 512],
                                        vsb[:, ktm * 130:ktm * 130 + 65],
                                        pt[:, parm * 1024 + half * 512: parm * 1024 + (half + 1) * 512],
                                        start=(ktm == 0), stop=False))
                                    E("tensor", lambda t, ktm=ktm, parm=parm, half=half, vsb=vsb: t.matmul(
                                        pB[64:128, 1024 + half * 512:1024 + (half + 1) * 512],
                                        vsb[:, ktm * 130 + 65:ktm * 130 + 129],
                                        pt[:, 2048 + parm * 1024 + half * 512: 2048 + parm * 1024 + (half + 1) * 512],
                                        start=(ktm == 0), stop=False, tile_position=(0, 64)))
                                    E("tensor", lambda t, ktm=ktm, parm=parm, half=half: t.matmul(
                                        pB[0:1, 1024 + half * 512:1024 + (half + 1) * 512],
                                        onescol_bf[:, :],
                                        pt[:, 2048 + parm * 1024 + half * 512: 2048 + parm * 1024 + (half + 1) * 512],
                                        start=(ktm == 0), stop=False))
                            E("scalar", lambda s, par=par: s.activation(
                                out=pt[:, par * 1024:(par + 1) * 1024], in_=pA[:, 0:1024],
                                func=AF.Exp, scale=0.125),
                                waits=[(cTE, qkA)], inc=cSC)
                            expA.append(cSC.n)
                            E("scalar", lambda s, par=par: s.activation(
                                out=pt[:, 2048 + par * 1024:2048 + (par + 1) * 1024], in_=pB[:, 0:1024],
                                func=AF.Exp, scale=0.125),
                                waits=[(cTE, qkB)], inc=cSC)
                            expB.append(cSC.n)
                        _pv_tail(vsb, expA, expB)
                        return expA[31], expB[31], cTE.n

                    def _pv_tail(vsb, expA, expB):
                        ktm, parm = 31, 1
                        for half in range(2):
                            E("tensor", lambda t, half=half, parm=parm, ktm=ktm, vsb=vsb: t.matmul(
                                pA[0:65, 1024 + half * 512:1024 + (half + 1) * 512],
                                vsb[:, ktm * 130:ktm * 130 + 65],
                                pt[:, parm * 1024 + half * 512: parm * 1024 + (half + 1) * 512],
                                start=False, stop=True),
                                waits=[(cSC, expA[31])] if half == 0 else [])
                            E("tensor", lambda t, half=half, parm=parm, ktm=ktm, vsb=vsb: t.matmul(
                                pB[64:128, 1024 + half * 512:1024 + (half + 1) * 512],
                                vsb[:, ktm * 130 + 65:ktm * 130 + 129],
                                pt[:, 2048 + parm * 1024 + half * 512: 2048 + parm * 1024 + (half + 1) * 512],
                                start=False, stop=True, tile_position=(0, 64)),
                                waits=[(cSC, expB[31])] if half == 0 else [])
                            E("tensor", lambda t, half=half, parm=parm: t.matmul(
                                pB[0:1, 1024 + half * 512:1024 + (half + 1) * 512],
                                onescol_bf[:, :],
                                pt[:, 2048 + parm * 1024 + half * 512: 2048 + parm * 1024 + (half + 1) * 512],
                                start=False, stop=True),
                                inc=cTE if half == 1 else None)

                    def gen_passB1(qc, u_dma, lnsc, entry_ve):
                        qoff = qc * 1024
                        exps = [[], []]
                        for i in range(64):
                            head, kt = i % 2, i // 2
                            kph, qph = ((kpA, qpA), (kpB, qpB))[head]
                            par = kt % 2
                            if i == 0:
                                w0 = [(cU, u_dma), (cSC, lnsc), (cVE, entry_ve)]
                            else:
                                w0 = [(cSC, exps[1 - head][-1])]
                            for half in range(2):
                                E("tensor", lambda t, kt=kt, half=half, qoff=qoff, kph=kph, qph=qph: t.matmul(
                                    pA[:, half * 512:(half + 1) * 512],
                                    kph[0:65, kt * 128:(kt + 1) * 128],
                                    qph[0:65, qoff + half * 512: qoff + (half + 1) * 512],
                                    start=True, stop=True),
                                    waits=w0 if half == 0 else [],
                                    inc=cTE if half == 1 else None)
                            qk = cTE.n
                            if kt > 0:
                                ktm = kt - 1
                                parm = ktm % 2
                                for half in range(2):
                                    if head == 0:
                                        E("tensor", lambda t, ktm=ktm, parm=parm, half=half, vsb=vsb: t.matmul(
                                            pA[0:65, 1024 + half * 512:1024 + (half + 1) * 512],
                                            vsb[:, ktm * 130:ktm * 130 + 65],
                                            pt[:, parm * 1024 + half * 512: parm * 1024 + (half + 1) * 512],
                                            start=(ktm == 0), stop=False))
                                    else:
                                        E("tensor", lambda t, ktm=ktm, parm=parm, half=half, vsb=vsb: t.matmul(
                                            pB[64:128, 1024 + half * 512:1024 + (half + 1) * 512],
                                            vsb[:, ktm * 130 + 65:ktm * 130 + 129],
                                            pt[:, 2048 + parm * 1024 + half * 512: 2048 + parm * 1024 + (half + 1) * 512],
                                            start=(ktm == 0), stop=False, tile_position=(0, 64)))
                                        E("tensor", lambda t, ktm=ktm, parm=parm, half=half: t.matmul(
                                            pB[0:1, 1024 + half * 512:1024 + (half + 1) * 512],
                                            onescol_bf[:, :],
                                            pt[:, 2048 + parm * 1024 + half * 512: 2048 + parm * 1024 + (half + 1) * 512],
                                            start=(ktm == 0), stop=False))
                            E("scalar", lambda s, par=par, head=head: s.activation(
                                out=pt[:, head * 2048 + par * 1024:head * 2048 + (par + 1) * 1024],
                                in_=pA[:, 0:1024],
                                func=AF.Exp, scale=0.125),
                                waits=[(cTE, qk)], inc=cSC)
                            exps[head].append(cSC.n)
                            yield
                        _pv_tail(vsb, exps[0], exps[1])
                        pb1_out[0] = (exps[0][31], exps[1][31], cTE.n)

                    def normalize(qc, expAl, expBl, pv_te):
                        nonlocal prev_norm
                        qoff = qc * 1024
                        E("vector", lambda v: v.tensor_copy(den_sb[64:65, :], pA[64:65, 1024:2048]),
                            waits=[(cTE, pv_te), (cAO, cAO.n)], inc=cVE)
                        E("vector", lambda v: v.tensor_copy(den_sb[0:1, :], pB[0:1, 1024:2048]), inc=cVE)
                        E("vector", lambda v: v.reciprocal(den_sb[0:65, :], den_sb[0:65, :]), inc=cVE)
                        den_ve = cVE.n
                        for half in range(2):
                            E("tensor", lambda t, half=half: t.matmul(
                                pA[:, half * 512:(half + 1) * 512],
                                ones_f32[64:65, :], den_sb[64:65, half * 512:(half + 1) * 512],
                                start=True, stop=True),
                                waits=[(cVE, den_ve), (cSC, expAl)] if half == 0 else [],
                                inc=cTE if half == 1 else None)
                        E("vector", lambda v: v.tensor_copy(bc_sb[:, :], pA[:, 0:1024]),
                            waits=[(cTE, cTE.n)], inc=cVE)
                        E("vector", lambda v: v.tensor_tensor(
                            ao_scr[0:64, :], pA[0:64, 1024:2048], bc_sb[0:64, :], ALU.mult),
                            waits=[(cVE, cVE.n)], inc=cVE)
                        for half in range(2):
                            E("tensor", lambda t, half=half: t.matmul(
                                pA[:, half * 512:(half + 1) * 512],
                                ones_f32[0:1, :], den_sb[0:1, half * 512:(half + 1) * 512],
                                start=True, stop=True),
                                waits=[(cVE, cVE.n), (cSC, expBl)] if half == 0 else [],
                                inc=cTE if half == 1 else None)
                        E("vector", lambda v: v.tensor_copy(bc_sb[:, :], pA[:, 0:1024]),
                            waits=[(cTE, cTE.n), (cVE, cVE.n)], inc=cVE)
                        E("vector", lambda v: v.tensor_tensor(
                            ao_scr[64:128, :], pB[64:128, 1024:2048], bc_sb[64:128, :], ALU.mult),
                            waits=[(cVE, cVE.n)], inc=cVE)
                        aov = cVE.n
                        E("sync", lambda s, p=p, qoff=qoff: s.dma_start(
                            out=dao[p * 128:(p + 1) * 128, qoff:qoff + 1024], in_=ao_scr[:, :]),
                            waits=[(cVE, aov)], inc=cAO, inck=16)
                        prev_norm = cVE.n

                    m0 = passA(0, cSC.n)
                    u0, ls0 = fold(0, m0)
                    eA0, eB0, pv0 = passB0(0, u0, ls0)
                    normalize(0, eA0, eB0, pv0)
                    m1 = passA(1, cSC.n)
                    u1, ls1 = fold(1, m1)
                    pb1_out = [None]
                    gens = [(gen_passB1(1, u1, ls1, cVE.n), 6)]
                    if p + 1 < NPAIR:
                        gens.append((gen_rope(p + 1), 1))
                    live = [[g, k] for g, k in gens]
                    while live:
                        for item in live[:]:
                            g, k = item
                            try:
                                for _ in range(k):
                                    next(g)
                            except StopIteration:
                                live.remove(item)
                    eA1, eB1, pv1 = pb1_out[0]
                    normalize(1, eA1, eB1, pv1)

                sch.flush(nc)
                ap_done_box[0] = cAO.n

    with contextlib.ExitStack() as s2:
        bufA = s2.enter_context(nc.sbuf_tensor("bufA", [128, NHC * TCT], BF16))
        bufB = s2.enter_context(nc.sbuf_tensor("bufB", [128, NHC * TCT], BF16))
        wh = [s2.enter_context(nc.sbuf_tensor(f"wh{j}", [128, 4096], BF16)) for j in range(3)]
        ao_sb = s2.enter_context(nc.sbuf_tensor("ao_p2", [128, NIC * T], BF16))
        wo_sb = [s2.enter_context(nc.sbuf_tensor(f"wo{j}", [128, 1024], BF16)) for j in range(2)]
        xres = [s2.enter_context(nc.sbuf_tensor(f"xres{j}", [128, 512], F32)) for j in range(2)]
        ytmp = s2.enter_context(nc.sbuf_tensor("ytmp", [128, 512], F32))
        ysb = [s2.enter_context(nc.sbuf_tensor(f"ysb{j}", [128, 512], F32)) for j in range(2)]

        with nc.psum_tensor("pF", [128, 2048], F32) as pF, \
             nc.psum_tensor("pF2", [128, 2048], F32) as pF2:
            for dc in range(NIC):
                E("sync", lambda s, dc=dc: s.dma_start(
                    out=ao_sb[:, dc * T:(dc + 1) * T], in_=dao[dc * 128:(dc + 1) * 128, :]),
                    waits=[(cAO, ap_done_box[0])] if dc == 0 else [], inc=cAO, inck=16)
            ao_ready = cAO.n
            prev_tail = [(cVE, cVE.n)] * 4
            prev_wh_marks = [cTE.n, cTE.n, cTE.n]
            y_ve_marks = []
            st_marks = []

            opdone = [0]

            def gen_op():
                pass
                E("gpsimd", lambda g: g.dma_start(out=wo_sb[0][:, :], in_=dwout[0:128, :]),
                    waits=[(cTE, cTE.n)], inc=cWO[0], inck=16)
                op_marks = []
                for oc in range(NIC):
                    ob = oc % 2
                    wt_r = cWO[ob].n
                    if oc < NIC - 1:
                        E("gpsimd", lambda g, oc=oc: g.dma_start(
                            out=wo_sb[(oc + 1) % 2][:, :], in_=dwout[(oc + 1) * 128:(oc + 2) * 128, :]),
                            waits=[(cTE, op_marks[oc * 4 - 1][1]) if oc >= 1 else (cTE, cTE.n)],
                            inc=cWO[(oc + 1) % 2], inck=16)
                    for qt in range(4):
                        idx = oc * 4 + qt
                        ps = pF2[:, (idx % 4) * 512:(idx % 4 + 1) * 512]
                        w0 = [(cAO, ao_ready), (cWO[ob], wt_r)]
                        if idx >= 4:
                            w0.append((cVE, op_marks[idx - 4][0]))
                        for dc in range(NIC):
                            E("tensor", lambda t, dc=dc, qt=qt, ps=ps, ob=ob: t.matmul(
                                ps, wo_sb[ob][:, dc * 128:(dc + 1) * 128],
                                ao_sb[:, dc * T + qt * 512:dc * T + (qt + 1) * 512],
                                start=(dc == 0), stop=(dc == NIC - 1)),
                                waits=w0 if dc == 0 else [],
                                inc=cTE if dc == NIC - 1 else None)
                        tm = cTE.n
                        E("vector", lambda v, oc=oc, qt=qt, ps=ps: v.tensor_copy(
                            attn_proj[:, oc * T + qt * 512:oc * T + (qt + 1) * 512], ps),
                            waits=[(cTE, tm)], inc=cVE)
                        op_marks.append((cVE.n, tm))
                        yield
                opdone[0] = cVE.n

            def gen_stage(dw, n_oc, n_ic, ncols, rhs_fn, evac_fn, first_extra=None,
                          dwc=None, first_pass=True):
                nonlocal prev_tail, prev_wh_marks
                te_marks = []
                ev = []
                for oc in range(n_oc):
                    wb = oc % 3
                    lw = [(cTE, te_marks[oc - 3])] if oc >= 3 else [(cTE, prev_wh_marks[oc])]
                    if dwc is not None and first_pass:
                        lw.append((cWB[wb % 2], cWB[wb % 2].n))
                    src_t = dw if (dwc is None or first_pass) else dwc
                    E("gpsimd", lambda g, oc=oc, wb=wb, src_t=src_t, ncols=ncols: g.dma_start(
                        out=wh[wb][:, 0:ncols], in_=src_t[oc * 128:(oc + 1) * 128, :]),
                        waits=lw, inc=cWH[wb], inck=16)
                    wt_r = cWH[wb].n
                    if dwc is not None and first_pass:
                        E("sync", lambda s, oc=oc, wb=wb, dwc=dwc, ncols=ncols: s.dma_start(
                            out=dwc[oc * 128:(oc + 1) * 128, :], in_=wh[wb][:, 0:ncols]),
                            waits=[(cWH[wb], wt_r)], inc=cWB[wb % 2], inck=16)
                    ps = pF[:, 512 * (oc % 4):512 * (oc % 4 + 1)]
                    w0 = [(cWH[wb], wt_r)]
                    if first_extra is not None and oc == 0:
                        w0.append(first_extra)
                    if oc >= 4:
                        w0.append(ev[oc - 4])
                    else:
                        w0.append(prev_tail[oc])
                    for ic in range(n_ic):
                        E("tensor", lambda t, ic=ic, ps=ps, wb=wb: t.matmul(
                            ps, wh[wb][:, ic * 128:(ic + 1) * 128], rhs_fn(ic),
                            start=(ic == 0), stop=(ic == n_ic - 1)),
                            waits=w0 if ic == 0 else [],
                            inc=cTE if ic == n_ic - 1 else None)
                    te_marks.append(cTE.n)
                    ev.append(evac_fn(oc, ps, cTE.n))
                    yield
                prev_tail = ev[-4:]
                prev_wh_marks = [te_marks[-3], te_marks[-2], te_marks[-1]]
                evout[0] = ev

            def stage(*a, **k):
                evout[0] = None
                for _ in gen_stage(*a, **k):
                    pass
                return evout[0]

            evout = [None]

            for tct in range(NTCT):
                tof = tct * TCT

                def evac_w2(oc, ps, tm):
                    E("vector", lambda v, oc=oc, ps=ps: v.tensor_scalar(
                        bufB[:, oc * TCT:(oc + 1) * TCT], ps, s2_sb[:, oc:oc + 1], None, ALU.mult),
                        waits=[(cTE, tm)], inc=cVE)
                    return (cVE, cVE.n)

                def evac_w1(oc, ps, tm):
                    E("scalar", lambda s, oc=oc, ps=ps: s.activation(
                        out=bufA[:, oc * TCT:(oc + 1) * TCT], in_=ps,
                        func=AF.Silu, scale=s1_sb[:, oc:oc + 1]),
                        waits=[(cTE, tm)], inc=cSC)
                    scm = cSC.n
                    E("vector", lambda v, oc=oc: v.tensor_tensor(
                        bufB[:, oc * TCT:(oc + 1) * TCT],
                        bufA[:, oc * TCT:(oc + 1) * TCT],
                        bufB[:, oc * TCT:(oc + 1) * TCT], ALU.mult),
                        waits=[(cSC, scm), (cVE, w2ev[oc][1])], inc=cVE)
                    return (cSC, scm)

                def evac_h1(oc, ps, tm):
                    E("scalar", lambda s, oc=oc, ps=ps: s.activation(
                        out=bufA[:, oc * TCT:(oc + 1) * TCT], in_=ps,
                        func=AF.Gelu, scale=h1s_sb[:, oc:oc + 1]),
                        waits=[(cTE, tm)], inc=cSC)
                    return (cSC, cSC.n)

                def evac_h2(oc, ps, tm):
                    E("scalar", lambda s, oc=oc, ps=ps: s.activation(
                        out=bufB[:, oc * TCT:(oc + 1) * TCT], in_=ps,
                        func=AF.Gelu, scale=h2s_sb[:, oc:oc + 1]),
                        waits=[(cTE, tm)], inc=cSC)
                    return (cSC, cSC.n)

                def evac_w3(oc, ps, tm):
                    jb = oc % 2
                    nst = len(st_marks)
                    lw = [(cVE, y_ve_marks[nst - 2])] if nst >= 2 else []
                    E("sync", lambda s, oc=oc, jb=jb, tof=tof: s.dma_start(
                        out=xres[jb][:, :], in_=dx[oc * 128:(oc + 1) * 128, tof:tof + TCT]),
                        waits=lw, inc=cXR[jb], inck=16)
                    ld = cXR[jb].n
                    E("vector", lambda v, oc=oc, ps=ps, tof=tof: v.scalar_tensor_tensor(
                        ytmp[:, :], ps, w3s_sb[:, oc:oc + 1],
                        attn_proj[:, oc * T + tof: oc * T + tof + TCT],
                        ALU.mult, ALU.add),
                        waits=[(cTE, tm), (cVE, opdone[0])], inc=cVE)
                    sw = [(cXR[jb], ld)]
                    if nst >= 2:
                        sw.append((cST[jb], st_marks[nst - 2]))
                    E("vector", lambda v, jb=jb: v.tensor_tensor(
                        ysb[jb][:, :], ytmp[:, :], xres[jb][:, :], ALU.add),
                        waits=sw, inc=cVE)
                    yv = cVE.n
                    E("sync", lambda s, oc=oc, jb=jb, tof=tof: s.dma_start(
                        out=dy[oc * 128:(oc + 1) * 128, tof:tof + TCT], in_=ysb[jb][:, :]),
                        waits=[(cVE, yv)], inc=cST[jb], inck=16)
                    y_ve_marks.append(yv)
                    st_marks.append(cST[jb].n)
                    return (cVE, cVE.n)

                if tct == 0:
                    evout[0] = None
                    g1 = gen_stage(dw2, NHC, NIC, 1024,
                                   lambda ic: xn_own[:, ic * T + tof: ic * T + tof + TCT],
                                   evac_w2, first_extra=(cTAB, scales_wt),
                                   dwc=dw2c, first_pass=True)
                    g2 = gen_op()
                    live = [g1, g2]
                    while live:
                        for g in live[:]:
                            try:
                                next(g)
                            except StopIteration:
                                live.remove(g)
                    ew2 = evout[0]
                else:
                    ew2 = stage(dw2, NHC, NIC, 1024,
                                lambda ic: xn_own[:, ic * T + tof: ic * T + tof + TCT],
                                evac_w2, first_extra=(cTAB, scales_wt),
                                dwc=dw2c, first_pass=False)
                w2ev = ew2
                ew1 = stage(dw1, NHC, NIC, 1024,
                            lambda ic: xn_own[:, ic * T + tof: ic * T + tof + TCT],
                            evac_w1, dwc=dw1c, first_pass=(tct == 0))
                h_done = cVE.n
                eh1 = stage(dh1, NHC, NHC, 4096,
                            lambda ic: bufB[:, ic * TCT:(ic + 1) * TCT],
                            evac_h1, first_extra=(cVE, h_done),
                            dwc=dh1c, first_pass=(tct == 0))
                eh2 = stage(dh2, NHC, NHC, 4096,
                            lambda ic: bufA[:, ic * TCT:(ic + 1) * TCT],
                            evac_h2, first_extra=(cSC, eh1[-1][1]),
                            dwc=dh2c, first_pass=(tct == 0))
                stage(dw3, NIC, NHC, 4096,
                      lambda ic: bufB[:, ic * TCT:(ic + 1) * TCT],
                      evac_w3, first_extra=(cSC, eh2[-1][1]),
                      dwc=dw3c, first_pass=(tct == 0))

            E("sync", lambda s: None, waits=[(cST[0], cST[0].n), (cST[1], cST[1].n)])
            sch.flush(nc)

    stack.close()
    return nc, dbg



def _strip_img(wT, n_ic):
    return wT.reshape(n_ic, 128, 128).transpose(1, 0, 2).reshape(128, n_ic * 128)


def _weight_img(wT, n_ic, n_oc):
    return np.ascontiguousarray(np.concatenate(
        [_strip_img(wT[:, oc * 128:(oc + 1) * 128], n_ic) for oc in range(n_oc)], axis=0))


def _rope_tables(h):
    e = np.arange(32, dtype=np.float32) / np.float32(32.0)
    inv = (np.float32(1.0) / np.float32(ROPE_BASE) ** e).astype(np.float32)
    pos = np.concatenate([np.arange(h * T, (h + 1) * T),
                          np.arange((1 - h) * T, (2 - h) * T)]).astype(np.float32)
    ang = (pos[:, None] * inv[None, :]).astype(np.float32)
    c32 = np.cos(ang.astype(np.float64)).astype(np.float32)
    s32 = np.sin(ang.astype(np.float64)).astype(np.float32)
    cos_rows = np.zeros((128, L), np.float32)
    sin_rows = np.zeros((128, L), np.float32)
    for d in range(128):
        dd = d % 64
        j = dd % 32
        cos_rows[d] = c32[:, j]
        sin_rows[d] = (np.float32(-1.0) if dd < 32 else np.float32(1.0)) * s32[:, j]
    return (np.ascontiguousarray(cos_rows), np.ascontiguousarray(sin_rows))


def make_in_maps(x, wqkv, wout, w1w, w1s, w2w, w2s, h1w, h1s, h2w, h2s, w3w, w3s):
    x = np.asarray(x, np.float32)
    wqkvT = np.ascontiguousarray(np.asarray(wqkv, np.float32).T)
    woutT = np.ascontiguousarray(np.asarray(wout, np.float32).T)
    w1T = np.ascontiguousarray(np.asarray(w1w, np.float32).T)
    w2T = np.ascontiguousarray(np.asarray(w2w, np.float32).T)
    h1T = np.ascontiguousarray(np.asarray(h1w, np.float32).T)
    h2T = np.ascontiguousarray(np.asarray(h2w, np.float32).T)
    w3T = np.ascontiguousarray(np.asarray(w3w, np.float32).T)

    blocks = []
    for p in range(NPAIR):
        for r in range(3):
            blocks.append(_strip_img(wqkvT[:, r * D + p * 128: r * D + (p + 1) * 128], NIC))
    wqkv_img = np.ascontiguousarray(np.concatenate(blocks, axis=0))
    imgs = {
        "wqkv": wqkv_img,
        "wout": _weight_img(woutT, NIC, NIC),
        "w1": _weight_img(w1T, NIC, NHC),
        "w2": _weight_img(w2T, NIC, NHC),
        "h1": _weight_img(h1T, NHC, NHC),
        "h2": _weight_img(h2T, NHC, NHC),
        "w3": _weight_img(w3T, NHC, NIC),
        "s1": np.ascontiguousarray(np.asarray(w1s, np.float32).reshape(32, 128).T),
        "s2": np.ascontiguousarray(np.asarray(w2s, np.float32).reshape(32, 128).T),
        "h1s": np.ascontiguousarray(np.asarray(h1s, np.float32).reshape(32, 128).T),
        "h2s": np.ascontiguousarray(np.asarray(h2s, np.float32).reshape(32, 128).T),
        "w3s": np.ascontiguousarray(np.asarray(w3s, np.float32).reshape(8, 128).T),
    }
    tabs = [_rope_tables(0), _rope_tables(1)]
    in_maps = []
    for c in range(8):
        b, h = c // 2, c % 2
        xr = np.ascontiguousarray(
            np.concatenate([x[b, h * T:(h + 1) * T, :], x[b, (1 - h) * T:(2 - h) * T, :]], axis=0).T)
        m = {"x": xr, "costab": tabs[h][0], "sintab": tabs[h][1]}
        m.update(imgs)
        in_maps.append(m)
    return in_maps


def kernel(L0_attn_x, L0_attn_w_qkv, L0_attn_w_out,
           L0_ffn_w1_w, L0_ffn_w1_s, L0_ffn_w2_w, L0_ffn_w2_s,
           L0_ffn_h1_w, L0_ffn_h1_s, L0_ffn_h2_w, L0_ffn_h2_s,
           L0_ffn_w3_w, L0_ffn_w3_s):
    in_maps = make_in_maps(L0_attn_x, L0_attn_w_qkv, L0_attn_w_out,
                           L0_ffn_w1_w, L0_ffn_w1_s, L0_ffn_w2_w, L0_ffn_w2_s,
                           L0_ffn_h1_w, L0_ffn_h1_s, L0_ffn_h2_w, L0_ffn_h2_s,
                           L0_ffn_w3_w, L0_ffn_w3_s)
    nc, _ = build_nc(debug=False)
    res = run_bass_kernel_spmd(nc, in_maps, core_ids=list(range(8)))
    out = np.empty((B, L, D), np.float32)
    for c in range(8):
        b, h = c // 2, c % 2
        out[b, h * T:(h + 1) * T, :] = np.asarray(res.results[c]["y"]).T
    return out
